# version 18
# baseline (speedup 1.0000x reference)
import numpy as np
from contextlib import ExitStack

import concourse.bass as bass
import concourse.mybir as mybir
from concourse.bass_utils import run_bass_kernel_spmd

F32 = mybir.dt.float32
BF16 = mybir.dt.bfloat16
FP16 = mybir.dt.float16
I32 = mybir.dt.int32
AF = mybir.ActivationFunctionType
ALU = mybir.AluOpType

T, NCORE = 16384, 8
TC = T // NCORE
L = 128
NCH = TC // L
H, DK, DV, DM, DI = 16, 64, 64, 1024, 2816
NP = H // 2
FC = DM // 128
EPS = 1e-5
MIDS = [(0.25, 0.5), (0.75, 0.5), (1.25, 0.5), (1.75, 0.5),
        (2.5, 1.0), (3.5, 1.0), (5.0, 2.0)]
CONSTS = [0.0, 1.0, EPS]


class Sched:
    ENGINES = ("sync", "gpsimd", "scalar", "vector", "tensor")

    def __init__(self, nc):
        self.nc = nc
        self.steps = []

    def step(self, engine, fn, deps=(), coll=False):
        self.steps.append((engine, fn, tuple(deps), 0, coll))
        return len(self.steps) - 1

    def dma(self, engine, fn, n, deps=()):
        self.steps.append((engine, fn, tuple(deps), n, False))
        return len(self.steps) - 1

    NDMA_SEMS = 5

    def run(self, es):
        nc = self.nc
        semnames = list(self.ENGINES)
        for e in ("sync", "gpsimd"):
            semnames += [e + "_dma%d" % r for r in range(self.NDMA_SEMS)]
        ncoll = sum(1 for s in self.steps if s[4])
        semnames += ["cc%d" % r for r in range(ncoll)]
        sems = {e: es.enter_context(nc.semaphore("sem_" + e)) for e in semnames}
        dma_ord = {e: 0 for e in ("sync", "gpsimd")}
        cc_ord = 0
        semname = [None] * len(self.steps)
        prior = [0] * len(self.steps)
        done = [0] * len(self.steps)
        count = {e: 0 for e in semnames}
        for i, (e, fn, deps, nd, coll) in enumerate(self.steps):
            if nd:
                sn = e + "_dma%d" % (dma_ord[e] % self.NDMA_SEMS)
                dma_ord[e] += 1
            elif coll:
                sn = "cc%d" % cc_ord
                cc_ord += 1
            else:
                sn = e
            semname[i] = sn
            prior[i] = count[sn]
            count[sn] += 16 * nd if nd else 1
            done[i] = count[sn]
        with nc.Block() as blk:
            for e in self.ENGINES:
                mine = [(i, fn, deps, nd)
                        for i, (ee, fn, deps, nd, _coll) in
                        enumerate(self.steps) if ee == e]
                if not mine:
                    continue

                def make(eng_name, mine):
                    def body(eng):
                        for i, fn, deps, nd in mine:
                            need_drain = False
                            for d in deps:
                                dsem = semname[d]
                                if dsem != eng_name or self.steps[d][3]:
                                    eng.wait_ge(sems[dsem], done[d])
                                else:
                                    need_drain = True
                            if need_drain and eng_name in ("scalar", "vector"):
                                eng.drain()
                            if nd:
                                if prior[i]:
                                    eng.wait_ge(sems[semname[i]], prior[i])
                                fn(eng, sems[semname[i]])
                            else:
                                fn(eng).then_inc(sems[semname[i]], 1)
                    return body
                getattr(blk, e)(make(e, mine))
        return sems, done


def alloc_common(nc, es, sch):
    cm = {}
    sb = lambda n, s, dt: es.enter_context(nc.sbuf_tensor(n, s, dt))
    cm["iot"] = sb("iot", [128, 128], I32)
    cm["idm"] = sb("idm", [128, 128], BF16)
    cm["m_incl"] = sb("m_incl", [128, 128], BF16)
    cm["m_str"] = sb("m_str", [128, 128], BF16)
    cm["m_strn"] = sb("m_strn", [128, 128], BF16)
    cm["zero128"] = sb("zero128", [128, 128], F32)
    cm["ones64"] = sb("ones64", [128, 64], BF16)
    cm["cpool"] = sb("cpool", [128, 4], F32)
    cm["ps_bf"] = es.enter_context(nc.psum_tensor("ps_bf", [128, 1024], BF16))
    cm["ps_a"] = es.enter_context(nc.psum_tensor("ps_a", [128, 512], F32))
    cm["ps_b"] = es.enter_context(nc.psum_tensor("ps_b", [128, 1024], F32))
    cm["ps_c"] = es.enter_context(nc.psum_tensor("ps_c", [128, 512], F32))
    cm["ps_d"] = es.enter_context(nc.psum_tensor("ps_d", [128, 512], F32))
    cm["ps_e"] = es.enter_context(nc.psum_tensor("ps_e", [128, 512], F32))
    cm["ps_f"] = es.enter_context(nc.psum_tensor("ps_f", [128, 512], F32))
    for i, cv in enumerate(CONSTS):
        nc.const_aps.register(F32, cv, cm["cpool"][:, i:i + 1])

    def c_gp(g):
        for i, cv in enumerate(CONSTS):
            g.memset(cm["cpool"][:, i:i + 1], cv)
        g.memset(cm["zero128"][:, :], 0.0)
        g.memset(cm["ones64"][:, :], 1.0)
        return g.iota(cm["iot"][:, :], [[1, 128]], channel_multiplier=-1)
    s0 = sch.step("gpsimd", c_gp)

    def c_v(v):
        v.tensor_scalar(out=cm["idm"][:, :], in0=cm["iot"][:, :], scalar1=0,
                        scalar2=None, op0=ALU.is_equal)
        v.tensor_scalar(out=cm["m_incl"][:, :], in0=cm["iot"][:, :], scalar1=0,
                        scalar2=None, op0=ALU.is_ge)
        v.tensor_scalar(out=cm["m_str"][:, :], in0=cm["iot"][:, :], scalar1=0,
                        scalar2=None, op0=ALU.is_gt)
        return v.tensor_scalar(out=cm["m_strn"][:, :], in0=cm["iot"][:, :],
                               scalar1=0, scalar2=-1.0, op0=ALU.is_gt,
                               op1=ALU.mult)
    cm["ready"] = sch.step("vector", c_v, deps=[s0])
    cm["ptrtok"] = [cm["ready"]]
    cm["tmpa"] = sb("utmp_a", [128, 512], F32)
    cm["tmpb"] = sb("utmp_b", [128, 512], F32)
    return cm


def emit_xT(nc, es, sch, cm, x_d, xT):
    xg = es.enter_context(nc.sbuf_tensor("xg", [128, 4, DM], BF16))
    ptr = cm["ps_bf"]
    last_evac = cm["ready"]
    for g in range(TC // 512):
        def ldf(gp, sem, g=g):
            gp.dma_start(
                out=xg[:, :, :],
                in_=x_d[g * 512:(g + 1) * 512, :]
                .rearrange("(a p) f -> p a f", p=128)).then_inc(sem, 16)
        ld = sch.dma("gpsimd", ldf, 1, deps=[last_evac])
        for tc4 in range(4):
            def tf(t, tc4=tc4):
                for fc in range(FC):
                    last = t.transpose(ptr[:, fc * 128:(fc + 1) * 128],
                                       xg[:, tc4, fc * 128:(fc + 1) * 128],
                                       cm["idm"][:, :])
                return last
            ts = sch.step("tensor", tf,
                          deps=[ld, cm["ready"], cm["ptrtok"][0]])

            def vf(v, g=g, tc4=tc4):
                t0 = g * 512 + tc4 * 128
                return v.tensor_copy(
                    out=xT[:, :, t0:t0 + 128],
                    in_=ptr[:, :].rearrange("p (a b) -> p a b", b=128))
            last_evac = sch.step("vector", vf, deps=[ts])
            cm["ptrtok"][0] = last_evac
    return last_evac


def emit_projections(nc, es, sch, cm, xT, xT_done, wd, outs, which=("q", "k", "gK", "gC", "v")):
    Wbufs = [es.enter_context(nc.sbuf_tensor("Wp0", [128, FC, DM], BF16)),
             es.enter_context(nc.sbuf_tensor("Wp1", [128, FC, DM], BF16))]
    stg = es.enter_context(nc.sbuf_tensor("stgp", [128, TC], BF16))
    stg16 = es.enter_context(nc.sbuf_tensor("stg16", [128, TC], FP16))
    spsA = [es.enter_context(nc.sbuf_tensor("spsp0", [128, 512], F32)),
            es.enter_context(nc.sbuf_tensor("spsp1", [128, 512], F32))]
    spfA = [es.enter_context(nc.sbuf_tensor("spfp0", [128, 512], F32)),
            es.enter_context(nc.sbuf_tensor("spfp1", [128, 512], F32))]
    pps = [cm["ps_a"], cm["ps_c"], cm["ps_d"], cm["ps_e"]]

    bgn8 = es.enter_context(nc.sbuf_tensor("bgn8", [128, 16], F32))

    def bg_ld(sy, sem):
        with nc.allow_non_contiguous_dma(reason="tiny bias gather"):
            sy.dma_start(out=bgn8[:, 0:8],
                         in_=wd["bg_K"].rearrange("o (a p) -> p (o a)", p=128)
                         ).then_inc(sem, 16)
            sy.dma_start(out=bgn8[:, 8:16],
                         in_=wd["bg_C"].rearrange("o (a p) -> p (o a)", p=128)
                         ).then_inc(sem, 16)
    bgl = sch.dma("sync", bg_ld, 2)
    bgneg = sch.step("vector", lambda v: v.tensor_scalar_mul(
        out=bgn8[:, :], in0=bgn8[:, :], scalar1=-1.0), deps=[bgl])

    NBANK = len(pps)
    evhist = [None] * NBANK
    sphist = [None, None]
    prev_dma = {"stg": None, "stg16": None}
    lastmm = []
    gi = [0, 0]
    finals = []

    def emit_one(mi, Wname, out_dram, kind, bg_off=None):
        Wp = Wbufs[mi % 2]

        def wldf(gp, sem, Wname=Wname, Wp=Wp):
            gp.dma_start(out=Wp[:, :, :],
                         in_=wd[Wname].rearrange("(a p) c -> p a c", p=128)
                         ).then_inc(sem, 16)
        wld = sch.dma("gpsimd", wldf, 1,
                      deps=([lastmm[mi - 2]] if mi >= 2 else []))
        fin = []
        mm = None
        if kind in ("fm", "gate"):
            sbuf = "stg" if kind == "fm" else "stg16"
            for oc in range(FC):
                mm_steps = []
                for tch in range(4):
                    par = gi[0] % NBANK
                    pp = pps[par]
                    gi[0] += 1
                    def mf(t, oc=oc, tch=tch, pp=pp, Wp=Wp):
                        for fk in range(FC):
                            last_i = t.matmul(
                                pp[:, :], Wp[:, fk, oc * 128:(oc + 1) * 128],
                                xT[:, fk, tch * 512:(tch + 1) * 512],
                                start=(fk == 0), stop=(fk == FC - 1))
                        return last_i
                    deps = [wld, xT_done]
                    if evhist[par] is not None:
                        deps.append(evhist[par])
                    mm = sch.step("tensor", mf, deps=deps)
                    if kind == "fm":
                        def ef(s, tch=tch, pp=pp):
                            return s.copy(out=stg[:, tch * 512:(tch + 1) * 512],
                                          in_=pp[:, :])
                        ev = sch.step("scalar", ef, deps=[mm] +
                                      ([prev_dma[sbuf]] if prev_dma[sbuf]
                                       else []))
                        evhist[par] = ev
                    else:
                        sp = gi[1] % 2
                        gi[1] += 1

                        def e1(s, bg_off=bg_off, oc=oc, pp=pp, sp=sp):
                            s.drain()
                            s.activation(out=spsA[sp][:, :], in_=pp[:, :],
                                         func=AF.Exp, scale=-1.0,
                                         bias=bgn8[:, bg_off + oc:bg_off + oc + 1])
                            s.drain()
                            return s.activation(out=spfA[sp][:, :],
                                                in_=spsA[sp][:, :],
                                                func=AF.Ln, bias=1.0)
                        sv = sch.step("scalar", e1, deps=[mm, bgneg] +
                                      ([sphist[sp]] if sphist[sp] else []))

                        def e2(v, tch=tch, sp=sp):
                            for c4 in range(4):
                                sl = slice(c4 * 128, (c4 + 1) * 128)
                                v.tensor_tensor_scan(
                                    out=spsA[sp][:, sl], data0=spfA[sp][:, sl],
                                    data1=cm["zero128"][:, :], initial=0.0,
                                    op0=ALU.add, op1=ALU.add)
                            v.drain()
                            return v.tensor_copy(
                                out=stg16[:, tch * 512:(tch + 1) * 512],
                                in_=spsA[sp][:, :])
                        ev = sch.step("vector", e2, deps=[sv] +
                                      ([prev_dma[sbuf]] if prev_dma[sbuf]
                                       else []))
                        evhist[par] = ev
                        sphist[sp] = ev
                    mm_steps.append((mm, ev))
                evs = [ev for _, ev in mm_steps]

                def df(sy, sem, oc=oc, kind=kind):
                    src = stg if kind == "fm" else stg16
                    for c in range(NCH):
                        sy.dma_start(
                            out=out_dram[c, :, oc * L:(oc + 1) * L],
                            in_=src[:, c * L:(c + 1) * L]).then_inc(sem, 16)
                dmas = sch.dma("sync", df, NCH, deps=evs)
                prev_dma[sbuf] = dmas
                fin.append(dmas)
        else:
            for tch in range(NCH):
                prev_ev = None
                for n2 in range(2):
                    par = gi[0] % NBANK
                    pp = pps[par]
                    gi[0] += 1
                    def mf2(t, tch=tch, n2=n2, pp=pp, Wp=Wp):
                        for fk in range(FC):
                            last_i = t.matmul(
                                pp[:, :],
                                xT[:, fk, tch * 128:(tch + 1) * 128],
                                Wp[:, fk, n2 * 512:(n2 + 1) * 512],
                                start=(fk == 0), stop=(fk == FC - 1))
                        return last_i
                    deps = [wld, xT_done]
                    if evhist[par] is not None:
                        deps.append(evhist[par])
                    mm = sch.step("tensor", mf2, deps=deps)

                    def ef(s, n2=n2, pp=pp):
                        return s.copy(out=stg[:, n2 * 512:(n2 + 1) * 512],
                                      in_=pp[:, :])
                    prev_ev = sch.step("scalar", ef, deps=[mm] +
                                       ([prev_dma["stg"]] if prev_dma["stg"]
                                        else []))
                    evhist[par] = prev_ev

                def df(sy, sem, tch=tch):
                    sy.dma_start(out=out_dram[tch * 128:(tch + 1) * 128, :],
                                 in_=stg[:, 0:DM]).then_inc(sem, 16)
                prev_dma["stg"] = sch.dma("sync", df, 1, deps=[prev_ev])
                fin.append(prev_dma["stg"])
        lastmm.append(mm)
        finals.append(fin[-1])

    mats = [("q", "W_q", outs["q_s"], "fm", None),
            ("k", "W_k", outs["k_s"], "fm", None),
            ("gK", "Wg_K", outs["spK_s"], "gate", 0),
            ("gC", "Wg_C", outs["spC_s"], "gate", 8),
            ("v", "W_v", outs["v_s"], "tm", None)]
    mats = [m for m in mats if m[0] in which]
    for mi, (_, wn, od, kd, bo) in enumerate(mats):
        emit_one(mi, wn, od, kd, bg_off=bo)
    return finals


class ChunkBufs:
    def __init__(self, nc, es, tag, with_sweep2, share_prep=None):
        sb = lambda n, s, dt: es.enter_context(nc.sbuf_tensor(n + tag, s, dt))
        if share_prep is not None:
            for f in ("qf", "kf", "sK", "sC", "Pinv", "Rg", "Rinv"):
                setattr(self, f, getattr(share_prep, f))
        else:
            self.qf = sb("qf", [128, FC, L], BF16)
            self.kf = sb("kf", [128, FC, L], BF16)
            self.sK = sb("sK", [128, FC, L], FP16)
            self.sC = sb("sC", [128, FC, L], FP16)
            self.Pinv = sb("Pinv", [128, DM], BF16)
            self.Rg = sb("Rg", [128, DM], BF16)
            self.Rinv = sb("Rinv", [128, DM], BF16)
        self.vt = sb("vt", [128, DM], BF16)
        self.Ut = sb("Ut", [128, DM], BF16)
        self.Qh = sb("Qh", [128, DM], BF16)
        self.Kb = sb("Kb", [128, DM], BF16)
        self.Utm = sb("Utm", [128, DM], BF16)
        self.Qhtm = sb("Qhtm", [128, DM], BF16)
        self.plK = sb("plK", [128, 8], F32)
        self.rlE = sb("rlE", [128, 8], F32)
        self.plbc_r = sb("plbc_r", [128, 512], BF16)
        self.plbc = sb("plbc", [128, 512], F32)
        if with_sweep2:
            self.P = sb("P", [128, DM], BF16)
            self.PRg = sb("PRg", [128, DM], BF16)
            self.Qb = sb("Qb", [128, DM], BF16)
            self.prg16 = sb("prg16", [128, DM], FP16)
        else:
            self.Kbtm = sb("Kbtm", [128, DM], BF16)


def emit_chunk_prep(nc, sch, cm, cb, srcs, c, sweep2, extra_dep=None):
    q_s, k_s, v_s, spK_s, spC_s = srcs

    def ldf(sy, sem, c=c):
        sy.dma_start(out=cb.qf[:, :, :].rearrange("p a b -> p (a b)"),
                     in_=q_s[c, :, :]).then_inc(sem, 16)
        sy.dma_start(out=cb.kf[:, :, :].rearrange("p a b -> p (a b)"),
                     in_=k_s[c, :, :]).then_inc(sem, 16)
        sy.dma_start(out=cb.vt[:, :],
                     in_=v_s[c * L:(c + 1) * L, :]).then_inc(sem, 16)
        sy.dma_start(out=cb.sK[:, :, :].rearrange("p a b -> p (a b)"),
                     in_=spK_s[c, :, :]).then_inc(sem, 16)
        sy.dma_start(out=cb.sC[:, :, :].rearrange("p a b -> p (a b)"),
                     in_=spC_s[c, :, :]).then_inc(sem, 16)
    deps = [] if extra_dep is None else list(extra_dep)
    ld = sch.dma("sync", ldf, 5, deps=deps)

    fl = lambda b: b[:, :, :].rearrange("p a b -> p (a b)")

    def expf(s):
        s.activation(out=cb.Pinv[:, :], in_=fl(cb.sK), func=AF.Exp)
        s.activation(out=cb.Rg[:, :], in_=fl(cb.sC), func=AF.Exp, scale=-1.0)
        s.activation(out=cb.Rinv[:, :], in_=fl(cb.sC), func=AF.Exp)
        s.activation(out=cb.plK[:, :], in_=cb.sK[:, :, L - 1:L].squeeze(2),
                     func=AF.Exp, scale=-1.0)
        last = s.activation(out=cb.rlE[:, :], in_=cb.sC[:, :, L - 1:L].squeeze(2),
                            func=AF.Exp, scale=-1.0)
        if sweep2:
            s.activation(out=cb.P[:, :], in_=fl(cb.sK), func=AF.Exp, scale=-1.0)
            last = s.activation(out=cb.PRg[:, :], in_=cb.prg16[:, :],
                                func=AF.Exp, scale=-1.0)
        return last
    if sweep2:
        def addsp(v):
            return v.tensor_add(out=cb.prg16[:, :], in0=fl(cb.sK), in1=fl(cb.sC))
        sp_add = sch.step("vector", addsp, deps=[ld])
        exps = sch.step("scalar", expf, deps=[ld, sp_add])
    else:
        exps = sch.step("scalar", expf, deps=[ld])

    def gatef(v):
        v.tensor_mul(out=cb.Ut[:, :], in0=fl(cb.kf), in1=cb.Pinv[:, :])
        v.tensor_mul(out=cb.Qh[:, :], in0=fl(cb.qf), in1=cb.Rinv[:, :])
        v.tensor_mul(out=cb.Kb[:, :], in0=fl(cb.kf), in1=cb.Rg[:, :])
        if sweep2:
            v.tensor_mul(out=cb.Qb[:, :], in0=fl(cb.qf), in1=cb.P[:, :])
        v.tensor_tensor(
            out=cb.plbc_r[0:64, :].rearrange("p (j k) -> p j k", k=64),
            in0=cb.plK[0:64, :].unsqueeze(2).broadcast_to([64, 8, 64]),
            in1=cm["idm"][0:64, 0:64].unsqueeze(1).broadcast_to([64, 8, 64]),
            op=ALU.mult)
        return v.tensor_tensor(
            out=cb.plbc_r[64:128, :].rearrange("p (j k) -> p j k", k=64),
            in0=cb.plK[64:128, :].unsqueeze(2).broadcast_to([64, 8, 64]),
            in1=cm["idm"][64:128, 64:128].unsqueeze(1).broadcast_to([64, 8, 64]),
            op=ALU.mult)
    gates = sch.step("vector", gatef, deps=[exps, cm["ready"]])
    return {"ld": ld, "exps": exps, "gates": gates}


MODULE_DOC_END = True


def rowbc(col8):
    return col8[:, :].unsqueeze(2).broadcast_to([128, 8, 64])


def emit_plbcast(nc, sch, cm, cb, pB, dep_gates):
    def mmf(t):
        t.matmul(pB[0:64, :], cm["ones64"][0:64, :], cb.plbc_r[0:64, :],
                 start=True, stop=True)
        return t.matmul(pB[64:128, :], cm["ones64"][64:128, :],
                        cb.plbc_r[64:128, :], start=True, stop=True)
    mm = sch.step("tensor", mmf, deps=dep_gates)
    ev = sch.step("vector", lambda v: v.tensor_copy(out=cb.plbc[:, :],
                                                    in_=pB[:, :]), deps=[mm])
    return ev


def emit_sweep1(nc, es, sch, cm, srcs, maps_o, start_dep, lvl=9, nch=NCH, co=None):
    sb = lambda n, s, dt: es.enter_context(nc.sbuf_tensor(n, s, dt))
    cbs = [ChunkBufs(nc, es, "_s1a", with_sweep2=False)]
    cbs.append(ChunkBufs(nc, es, "_s1b", with_sweep2=False,
                         share_prep=cbs[0]))
    LSm = sb("LSm", [128, 512], F32)
    LCm = sb("LCm", [128, 512], F32)
    Xtm = sb("Xtm", [128, 512], F32)
    LGm = sb("LGm", [128, 512], F32)
    Dm = sb("Dm", [128, 8], F32)
    Em = sb("Em", [128, 8], F32)
    LC_b = sb("LC_b", [128, 512], BF16)
    Yt_b = sb("Yt_b", [128, 512], BF16)
    Wl_b = sb("Wl_b", [128, DM], BF16)
    s3ms = [sb("s3m_s1a", [128, 256], BF16), sb("s3m_s1b", [128, 256], BF16)]

    ptr = cm["ps_bf"]
    ps3es = [cm["ps_a"][:, 0:128], cm["ps_a"][:, 128:256]]
    ps3os = [cm["ps_b"][:, 512:640], cm["ps_b"][:, 640:768]]
    pWs = [cm["ps_b"][:, 0:128], cm["ps_b"][:, 128:256]]
    pUU = cm["ps_c"]
    pQV = cm["ps_d"]
    pG = cm["ps_e"]
    pB = cm["ps_f"]

    def init(g):
        for b in (LSm, LCm, Xtm, LGm):
            g.memset(b[:, :], 0.0)
        g.memset(LC_b[:, :], 0.0)
        g.memset(Dm[:, :], 1.0)
        return g.memset(Em[:, :], 1.0)
    upd = sch.step("gpsimd", init,
                   deps=(start_dep if isinstance(start_dep, list)
                         else [start_dep]))

    upd_hist = [upd, upd]
    ev3_h = [None, None]
    wl_h = [None, None]
    ew_h = [None, None]

    last_gates = None
    for c in range(nch if lvl >= 1 else 0):
        cb = cbs[c % 2]
        pr = emit_chunk_prep(nc, sch, cm, cb, srcs, c, sweep2=False,
                             extra_dep=[upd_hist[c % 2]] +
                             ([last_gates] if last_gates else []))
        last_gates = pr["gates"]
        if lvl < 2:
            upd = pr["gates"]
            upd_hist[c % 2] = upd
            continue
        tms = {}
        for name, src, dst in (("Ut", cb.Ut, cb.Utm), ("Qh", cb.Qh, cb.Qhtm),
                               ("Kb", cb.Kb, cb.Kbtm)):
            def tf(t, src=src):
                for j in range(NP):
                    last = t.transpose(ptr[:, j * 128:(j + 1) * 128],
                                       src[:, j * 128:(j + 1) * 128],
                                       cm["idm"][:, :])
                return last
            ts = sch.step("tensor", tf, deps=[pr["gates"], cm["ptrtok"][0]])

            def vf(s, dst=dst):
                return s.copy(out=dst[:, :], in_=ptr[:, :])
            prev_ev = sch.step("scalar", vf, deps=[ts])
            cm["ptrtok"][0] = prev_ev
            tms[name] = prev_ev
        plb = emit_plbcast(nc, sch, cm, cb, pB, [pr["gates"], upd])
        if lvl < 3:
            upd = plb
            upd_hist[c % 2] = upd
            continue

        wl_steps, yt_evs, uu_steps, qv_steps, gg_steps = [], [], [], [], []
        prev_yt = plb
        for j in range(NP):
            he, ho = 2 * j, 2 * j + 1
            pj = slice(j * 128, (j + 1) * 128)
            jp = j % 2
            ps3e, ps3o, pW, s3m = ps3es[jp], ps3os[jp], pWs[jp], s3ms[jp]

            def s3f(t, j=j, pj=pj, cb=cb, ps3e=ps3e, ps3o=ps3o):
                t.matmul(ps3e[:, :], cb.Qh[0:64, pj], cb.Kb[0:64, pj],
                         start=True, stop=True)
                return t.matmul(ps3o[:, :], cb.Qh[64:128, pj],
                                cb.Kb[64:128, pj], start=True, stop=True)
            deps = [pr["gates"]] + ([ev3_h[jp]] if ev3_h[jp] else [])
            s3 = sch.step("tensor", s3f, deps=deps)
            del deps

            def ev3f(v, ps3e=ps3e, ps3o=ps3o, s3m=s3m):
                v.tensor_tensor(out=s3m[:, 0:128], in0=ps3e[:, :],
                                in1=cm["m_str"][:, :], op=ALU.mult)
                return v.tensor_tensor(out=s3m[:, 128:256], in0=ps3o[:, :],
                                       in1=cm["m_str"][:, :], op=ALU.mult)
            ev3_deps = [s3] + ([wl_h[jp]] if wl_h[jp] else [])
            ev3 = sch.step("vector", ev3f, deps=ev3_deps)
            ev3_h[jp] = ev3
            if lvl < 4:
                continue

            def wlf(t, j=j, pj=pj, cb=cb, pW=pW, s3m=s3m):
                for hh, hb in ((2 * j, 0), (2 * j + 1, 64)):
                    t.matmul(pW[:, (hh % 2) * 64:(hh % 2) * 64 + 64],
                             cb.Kb[hb:hb + 64, pj],
                             LC_b[hb:hb + 64, j * 64:(j + 1) * 64],
                             start=True, stop=False)
                    last = t.matmul(pW[:, (hh % 2) * 64:(hh % 2) * 64 + 64],
                                    s3m[:, (hb // 64) * 128:(hb // 64) * 128 + 128],
                                    cb.vt[:, hh * 64:(hh + 1) * 64],
                                    start=False, stop=True)
                return last
            wl = sch.step("tensor", wlf,
                          deps=[ev3, upd] + ([ew_h[jp]] if ew_h[jp] else []))
            wl_steps.append(wl)
            wl_h[jp] = wl

            def ewf(s, j=j, pW=pW):
                return s.copy(out=Wl_b[:, j * 128:(j + 1) * 128],
                              in_=pW[:, :])
            ew = sch.step("scalar", ewf, deps=[wl])
            ew_h[jp] = ew

            def ytf(t, j=j, cb=cb):
                for hh, hb in ((2 * j, 0), (2 * j + 1, 64)):
                    last = t.matmul(pB[hb:hb + 64, j * 64:(j + 1) * 64],
                                    cb.Kbtm[:, hh * 64:(hh + 1) * 64],
                                    cb.Utm[:, hh * 64:(hh + 1) * 64],
                                    start=True, stop=True)
                return last
            yt = sch.step("tensor", ytf, deps=[tms["Ut"], tms["Kb"], prev_yt])

            def eyf(v, j=j):
                return v.tensor_copy(out=Yt_b[:, j * 64:(j + 1) * 64],
                                     in_=pB[:, j * 64:(j + 1) * 64])
            ey = sch.step("vector", eyf, deps=[yt])
            yt_evs.append(ey)
            prev_yt = ey

            def uuf(t, j=j, cb=cb):
                for hh, hb in ((2 * j, 0), (2 * j + 1, 64)):
                    last = t.matmul(pUU[hb:hb + 64, j * 64:(j + 1) * 64],
                                    cb.Utm[:, hh * 64:(hh + 1) * 64],
                                    cb.Utm[:, hh * 64:(hh + 1) * 64],
                                    start=True, stop=True)
                return last
            uu_steps.append(sch.step("tensor", uuf, deps=[tms["Ut"], upd]))

            def qvf(t, j=j, cb=cb):
                for hh, hb in ((2 * j, 0), (2 * j + 1, 64)):
                    last = t.matmul(pQV[hb:hb + 64, j * 64:(j + 1) * 64],
                                    cb.Qhtm[:, hh * 64:(hh + 1) * 64],
                                    cb.vt[:, hh * 64:(hh + 1) * 64],
                                    start=True, stop=True)
                return last
            qv_steps.append(sch.step("tensor", qvf, deps=[tms["Qh"], upd]))

            def ggf(t, j=j, cb=cb):
                for hh, hb in ((2 * j, 0), (2 * j + 1, 64)):
                    last = t.matmul(pG[hb:hb + 64, j * 64:(j + 1) * 64],
                                    cb.Utm[:, hh * 64:(hh + 1) * 64],
                                    Wl_b[:, hh * 64:(hh + 1) * 64],
                                    start=True, stop=True)
                return last
            gg_steps.append(sch.step("tensor", ggf, deps=[ew, tms["Ut"], upd]))

        if lvl < 4:
            upd = ev3_h[(NP - 1) % 2]
            upd_hist[c % 2] = upd
            continue
        if lvl < 5:
            upd = sch.step("vector",
                           lambda v: v.tensor_copy(out=cm["tmpa"][:, :],
                                                   in_=pUU[:, :]),
                           deps=uu_steps + qv_steps + gg_steps + yt_evs)
            upd_hist[c % 2] = upd
            continue
        def updf(v, cb=cb):
            r8 = lambda x: x[:, :].rearrange("p (a b) -> p a b", b=64)
            v.tensor_tensor(out=cm["tmpa"][:, :], in0=pG[:, :], in1=LGm[:, :],
                            op=ALU.add)
            v.drain()
            v.tensor_tensor(out=r8(LGm), in0=r8(cm["tmpa"]), in1=rowbc(cb.plK),
                            op=ALU.mult)
            v.drain()
            v.tensor_tensor(out=r8(cm["tmpa"]), in0=r8(Yt_b), in1=rowbc(Em),
                            op=ALU.mult)
            v.drain()
            v.tensor_tensor(out=cm["tmpb"][:, :], in0=Xtm[:, :], in1=cm["tmpa"][:, :],
                            op=ALU.add)
            v.drain()
            v.tensor_tensor(out=Xtm[:, :], in0=cm["tmpb"][:, :], in1=cb.plbc[:, :],
                            op=ALU.mult)
            v.drain()
            v.tensor_tensor(out=cm["tmpa"][:, :], in0=pUU[:, :], in1=LSm[:, :],
                            op=ALU.add)
            v.drain()
            v.tensor_tensor(out=r8(cm["tmpb"]), in0=r8(cm["tmpa"]), in1=rowbc(cb.plK),
                            op=ALU.mult)
            v.drain()
            v.tensor_tensor(out=LSm[:, :], in0=cm["tmpb"][:, :], in1=cb.plbc[:, :],
                            op=ALU.mult)
            v.drain()
            v.tensor_tensor(out=cm["tmpa"][:, :], in0=pQV[:, :], in1=LCm[:, :],
                            op=ALU.add)
            v.drain()
            v.tensor_tensor(out=r8(LCm), in0=r8(cm["tmpa"]), in1=rowbc(cb.rlE),
                            op=ALU.mult)
            v.tensor_mul(out=Dm[:, :], in0=Dm[:, :], in1=cb.plK[:, :])
            v.tensor_mul(out=Em[:, :], in0=Em[:, :], in1=cb.rlE[:, :])
            v.drain()
            return v.tensor_copy(out=LC_b[:, :], in_=LCm[:, :])
        upd = sch.step("vector", updf,
                       deps=gg_steps + uu_steps + qv_steps + yt_evs + [plb])
        upd_hist[c % 2] = upd
        if co:
            co.pop(0)()

    while co:
        co.pop(0)()

    def mo(sy, sem):
        sy.dma_start(out=maps_o[:, 0:512], in_=LSm[:, :]).then_inc(sem, 16)
        sy.dma_start(out=maps_o[:, 512:1024], in_=LCm[:, :]).then_inc(sem, 16)
        sy.dma_start(out=maps_o[:, 1024:1536], in_=Xtm[:, :]).then_inc(sem, 16)
        sy.dma_start(out=maps_o[:, 1536:2048], in_=LGm[:, :]).then_inc(sem, 16)
        sy.dma_start(out=maps_o[:, 2048:2056], in_=Dm[:, :]).then_inc(sem, 16)
        sy.dma_start(out=maps_o[:, 2056:2064], in_=Em[:, :]).then_inc(sem, 16)
    fin = sch.dma("sync", mo, 6, deps=[upd])
    sch.step("gpsimd", lambda g: g.engine_nop() if hasattr(g, "engine_nop")
             else g.memset(cm["cpool"][:, 3:4], 0.0), deps=[fin])
    return fin


def build_phase1(stages=("xt", "proj", "sweep1"), s1_lvl=9, s1_nch=NCH):
    nc = bass.Bass("TRN2", target_bir_lowering=False, debug=False)
    sch = Sched(nc)
    x_d = nc.dram_tensor("x", [TC, DM], F32, kind="ExternalInput").ap()
    wd = {n: nc.dram_tensor(n, [DM, DM], F32, kind="ExternalInput").ap()
          for n in ("W_q", "W_k", "W_v", "Wg_K", "Wg_C")}
    wd["bg_K"] = nc.dram_tensor("bg_K", [1, DM], F32, kind="ExternalInput").ap()
    wd["bg_C"] = nc.dram_tensor("bg_C", [1, DM], F32, kind="ExternalInput").ap()

    q_o = nc.dram_tensor("q_s", [NCH, 128, FC * L], BF16,
                         kind="ExternalOutput").ap()
    k_o = nc.dram_tensor("k_s", [NCH, 128, FC * L], BF16,
                         kind="ExternalOutput").ap()
    v_o = nc.dram_tensor("v_s", [TC, DM], BF16, kind="ExternalOutput").ap()
    sK_o = nc.dram_tensor("spK_s", [NCH, 128, FC * L], FP16,
                          kind="ExternalOutput").ap()
    sC_o = nc.dram_tensor("spC_s", [NCH, 128, FC * L], FP16,
                          kind="ExternalOutput").ap()
    maps_o = nc.dram_tensor("maps", [128, 2064], F32, kind="ExternalOutput").ap()

    xt_dbg = nc.dram_tensor("xt_dbg", [128, FC, TC], BF16,
                            kind="ExternalOutput").ap()
    es = ExitStack()
    with es:
        cm = alloc_common(nc, es, sch)
        xT = es.enter_context(nc.sbuf_tensor("xT", [128, FC, TC], BF16))
        xt_done = emit_xT(nc, es, sch, cm, x_d, xT)
        last = xt_done
        if "proj" in stages:
            outs = {"q_s": q_o, "k_s": k_o, "spK_s": sK_o, "spC_s": sC_o,
                    "v_s": v_o}
            which = [st for st in ("q", "k", "gK", "gC", "v")
                     if "no_" + st not in stages]
            last = emit_projections(nc, es, sch, cm, xT, xt_done, wd, outs,
                                    which)
        if "sweep1" in stages:
            last = emit_sweep1(nc, es, sch, cm, (q_o, k_o, v_o, sK_o, sC_o),
                               maps_o, last, lvl=s1_lvl, nch=s1_nch)

        def xdbg(sy, sem):
            sy.dma_start(out=xt_dbg[:, :, :], in_=xT[:, :, :]).then_inc(sem, 16)
        fin = sch.dma("sync", xdbg, 1,
                      deps=(last if isinstance(last, list) else [last]) +
                      [xt_done])
        sch.step("gpsimd", lambda g: g.memset(cm["cpool"][:, 3:4], 0.0),
                 deps=[fin])
        sch.run(es)
    return nc


def emit_compose(nc, es, sch, cm, maps_d, gates_d, st, start_dep):
    sb = lambda n, s, dt: es.enter_context(nc.sbuf_tensor(n, s, dt))
    mapb = [sb("mapb0", [128, 2064], F32), sb("mapb1", [128, 2064], F32)]
    gt = sb("gt", [128, 8], F32)
    Dk = [sb("Dk0", [128, 8], F32), sb("Dk1", [128, 8], F32)]
    Ek = [sb("Ek0", [128, 8], F32), sb("Ek1", [128, 8], F32)]
    mXtb = [sb("mXtb0", [128, 512], BF16), sb("mXtb1", [128, 512], BF16)]
    C_b = sb("C_bc", [128, 512], BF16)
    dbc_r = [sb("dbc_r0", [128, 512], BF16), sb("dbc_r1", [128, 512], BF16)]
    tmpa = sb("tmpac", [128, 512], F32)
    tmpb = sb("tmpbc", [128, 512], F32)
    pXC = cm["ps_c"]
    pB = cm["ps_f"]

    def initf(g):
        for b in (st["S0f"], st["C0f"], st["Gnf"]):
            g.memset(b[:, :], 0.0)
        return g.memset(C_b[:, :], 0.0)
    init = sch.step("gpsimd", initf, deps=[start_dep])

    def gld(sy, sem):
        sy.dma_start(out=gt[:, :], in_=gates_d[:, :]).then_inc(sem, 16)
    gl = sch.dma("sync", gld, 1)

    fold_h = [init, init]
    dm_h = [init, init]
    upd = init

    def emit_mlbl(k):
        par = k % 2

        def mld(sy, sem, k=k, par=par):
            sy.dma_start(out=mapb[par][:, :],
                         in_=maps_d[k, :, :]).then_inc(sem, 16)
        ml = sch.dma("sync", mld, 1, deps=[fold_h[par]])

        def blendf(v, k=k, par=par):
            g_ap = gt[:, k:k + 1]
            v.tensor_scalar(out=Dk[par][:, :], in0=mapb[par][:, 2048:2056],
                            scalar1=1.0, scalar2=g_ap,
                            op0=ALU.subtract, op1=ALU.mult)
            v.tensor_scalar(out=Ek[par][:, :], in0=mapb[par][:, 2056:2064],
                            scalar1=1.0, scalar2=g_ap,
                            op0=ALU.subtract, op1=ALU.mult)
            v.drain()
            v.tensor_scalar_add(out=Dk[par][:, :], in0=Dk[par][:, :],
                                scalar1=1.0)
            v.tensor_scalar_add(out=Ek[par][:, :], in0=Ek[par][:, :],
                                scalar1=1.0)
            v.tensor_scalar(out=mapb[par][:, 0:2048],
                            in0=mapb[par][:, 0:2048],
                            scalar1=g_ap, scalar2=None, op0=ALU.mult)
            v.drain()
            v.tensor_copy(out=mXtb[par][:, :], in_=mapb[par][:, 1024:1536])
            v.tensor_tensor(
                out=dbc_r[par][0:64, :].rearrange("p (j k) -> p j k", k=64),
                in0=Dk[par][0:64, :].unsqueeze(2).broadcast_to([64, 8, 64]),
                in1=cm["idm"][0:64, 0:64].unsqueeze(1).broadcast_to([64, 8, 64]),
                op=ALU.mult)
            return v.tensor_tensor(
                out=dbc_r[par][64:128, :].rearrange("p (j k) -> p j k", k=64),
                in0=Dk[par][64:128, :].unsqueeze(2).broadcast_to([64, 8, 64]),
                in1=cm["idm"][64:128, 64:128].unsqueeze(1)
                .broadcast_to([64, 8, 64]),
                op=ALU.mult)
        par = k % 2
        bl = sch.step("vector", blendf,
                      deps=[ml, gl, cm["ready"], fold_h[par], dm_h[par]])
        return bl

    bl = emit_mlbl(0)
    for k in range(NCORE - 1):
        par = k % 2

        def dmxcf(t, par=par):
            t.matmul(pB[0:64, :], cm["ones64"][0:64, :], dbc_r[par][0:64, :],
                     start=True, stop=True)
            t.matmul(pB[64:128, :], cm["ones64"][64:128, :],
                     dbc_r[par][64:128, :], start=True, stop=True)
            for j in range(NP):
                for hb in (0, 64):
                    last = t.matmul(pXC[hb:hb + 64, j * 64:(j + 1) * 64],
                                    mXtb[par][hb:hb + 64, j * 64:(j + 1) * 64],
                                    C_b[hb:hb + 64, j * 64:(j + 1) * 64],
                                    start=True, stop=True)
            return last
        dmxc = sch.step("tensor", dmxcf, deps=[bl, upd])
        dm_h[par] = dmxc
        if k < NCORE - 2:
            bl = emit_mlbl(k + 1)

        def foldf(v, k=k, par=par):
            r8 = lambda x: x[:, :].rearrange("p (a b) -> p a b", b=64)
            v.tensor_tensor(out=r8(tmpa), in0=r8(st["Gnf"]), in1=rowbc(Dk[par]),
                            op=ALU.mult)
            v.drain()
            v.tensor_tensor(out=tmpb[:, :], in0=tmpa[:, :], in1=pXC[:, :],
                            op=ALU.subtract)
            v.drain()
            v.tensor_tensor(out=st["Gnf"][:, :], in0=tmpb[:, :],
                            in1=mapb[par][:, 1536:2048], op=ALU.subtract)
            v.drain()
            v.tensor_tensor(out=r8(tmpa), in0=r8(st["S0f"]), in1=rowbc(Dk[par]),
                            op=ALU.mult)
            v.drain()
            v.tensor_tensor(out=tmpb[:, :], in0=tmpa[:, :], in1=pB[:, :],
                            op=ALU.mult)
            v.drain()
            v.tensor_tensor(out=st["S0f"][:, :], in0=tmpb[:, :],
                            in1=mapb[par][:, 0:512], op=ALU.add)
            v.drain()
            v.tensor_tensor(out=r8(tmpa), in0=r8(st["C0f"]), in1=rowbc(Ek[par]),
                            op=ALU.mult)
            v.drain()
            v.tensor_tensor(out=st["C0f"][:, :], in0=tmpa[:, :],
                            in1=mapb[par][:, 512:1024], op=ALU.add)
            v.drain()
            return v.tensor_copy(out=C_b[:, :], in_=st["C0f"][:, :])
        upd = sch.step("vector", foldf, deps=[dmxc])
        fold_h[par] = upd

    def castf(v):
        v.tensor_copy(out=st["S0b"][:, :], in_=st["S0f"][:, :])
        v.tensor_copy(out=st["C0b"][:, :], in_=st["C0f"][:, :])
        v.tensor_copy(out=st["Gnb"][:, :], in_=st["Gnf"][:, :])
        return v.tensor_scalar_mul(out=st["Cnb"][:, :], in0=st["C0f"][:, :],
                                   scalar1=-1.0)
    return sch.step("vector", castf, deps=[upd])


def emit_sweep2(nc, es, sch, cm, srcs, st, OTs, start_dep):
    cbs = [ChunkBufs(nc, es, "_s2a", with_sweep2=True),
           ChunkBufs(nc, es, "_s2b", with_sweep2=True)]
    sb = lambda n, s, dt: es.enter_context(nc.sbuf_tensor(n, s, dt))
    s1m = sb("s1m_2", [128, 256], BF16)
    s2m = sb("s2m_2", [128, 256], BF16)
    s3m = sb("s3m_2", [128, 256], BF16)
    rb = sb("rb_2", [128, 128], BF16)
    Wn_b = sb("Wn_b2", [128, DM], BF16)

    ptr = cm["ps_bf"]
    S1e = cm["ps_a"][:, 0:128]
    S3e = cm["ps_a"][:, 128:256]
    S2e = cm["ps_a"][:, 256:384]
    pR = cm["ps_a"][:, 384:512]
    S1o = cm["ps_b"][:, 512:640]
    S3o = cm["ps_b"][:, 640:768]
    S2o = cm["ps_b"][:, 768:896]
    pW = cm["ps_b"][:, 0:128]
    pOT = cm["ps_b"][:, 128:256]
    pUU = cm["ps_c"]
    pQV = cm["ps_d"]
    pGn = cm["ps_e"]
    pB = cm["ps_f"]

    upd = start_dep
    upd_hist = [start_dep, start_dep]
    prev = {"eva": None, "b": None, "ewn": None, "evr": None, "s2": None,
            "ev2": None, "otug": None, "evot": None, "tmev": None, "plb": None}
    for c in range(NCH):
        par = c % 2
        cb = cbs[par]
        pr = emit_chunk_prep(nc, sch, cm, cb, srcs, c, sweep2=True,
                             extra_dep=[upd_hist[par]])
        tms = {}
        for name, src, dst in (("Ut", cb.Ut, cb.Utm), ("Qh", cb.Qh, cb.Qhtm)):
            def tf(t, src=src):
                for j in range(NP):
                    last = t.transpose(ptr[:, j * 128:(j + 1) * 128],
                                       src[:, j * 128:(j + 1) * 128],
                                       cm["idm"][:, :])
                return last
            ts = sch.step("tensor", tf, deps=[pr["gates"], cm["ptrtok"][0]])

            def vf(s, dst=dst):
                return s.copy(out=dst[:, :], in_=ptr[:, :])
            prev_ev = sch.step("scalar", vf, deps=[ts])
            cm["ptrtok"][0] = prev_ev
            tms[name] = prev_ev
        plb = emit_plbcast(nc, sch, cm, cb, pB,
                           [pr["gates"]] + ([prev["plb"]] if prev["plb"]
                                            else [upd]))
        prev["plb"] = plb

        uqg_steps, ot_evs = [], []
        for j in range(NP):
            pj = slice(j * 128, (j + 1) * 128)

            def af(t, j=j, pj=pj, cb=cb):
                t.matmul(S1e[:, :], cb.Ut[0:64, pj], cb.Qb[0:64, pj],
                         start=True, stop=True)
                t.matmul(S1o[:, :], cb.Ut[64:128, pj], cb.Qb[64:128, pj],
                         start=True, stop=True)
                t.matmul(S3e[:, :], cb.Qh[0:64, pj], cb.Kb[0:64, pj],
                         start=True, stop=True)
                return t.matmul(S3o[:, :], cb.Qh[64:128, pj],
                                cb.Kb[64:128, pj], start=True, stop=True)
            a = sch.step("tensor", af, deps=[pr["gates"]] +
                         ([prev["eva"]] if prev["eva"] else []))

            def evaf(v):
                v.tensor_tensor(out=s1m[:, 0:128], in0=S1e[:, :],
                                in1=cm["m_incl"][:, :], op=ALU.mult)
                v.tensor_tensor(out=s1m[:, 128:256], in0=S1o[:, :],
                                in1=cm["m_incl"][:, :], op=ALU.mult)
                v.tensor_tensor(out=s3m[:, 0:128], in0=S3e[:, :],
                                in1=cm["m_strn"][:, :], op=ALU.mult)
                return v.tensor_tensor(out=s3m[:, 128:256], in0=S3o[:, :],
                                       in1=cm["m_strn"][:, :], op=ALU.mult)
            eva = sch.step("vector", evaf, deps=[a] +
                           ([prev["otug"]] if prev["otug"] else []))
            prev["eva"] = eva

            def bf(t, j=j, pj=pj, cb=cb):
                for hh, hb in ((2 * j, 0), (2 * j + 1, 64)):
                    msl = slice((hb // 64) * 128, (hb // 64) * 128 + 128)
                    t.matmul(pW[:, (hh % 2) * 64:(hh % 2) * 64 + 64],
                             cb.Kb[hb:hb + 64, pj],
                             st["Cnb"][hb:hb + 64, j * 64:(j + 1) * 64],
                             start=True, stop=False)
                    t.matmul(pW[:, (hh % 2) * 64:(hh % 2) * 64 + 64],
                             s3m[:, msl], cb.vt[:, hh * 64:(hh + 1) * 64],
                             start=False, stop=True)
                    t.matmul(pR[hb:hb + 64, :],
                             st["S0b"][hb:hb + 64, j * 64:(j + 1) * 64],
                             cb.Qb[hb:hb + 64, pj], start=True, stop=False)
                    last = t.matmul(pR[hb:hb + 64, :],
                                    cb.Utm[:, hh * 64:(hh + 1) * 64],
                                    s1m[:, msl], start=False, stop=True)
                return last
            b = sch.step("tensor", bf,
                         deps=[eva, tms["Ut"], upd] +
                         ([prev["ewn"]] if prev["ewn"] else []) +
                         ([prev["evr"]] if prev["evr"] else []))

            def ewnf(s, j=j):
                return s.copy(out=Wn_b[:, j * 128:(j + 1) * 128], in_=pW[:, :])
            ewn = sch.step("scalar", ewnf, deps=[b])
            prev["ewn"] = ewn

            def evrf(v, j=j, pj=pj, cb=cb):
                return v.tensor_tensor(out=rb[:, :], in0=pR[:, :],
                                       in1=cb.PRg[:, pj], op=ALU.mult)
            evr = sch.step("vector", evrf, deps=[b, pr["gates"]] +
                           ([prev["otug"]] if prev["otug"] else []))
            prev["evr"] = evr

            def s2f(t, pj=pj, cb=cb):
                t.matmul(S2e[:, :], cb.Qh[0:64, pj], rb[0:64, :],
                         start=True, stop=True)
                return t.matmul(S2o[:, :], cb.Qh[64:128, pj], rb[64:128, :],
                                start=True, stop=True)
            s2 = sch.step("tensor", s2f, deps=[evr] +
                          ([prev["ev2"]] if prev["ev2"] else []))
            prev["s2"] = s2

            def ev2f(v):
                v.tensor_tensor(out=s2m[:, 0:128], in0=S2e[:, :],
                                in1=cm["m_incl"][:, :], op=ALU.mult)
                return v.tensor_tensor(out=s2m[:, 128:256], in0=S2o[:, :],
                                       in1=cm["m_incl"][:, :], op=ALU.mult)
            ev2 = sch.step("vector", ev2f, deps=[s2] +
                           ([prev["otug"]] if prev["otug"] else []))
            prev["ev2"] = ev2

            def otugf(t, j=j, pj=pj, cb=cb):
                for hh, hb in ((2 * j, 0), (2 * j + 1, 64)):
                    sl64 = slice(j * 64, (j + 1) * 64)
                    msl = slice((hb // 64) * 128, (hb // 64) * 128 + 128)
                    t.matmul(pOT[hb:hb + 64, :], st["C0b"][hb:hb + 64, sl64],
                             rb[hb:hb + 64, :], start=True, stop=False)
                    t.matmul(pOT[hb:hb + 64, :],
                             cb.vt[:, hh * 64:(hh + 1) * 64], s2m[:, msl],
                             start=False, stop=False)
                    t.matmul(pOT[hb:hb + 64, :], st["Gnb"][hb:hb + 64, sl64],
                             cb.Qb[hb:hb + 64, pj], start=False, stop=False)
                    t.matmul(pOT[hb:hb + 64, :],
                             Wn_b[:, hh * 64:(hh + 1) * 64], s1m[:, msl],
                             start=False, stop=True)
                    t.matmul(pUU[hb:hb + 64, j * 64:(j + 1) * 64],
                             cb.Utm[:, hh * 64:(hh + 1) * 64],
                             cb.Utm[:, hh * 64:(hh + 1) * 64],
                             start=True, stop=True)
                    t.matmul(pQV[hb:hb + 64, j * 64:(j + 1) * 64],
                             cb.Qhtm[:, hh * 64:(hh + 1) * 64],
                             cb.vt[:, hh * 64:(hh + 1) * 64],
                             start=True, stop=True)
                    last = t.matmul(pGn[hb:hb + 64, j * 64:(j + 1) * 64],
                                    cb.Utm[:, hh * 64:(hh + 1) * 64],
                                    Wn_b[:, hh * 64:(hh + 1) * 64],
                                    start=True, stop=True)
                return last
            otug = sch.step("tensor", otugf,
                            deps=[eva, ev2, ewn, evr, tms["Ut"], tms["Qh"],
                                  upd] +
                            ([prev["evot"]] if prev["evot"] else []))
            prev["otug"] = otug
            uqg_steps.append(otug)

            def evotf(v, j=j, c=c):
                return v.tensor_copy(out=OTs[:, j, c * L:(c + 1) * L],
                                     in_=pOT[:, :])
            evot = sch.step("vector", evotf, deps=[otug])
            prev["evot"] = evot
            ot_evs.append(evot)

        def updf(v, cb=cb):
            r8 = lambda x: x[:, :].rearrange("p (a b) -> p a b", b=64)
            v.tensor_tensor(out=cm["tmpa"][:, :], in0=pUU[:, :], in1=st["S0f"][:, :],
                            op=ALU.add)
            v.drain()
            v.tensor_tensor(out=r8(cm["tmpb"]), in0=r8(cm["tmpa"]), in1=rowbc(cb.plK),
                            op=ALU.mult)
            v.drain()
            v.tensor_tensor(out=st["S0f"][:, :], in0=cm["tmpb"][:, :],
                            in1=cb.plbc[:, :], op=ALU.mult)
            v.drain()
            v.tensor_tensor(out=cm["tmpa"][:, :], in0=pQV[:, :], in1=st["C0f"][:, :],
                            op=ALU.add)
            v.drain()
            v.tensor_tensor(out=r8(st["C0f"]), in0=r8(cm["tmpa"]), in1=rowbc(cb.rlE),
                            op=ALU.mult)
            v.drain()
            v.tensor_tensor(out=cm["tmpa"][:, :], in0=pGn[:, :], in1=st["Gnf"][:, :],
                            op=ALU.add)
            v.drain()
            v.tensor_tensor(out=r8(st["Gnf"]), in0=r8(cm["tmpa"]), in1=rowbc(cb.plK),
                            op=ALU.mult)
            v.drain()
            v.tensor_copy(out=st["S0b"][:, :], in_=st["S0f"][:, :])
            v.tensor_copy(out=st["C0b"][:, :], in_=st["C0f"][:, :])
            v.tensor_copy(out=st["Gnb"][:, :], in_=st["Gnf"][:, :])
            return v.tensor_scalar_mul(out=st["Cnb"][:, :], in0=st["C0f"][:, :],
                                       scalar1=-1.0)
        upd = sch.step("vector", updf, deps=uqg_steps + ot_evs + [plb, upd])
        upd_hist[par] = upd
    return upd


def emit_quant_sg(nc, sch, cm, qx, ws_f32, W, deps):
    def sgf(s):
        s.activation(out=qx["xa"][:, 0:W], in_=ws_f32[:, 0:W], func=AF.Abs)
        return s.activation(out=qx["sgb"][:, 0:W], in_=ws_f32[:, 0:W],
                            func=AF.Sign)
    return sch.step("scalar", sgf, deps=deps)


def emit_quant(nc, sch, cm, qb, ws_f32, W, out_bf16, deps, prefix=None):
    sg = emit_quant_sg(nc, sch, cm, qb, ws_f32, W, deps)
    return emit_quant_qv(nc, sch, cm, qb, qb, ws_f32, W, out_bf16,
                         list(deps) + [sg], prefix=prefix)


def emit_quant_qv(nc, sch, cm, qb, qx, ws_f32, W, out_bf16, deps,
                  prefix=None):
    NB = W // 16

    def vf(v):
        if prefix is not None:
            prefix(v)
            v.drain()
        v.tensor_reduce(out=qb["bm"][:, 0:NB],
                        in_=ws_f32[:, 0:W].rearrange("p (b w) -> p b w", w=16),
                        axis=mybir.AxisListType.X, op=ALU.max,
                        apply_absolute_value=True)
        v.drain()
        v.tensor_scalar(out=qb["sc"][:, 0:NB], in0=qb["bm"][:, 0:NB],
                        scalar1=1.0 / 6.0, scalar2=1e-12, op0=ALU.mult,
                        op1=ALU.max)
        v.drain()
        v.reciprocal(out=qb["ri"][:, 0:NB], in_=qb["sc"][:, 0:NB])
        v.drain()
        v.tensor_tensor(
            out=qb["tn"][:, 0:W].rearrange("p (b w) -> p b w", w=16),
            in0=qx["xa"][:, 0:W].rearrange("p (b w) -> p b w", w=16),
            in1=qb["ri"][:, 0:NB].unsqueeze(2).broadcast_to([128, NB, 16]),
            op=ALU.mult)
        v.drain()
        v.tensor_scalar(out=qb["cqA"][:, 0:W], in0=qb["tn"][:, 0:W],
                        scalar1=0.25, scalar2=None, op0=ALU.is_gt)
        for m in (0.75, 1.25, 1.75):
            v.drain()
            v.scalar_tensor_tensor(out=qb["cqA"][:, 0:W],
                                   in0=qb["tn"][:, 0:W], scalar=m,
                                   in1=qb["cqA"][:, 0:W],
                                   op0=ALU.is_gt, op1=ALU.add)
        v.tensor_scalar(out=qb["cqB"][:, 0:W], in0=qb["tn"][:, 0:W],
                        scalar1=2.5, scalar2=None, op0=ALU.is_gt)
        v.drain()
        v.scalar_tensor_tensor(out=qb["cqB"][:, 0:W], in0=qb["tn"][:, 0:W],
                               scalar=3.5, in1=qb["cqB"][:, 0:W],
                               op0=ALU.is_gt, op1=ALU.add)
        v.drain()
        v.scalar_tensor_tensor(out=qb["cqB"][:, 0:W], in0=qb["cqA"][:, 0:W],
                               scalar=0.5, in1=qb["cqB"][:, 0:W],
                               op0=ALU.mult, op1=ALU.add)
        v.drain()
        v.tensor_scalar(out=qb["cqA"][:, 0:W], in0=qb["tn"][:, 0:W],
                        scalar1=5.0, scalar2=2.0, op0=ALU.is_gt, op1=ALU.mult)
        v.drain()
        v.tensor_tensor(out=qb["cqA"][:, 0:W], in0=qb["cqA"][:, 0:W],
                        in1=qb["cqB"][:, 0:W], op=ALU.add)
        v.drain()
        v.tensor_tensor(out=qb["cqA"][:, 0:W], in0=qb["cqA"][:, 0:W],
                        in1=qx["sgb"][:, 0:W], op=ALU.mult)
        v.drain()
        return v.tensor_tensor(
            out=out_bf16[:, 0:W].rearrange("p (b w) -> p b w", w=16),
            in0=qb["cqA"][:, 0:W].rearrange("p (b w) -> p b w", w=16),
            in1=qb["sc"][:, 0:NB].unsqueeze(2).broadcast_to([128, NB, 16]),
            op=ALU.mult)
    return sch.step("vector", vf, deps=list(deps))


def alloc_qb(nc, es, tag, W):
    sb = lambda n, s, dt: es.enter_context(nc.sbuf_tensor(n + tag, s, dt))
    return {"xa": sb("q_xa", [128, W], F32), "bm": sb("q_bm", [128, W // 16], F32),
            "sc": sb("q_sc", [128, W // 16], F32),
            "ri": sb("q_ri", [128, W // 16], F32),
            "tn": sb("q_tn", [128, W], F32), "cqA": sb("q_cqA", [128, W], BF16),
            "cqB": sb("q_cqB", [128, W], BF16), "sgb": sb("q_sgb", [128, W], BF16)}


def emit_ln(nc, sch, cm, lb, psum_in, xres, out_f32, deps, sq_war=()):
    def v1(v):
        if isinstance(psum_in, (list, tuple)):
            v.tensor_tensor(out=lb["uf"][:, 0:512], in0=psum_in[0][:, :],
                            in1=xres[:, 0:512], op=ALU.add)
            v.tensor_tensor(out=lb["uf"][:, 512:1024], in0=psum_in[1][:, :],
                            in1=xres[:, 512:1024], op=ALU.add)
        else:
            v.tensor_tensor(out=lb["uf"][:, :], in0=psum_in[:, :],
                            in1=xres[:, :], op=ALU.add)
        v.drain()
        return v.reduce_sum(out=lb["st4"][:, 0:1], in_=lb["uf"][:, :],
                            axis=mybir.AxisListType.X)
    s1 = sch.step("vector", v1, deps=deps)

    def sq(s):
        return s.activation(out=lb["sq"][:, :], in_=lb["uf"][:, :],
                            func=AF.Square)
    s2 = sch.step("scalar", sq, deps=[s1] + list(sq_war))

    def v2(v):
        v.reduce_sum(out=lb["st4"][:, 1:2], in_=lb["sq"][:, :],
                     axis=mybir.AxisListType.X)
        v.drain()
        v.tensor_scalar_mul(out=lb["st4"][:, 0:2], in0=lb["st4"][:, 0:2],
                            scalar1=1.0 / DM)
        v.drain()
        v.tensor_tensor(out=lb["st4"][:, 2:3], in0=lb["st4"][:, 0:1],
                        in1=lb["st4"][:, 0:1], op=ALU.mult)
        v.drain()
        return v.tensor_tensor(out=lb["st4"][:, 3:4], in0=lb["st4"][:, 1:2],
                               in1=lb["st4"][:, 2:3], op=ALU.subtract)
    s3 = sch.step("vector", v2, deps=[s2])

    def s4f(s):
        s.activation(out=lb["st4"][:, 2:3], in_=lb["st4"][:, 3:4], func=AF.Ln,
                     bias=EPS)
        s.drain()
        return s.activation(out=lb["st4"][:, 3:4], in_=lb["st4"][:, 2:3],
                            func=AF.Exp, scale=-0.5)
    s4 = sch.step("scalar", s4f, deps=[s3])

    def v3(v):
        return v.tensor_scalar(out=out_f32[:, :], in0=lb["uf"][:, :],
                               scalar1=lb["st4"][:, 0:1],
                               scalar2=lb["st4"][:, 3:4],
                               op0=ALU.subtract, op1=ALU.mult)
    return sch.step("vector", v3, deps=[s4]), s1, s3




def emit_wtr(nc, sch, cm, ptr, wq, qds, evhist, tras, slabs, i):
    wdram, dst, sl, Wrow = slabs[i]
    par = i % 2
    ntile = Wrow // 128
    prev = qds[i]
    for g0 in range(0, ntile, 8):
        gn = min(8, ntile - g0)

        def trg(t, g0=g0, gn=gn, par=par):
            for it in range(gn):
                lastt = t.transpose(ptr[:, it * 128:(it + 1) * 128],
                                    wq[par][:, (g0 + it) * 128:
                                            (g0 + it + 1) * 128],
                                    cm["idm"][:, :])
            return lastt
        tg = sch.step("tensor", trg,
                      deps=[prev] + ([evhist[0]] if evhist[0] else []))

        def evg(v, g0=g0, gn=gn, dst=dst, sl=sl):
            return v.tensor_copy(
                out=dst[:, g0:g0 + gn, sl * 128:(sl + 1) * 128],
                in_=ptr[:, 0:gn * 128].rearrange("p (a b) -> p a b", b=128))
        prev = sch.step("vector", evg, deps=[tg])
        evhist[0] = prev
    tras[i] = prev


def emit_fdn(nc, sch, cm, ptr, hqc, hqT, wqdT, pDN, dn_mms, hist, ICH,
             hq, ici, i0, ilen, par, tc):
    war = lambda k, n=2: ([hist[k][-n]] if len(hist[k]) >= n else [])
    ntile = ilen // 128

    def htr(t, ntile=ntile, par=par):
        for it in range(ntile):
            lastt = t.transpose(
                ptr[:, (par * 4 + it) * 128:(par * 4 + it + 1) * 128],
                hqc[par][:, it * 128:(it + 1) * 128], cm["idm"][:, :])
        return lastt
    ht = sch.step("tensor", htr, deps=[hq] + war("hte"))
    hist["ht"].append(ht)

    def htef(s, ntile=ntile, par=par):
        return s.copy(
            out=hqT[par][:, 0:ntile, :].rearrange("p a b -> p (a b)"),
            in_=ptr[:, par * 512:par * 512 + ntile * 128])
    hte = sch.step("scalar", htef, deps=[ht] + war("dm", 1))
    hist["hte"].append(hte)

    def dmf(t, i0=i0, ntile=ntile, ici=ici, par=par, tc=tc):
        for it in range(ntile):
            git = i0 // 128 + it
            for n2 in range(2):
                last_i = t.matmul(
                    pDN[:, n2 * 512:(n2 + 1) * 512], hqT[par][:, it, :],
                    wqdT[:, git, n2 * 512:(n2 + 1) * 512],
                    start=(ici == 0 and it == 0),
                    stop=(ici == len(ICH) - 1 and it == ntile - 1))
        return last_i
    dm = sch.step("tensor", dmf, deps=[hte] + dn_mms[-1:] +
                  (hist["ln2s1"][-1:] if ici == 0 else []))
    dn_mms.append(dm)
    hist["dm"].append(dm)


def build_fused():
    nc = bass.Bass("TRN2", target_bir_lowering=False, debug=False)
    sch = Sched(nc)
    x_d = nc.dram_tensor("x", [TC, DM], F32, kind="ExternalInput").ap()
    wd = {n: nc.dram_tensor(n, [DM, DM], F32, kind="ExternalInput").ap()
          for n in ("W_q", "W_k", "W_v", "Wg_K", "Wg_C")}
    wd["bg_K"] = nc.dram_tensor("bg_K", [1, DM], F32, kind="ExternalInput").ap()
    wd["bg_C"] = nc.dram_tensor("bg_C", [1, DM], F32, kind="ExternalInput").ap()
    gates_d = nc.dram_tensor("gates", [128, 8], F32, kind="ExternalInput").ap()
    Wo_d = nc.dram_tensor("W_o", [DM, DM], F32, kind="ExternalInput").ap()
    y_d = nc.dram_tensor("y", [TC, DM], F32, kind="ExternalOutput").ap()

    q_s = nc.dram_tensor("q_s", [NCH, 128, FC * L], BF16, kind="Internal").ap()
    k_s = nc.dram_tensor("k_s", [NCH, 128, FC * L], BF16, kind="Internal").ap()
    v_s = nc.dram_tensor("v_s", [TC, DM], BF16, kind="Internal").ap()
    sK_s = nc.dram_tensor("spK_s", [NCH, 128, FC * L], FP16,
                          kind="Internal").ap()
    sC_s = nc.dram_tensor("spC_s", [NCH, 128, FC * L], FP16,
                          kind="Internal").ap()
    x1_d = nc.dram_tensor("x1_s", [TC, DM], F32, kind="Internal").ap()
    mb_in = nc.dram_tensor("mb_in", [128, 2064], F32, kind="Internal")
    mb_out = nc.dram_tensor("mb_out", [NCORE, 128, 2064], F32, kind="Internal")
    wg_my = nc.dram_tensor("wg_my", [384, DM], F32, kind="ExternalInput").ap()
    wu_my = nc.dram_tensor("wu_my", [384, DM], F32, kind="ExternalInput").ap()
    wdn_my = nc.dram_tensor("wdn_my", [128, DI], F32, kind="ExternalInput").ap()
    wqb_in = nc.dram_tensor("wqb_in", [128, 8960], BF16, kind="Internal")
    wqb_out = nc.dram_tensor("wqb_out", [NCORE, 128, 8960], BF16,
                             kind="Internal")

    es = ExitStack()
    with es:
        cm = alloc_common(nc, es, sch)

        qb = alloc_qb(nc, es, "_f", 1024)
        ws = [es.enter_context(nc.sbuf_tensor("ws0", [128, DM], F32)),
              es.enter_context(nc.sbuf_tensor("ws1", [128, DM], F32))]
        wq = [es.enter_context(nc.sbuf_tensor("wq0", [128, DM], BF16)),
              es.enter_context(nc.sbuf_tensor("wq1", [128, DM], BF16))]


        myslabs = [(wdn_my, 0, DI, 6144),
                   (wg_my, 0, DM, 0), (wu_my, 0, DM, 3072),
                   (wg_my, 1, DM, 1024), (wu_my, 1, DM, 4096),
                   (wg_my, 2, DM, 2048), (wu_my, 2, DM, 5120)]
        wctx = {"ld": [cm["ready"], cm["ready"]], "ev": [None],
                "tr": {}, "qd": [], "fins": [], "pi": 0}

        def co_piece(wdram, row, p0, plen, off):
            def go():
                par = wctx["pi"] % 2
                wctx["pi"] += 1

                def wl(sy, sem, wdram=wdram, row=row, p0=p0, plen=plen,
                       par=par):
                    sy.dma_start(out=ws[par][:, 0:plen],
                                 in_=wdram[row * 128:(row + 1) * 128,
                                           p0:p0 + plen]
                                 ).then_inc(sem, 16)
                ld = sch.dma("sync", wl, 1, deps=[wctx["ld"][par]])
                qd = emit_quant(nc, sch, cm, qb, ws[par][:, 0:plen],
                                plen, wq[par][:, 0:plen],
                                [ld] + wctx["qd"][-1:])
                wctx["qd"].append(qd)
                wctx["ld"][par] = qd
                ntile = plen // 128
                sview = wqstage[:, off + p0:off + p0 + plen].rearrange(
                    "p (a b) -> p a b", b=128)

                def trg(t, ntile=ntile, par=par):
                    for it in range(ntile):
                        lastt = t.transpose(
                            cm["ps_bf"][:, it * 128:(it + 1) * 128],
                            wq[par][:, it * 128:(it + 1) * 128],
                            cm["idm"][:, :])
                    return lastt
                tg = sch.step("tensor", trg,
                              deps=[qd, cm["ptrtok"][0]])

                def evg(v, ntile=ntile, sview=sview):
                    return v.tensor_copy(
                        out=sview[:, :, :],
                        in_=cm["ps_bf"][:, 0:ntile * 128]
                        .rearrange("p (a b) -> p a b", b=128))
                prev = sch.step("vector", evg, deps=[tg])
                cm["ptrtok"][0] = prev
                wctx["ev"][0] = prev
                wctx["fins"].append(prev)
            return go

        co = []
        for i in range(len(myslabs)):
            wdram, row, Wrow, off = myslabs[i]
            for p0 in range(0, Wrow, 512):
                plen = min(512, Wrow - p0)
                co.append(co_piece(wdram, row, p0, plen, off))

        esA = ExitStack()
        with esA:
            wqstage = esA.enter_context(
                nc.sbuf_tensor("wqstage", [128, 8960], BF16))
            xT = esA.enter_context(nc.sbuf_tensor("xT", [128, FC, TC], BF16))
            xt_done = emit_xT(nc, esA, sch, cm, x_d, xT)
            outs = {"q_s": q_s, "k_s": k_s, "spK_s": sK_s, "spC_s": sC_s,
                    "v_s": v_s}
            pj_done = emit_projections(nc, esA, sch, cm, xT, xt_done, wd, outs)
            s1_fin = emit_sweep1(nc, esA, sch, cm,
                                 (q_s, k_s, v_s, sK_s, sC_s),
                                 mb_in.ap(), pj_done, co=co)

            def wst(sy, sem):
                sy.dma_start(out=wqb_in.ap()[:, :],
                             in_=wqstage[:, :]).then_inc(sem, 16)
            wst_d = sch.dma("sync", wst, 1, deps=wctx["fins"])

        def ccf(g):
            return g.collective_compute(
                "AllGather", ALU.bypass, replica_groups=[list(range(NCORE))],
                ins=[mb_in.ap().opt()], outs=[mb_out.ap().opt()])
        cc = sch.step("gpsimd", ccf, deps=[s1_fin, wst_d], coll=True)

        def ccw(g):
            return g.collective_compute(
                "AllGather", ALU.bypass, replica_groups=[list(range(NCORE))],
                ins=[wqb_in.ap().opt()], outs=[wqb_out.ap().opt()])
        ccw_s = sch.step("gpsimd", ccw, deps=[wst_d], coll=True)

        esB = ExitStack()
        with esB:
            sb1 = lambda n, s, dt: esB.enter_context(nc.sbuf_tensor(n, s, dt))
            st = {"S0f": sb1("S0f", [128, 512], F32),
                  "C0f": sb1("C0f", [128, 512], F32),
                  "Gnf": sb1("Gnf", [128, 512], F32),
                  "S0b": sb1("S0b", [128, 512], BF16),
                  "C0b": sb1("C0b", [128, 512], BF16),
                  "Cnb": sb1("Cnb", [128, 512], BF16),
                  "Gnb": sb1("Gnb", [128, 512], BF16)}
            OTs = sb1("OTs", [128, FC, TC], BF16)
            comp = emit_compose(nc, esB, sch, cm, mb_out.ap(), gates_d, st, cc)
            s2_done = emit_sweep2(nc, esB, sch, cm, (q_s, k_s, v_s, sK_s, sC_s),
                                  st, OTs, comp)
            Wo = sb1("Wo", [128, FC, DM], BF16)
            xrs = [qb["xa"], sb1("xr1", [128, DM], F32)]
            lbs = [{"uf": sb1("ln_uf", [128, DM], F32),
                    "sq": qb["tn"],
                    "st4": sb1("ln_st4", [128, 4], F32)},
                   {"uf": sb1("ln_uf1", [128, DM], F32),
                    "sq": qb["cqB"],
                    "st4": sb1("ln_st4b", [128, 4], F32)}]
            pLNs = [cm["ps_b"],
                    [cm["ps_c"], cm["ps_d"]]]

            def wol(gp, sem):
                gp.dma_start(out=Wo[:, :, :],
                             in_=Wo_d.rearrange("(a p) c -> p a c", p=128)
                             ).then_inc(sem, 16)
            wold = sch.dma("gpsimd", wol, 1, deps=[cc])
            s1h, s3h, xsoh = [None, None], [None, None], [None, None]
            xso_all = []
            last = s2_done
            for tc in range(NCH):
                par = tc % 2
                pLN = pLNs[par]

                def xrl(sy, sem, tc=tc, par=par):
                    sy.dma_start(out=xrs[par][:, :],
                                 in_=x_d[tc * 128:(tc + 1) * 128, :]
                                 ).then_inc(sem, 16)
                xl = sch.dma("sync", xrl, 1,
                             deps=([s1h[par]] if s1h[par] else [comp]))

                def mmf(t, tc=tc, pLN=pLN, par=par):
                    for n2 in range(2):
                        dst = (pLN[:, n2 * 512:(n2 + 1) * 512] if par == 0
                               else pLN[n2][:, :])
                        for fk in range(FC):
                            last_i = t.matmul(
                                dst,
                                OTs[:, fk, tc * 128:(tc + 1) * 128],
                                Wo[:, fk, n2 * 512:(n2 + 1) * 512],
                                start=(fk == 0), stop=(fk == FC - 1))
                    return last_i
                mm = sch.step("tensor", mmf, deps=[wold, s2_done] +
                              ([s1h[par]] if s1h[par] else []))
                ln, ls1, ls3 = emit_ln(
                    nc, sch, cm, lbs[par], pLN, xrs[par], lbs[par]["uf"],
                    [mm, xl] + ([xsoh[par]] if xsoh[par] else []),
                    sq_war=([s3h[par]] if s3h[par] else []))
                s1h[par], s3h[par] = ls1, ls3

                def xso(sy, sem, tc=tc, par=par):
                    sy.dma_start(out=x1_d[tc * 128:(tc + 1) * 128, :],
                                 in_=lbs[par]["uf"][:, :]).then_inc(sem, 16)
                last = sch.dma("sync", xso, 1, deps=[ln])
                xsoh[par] = last
                xso_all.append(last)
            wo_done = sch.step("gpsimd", lambda g: g.memset(
                cm["cpool"][:, 3:4], 0.0), deps=xso_all)

        esC = ExitStack()
        with esC:
            sb2 = lambda n, s, dt: esC.enter_context(nc.sbuf_tensor(n, s, dt))
            wqgT = sb2("wqgT", [128, FC, DI], BF16)
            wquT = sb2("wquT", [128, FC, DI], BF16)
            wqdT = sb2("wqdT", [128, DI // 128, DM], BF16)
            ptr = cm["ps_bf"]

            def wload_g(sy, sem):
                for sl in range(DI // 128):
                    core, k = (sl // 3, sl % 3) if sl < 21 else (7, 0)
                    sy.dma_start(
                        out=wqgT[:, :, sl * 128:(sl + 1) * 128],
                        in_=wqb_out.ap()[core, :, k * 1024:(k + 1) * 1024]
                        .rearrange("p (a b) -> p a b", b=128)
                    ).then_inc(sem, 16)
                for sl in range(DM // 256):
                    sy.dma_start(
                        out=wqdT[:, :, sl * 128:(sl + 1) * 128],
                        in_=wqb_out.ap()[sl, :, 6144:6144 + DI]
                        .rearrange("p (a b) -> p a b", b=128)
                    ).then_inc(sem, 16)

            def wload_u(gp, sem):
                for sl in range(DI // 128):
                    core, k = (sl // 3, sl % 3) if sl < 21 else (7, 0)
                    gp.dma_start(
                        out=wquT[:, :, sl * 128:(sl + 1) * 128],
                        in_=wqb_out.ap()[core, :, 3072 + k * 1024:
                                         3072 + (k + 1) * 1024]
                        .rearrange("p (a b) -> p a b", b=128)
                    ).then_inc(sem, 16)
                for sl in range(DM // 256, DM // 128):
                    gp.dma_start(
                        out=wqdT[:, :, sl * 128:(sl + 1) * 128],
                        in_=wqb_out.ap()[sl, :, 6144:6144 + DI]
                        .rearrange("p (a b) -> p a b", b=128)
                    ).then_inc(sem, 16)
            wq_done_g = sch.dma("sync", wload_g, 26, deps=[ccw_s, wo_done])
            wq_done_u = sch.dma("gpsimd", wload_u, 26, deps=[ccw_s, wo_done])
            wq_done2 = [wq_done_g, wq_done_u]

            x1b = [sb2("x1b0", [128, DM], F32), sb2("x1b1", [128, DM], F32)]
            x1q0 = sb2("x1q0", [128, DM], BF16)
            x1q = [x1q0, x1q0]
            xqT = [sb2("xqT0", [128, FC, 128], BF16),
                   sb2("xqT1", [128, FC, 128], BF16)]
            hg = [sb2("hg0", [128, 512], F32), sb2("hg1", [128, 512], F32)]
            hu = [sb2("hu0", [128, 512], F32), sb2("hu1", [128, 512], F32)]
            hh = [sb2("hh0", [128, 512], F32), sb2("hh1", [128, 512], F32)]
            hqc = [sb2("hqc0", [128, 512], BF16),
                   sb2("hqc1", [128, 512], BF16)]
            hqT0 = sb2("hqT0", [128, 4, 128], BF16)
            hqT = [hqT0, hqT0]
            qxh = [{"xa": sb2("xah0", [128, 512], F32),
                    "sgb": sb2("sgh0", [128, 512], BF16)},
                   {"xa": sb2("xah1", [128, 512], F32),
                    "sgb": sb2("sgh1", [128, 512], BF16)}]
            lb2 = {"uf": sb2("l2_uf", [128, DM], F32),
                   "sq": sb2("l2sq", [128, DM], BF16),
                   "st4": sb2("l2_st4", [128, 4], F32)}
            yc = lb2["uf"]
            pG2 = [cm["ps_c"], cm["ps_e"]]
            pU2 = [cm["ps_d"], cm["ps_f"]]
            pDN = cm["ps_b"]

            ICH = [(0, 512), (512, 512), (1024, 512), (1536, 512),
                   (2048, 512), (2560, 256)]
            hist = {"ge": [], "hf": [], "qv": [], "ht": [], "hte": [],
                    "dm": [], "ln2": [], "ln2s1": [], "ln2s3": [],
                    "qdq": [], "xt_all": [], "xte": [None, None],
                    "ydma": []}
            war = lambda k, n=2: ([hist[k][-n]] if len(hist[k]) >= n else [])

            def emit_x1(tc):
                tpar = tc % 2

                def x1l(sy, sem, tc=tc, tpar=tpar):
                    sy.dma_start(out=x1b[tpar][:, :],
                                 in_=x1_d[tc * 128:(tc + 1) * 128, :]
                                 ).then_inc(sem, 16)
                xl = sch.dma("sync", x1l, 1,
                             deps=[wo_done] + hist["ln2"][-1:] +
                             hist["qdq"][-2:][:1])
                qdeps = [xl] + wq_done2 + hist["qdq"][-1:]
                qdeps += hist["xt_all"][-1:]
                if hist["xte"][tpar]:
                    qdeps.append(hist["xte"][tpar])
                qsg = emit_quant_sg(nc, sch, cm, qb, x1b[tpar], DM, qdeps)
                qd = emit_quant_qv(nc, sch, cm, qb, qb, x1b[tpar], DM,
                                   x1q[tpar], qdeps + [qsg])
                hist["qdq"].append(qd)

                def xtr(t, tpar=tpar):
                    for fc in range(FC):
                        lastt = t.transpose(ptr[:, fc * 128:(fc + 1) * 128],
                                            x1q[tpar][:, fc * 128:(fc + 1) * 128],
                                            cm["idm"][:, :])
                    return lastt
                xt = sch.step("tensor", xtr,
                              deps=[qd, cm["ptrtok"][0]] + war("hte", 1))
                hist["xt_all"].append(xt)
                xte = sch.step("scalar", lambda s, tpar=tpar: s.copy(
                    out=xqT[tpar][:, :, :].rearrange("p a b -> p (a b)"),
                    in_=ptr[:, 0:1024]), deps=[xt])
                hist["xte"][tpar] = xte
                cm["ptrtok"][0] = xte
                return qd, xte

            cur = emit_x1(0)
            for tc in range(NCH):
                tpar = tc % 2
                qd, xte = cur

                dn_mms = []
                pend_sg = None
                pend_fdn = None

                def emit_qv(pq):
                    sg_p, ici_p, i0_p, ilen_p, par_p = pq
                    qv = emit_quant_qv(
                        nc, sch, cm, qb, qxh[par_p], hh[par_p], ilen_p,
                        hqc[par_p], [sg_p] + war("ht", 1))
                    hist["qv"].append(qv)
                    return (qv, ici_p, i0_p, ilen_p, par_p)

                def emit_f(pf):
                    qv_p, ici_p, i0_p, ilen_p, par_p = pf
                    emit_fdn(nc, sch, cm, ptr, hqc, hqT, wqdT, pDN,
                             dn_mms, hist, ICH, qv_p, ici_p, i0_p, ilen_p,
                             par_p, tc)

                for ici, (i0, ilen) in enumerate(ICH):
                    par = ici % 2
                    pg, pu = pG2[par], pU2[par]

                    def gmumf(t, i0=i0, ilen=ilen, pg=pg, pu=pu, tpar=tpar):
                        for fk in range(FC):
                            t.matmul(pg[:, 0:ilen], xqT[tpar][:, fk, :],
                                     wqgT[:, fk, i0:i0 + ilen],
                                     start=(fk == 0), stop=(fk == FC - 1))
                        for fk in range(FC):
                            last_i = t.matmul(pu[:, 0:ilen], xqT[tpar][:, fk, :],
                                              wquT[:, fk, i0:i0 + ilen],
                                              start=(fk == 0),
                                              stop=(fk == FC - 1))
                        return last_i
                    gm = sch.step("tensor", gmumf,
                                  deps=[xte] + wq_done2 + war("ge"))

                    def geuef(s, ilen=ilen, pg=pg, pu=pu, par=par):
                        s.activation(out=hg[par][:, 0:ilen], in_=pg[:, 0:ilen],
                                     func=AF.Silu)
                        return s.copy(out=hu[par][:, 0:ilen], in_=pu[:, 0:ilen])
                    ge = sch.step("scalar", geuef, deps=[gm] + war("hf"))
                    hist["ge"].append(ge)

                    def hff(v, ilen=ilen, par=par):
                        return v.tensor_tensor(out=hh[par][:, 0:ilen],
                                               in0=hg[par][:, 0:ilen],
                                               in1=hu[par][:, 0:ilen],
                                               op=ALU.mult)
                    hstep = sch.step("vector", hff,
                                     deps=[ge] + war("qv", 1))
                    hist["hf"].append(hstep)
                    sg = emit_quant_sg(nc, sch, cm, qxh[par], hh[par], ilen,
                                       [hstep] + war("qv", 1))
                    if pend_fdn is not None:
                        emit_f(pend_fdn)
                        pend_fdn = None
                    if pend_sg is not None:
                        pend_fdn = emit_qv(pend_sg)
                    pend_sg = (sg, ici, i0, ilen, par)
                if pend_fdn is not None:
                    emit_f(pend_fdn)
                emit_f(emit_qv(pend_sg))

                if tc + 1 < NCH:
                    cur = emit_x1(tc + 1)

                ln2, l2s1, l2s3 = emit_ln(
                    nc, sch, cm, lb2, pDN, x1b[tpar], yc,
                    [dn_mms[-1], qd] + hist["ydma"][-1:],
                    sq_war=hist["ln2s3"][-1:])
                hist["ln2"].append(ln2)
                hist["ln2s1"].append(l2s1)
                hist["ln2s3"].append(l2s3)

                def yso(gp, sem, tc=tc):
                    gp.dma_start(out=y_d[tc * 128:(tc + 1) * 128, :],
                                 in_=yc[:, :]).then_inc(sem, 16)
                ydma = sch.dma("gpsimd", yso, 1, deps=[ln2])
                hist["ydma"].append(ydma)

            sch.step("gpsimd", lambda g: g.memset(cm["cpool"][:, 3:4], 0.0),
                     deps=[hist["ydma"][-1]])
        sch.run(es)
    return nc


def kernel(**inputs):
    x = np.asarray(inputs["L0_attn_x"], np.float32).reshape(T, DM)
    w = {k: np.ascontiguousarray(np.asarray(v, np.float32))
         for k, v in inputs.items()}
    gates = np.zeros((NCORE, 128, 8), np.float32)
    for c in range(NCORE):
        gates[c, :, :c] = 1.0
    def myw(warr):
        outs = []
        for c in range(NCORE):
            buf = np.zeros((384, DM), np.float32)
            lo = min(3 * c, 21)
            n = min(3, 22 - lo) if c < 7 else 1
            buf[0:n * 128] = warr[lo * 128:(lo + n) * 128]
            outs.append(buf)
        return outs
    wg_list = myw(w["L0_ffn_w_gate"])
    wu_list = myw(w["L0_ffn_w_up"])

    nc = build_fused()
    ins = [{"x": np.ascontiguousarray(x[c * TC:(c + 1) * TC]),
            "W_q": w["L0_attn_W_q"], "W_k": w["L0_attn_W_k"],
            "W_v": w["L0_attn_W_v"], "Wg_K": w["L0_attn_Wg_K"],
            "Wg_C": w["L0_attn_Wg_C"],
            "bg_K": w["L0_attn_bg_K"].reshape(1, DM),
            "bg_C": w["L0_attn_bg_C"].reshape(1, DM),
            "gates": gates[c], "W_o": w["L0_attn_W_o"],
            "wg_my": wg_list[c], "wu_my": wu_list[c],
            "wdn_my": np.ascontiguousarray(
                w["L0_ffn_w_down"][c * 128:(c + 1) * 128])}
           for c in range(NCORE)]
    r = run_bass_kernel_spmd(nc, ins, core_ids=list(range(NCORE))).results
    y = np.concatenate([np.asarray(r[c]["y"]) for c in range(NCORE)], axis=0)
    return y.reshape(1, T, DM)


# version 19
# speedup vs baseline: 1.0044x; 1.0044x over previous
import numpy as np
from contextlib import ExitStack

import concourse.bass as bass
import concourse.mybir as mybir
from concourse.bass_utils import run_bass_kernel_spmd

F32 = mybir.dt.float32
BF16 = mybir.dt.bfloat16
FP16 = mybir.dt.float16
I32 = mybir.dt.int32
AF = mybir.ActivationFunctionType
ALU = mybir.AluOpType

T, NCORE = 16384, 8
TC = T // NCORE
L = 128
NCH = TC // L
H, DK, DV, DM, DI = 16, 64, 64, 1024, 2816
NP = H // 2
FC = DM // 128
EPS = 1e-5
MIDS = [(0.25, 0.5), (0.75, 0.5), (1.25, 0.5), (1.75, 0.5),
        (2.5, 1.0), (3.5, 1.0), (5.0, 2.0)]
CONSTS = [0.0, 1.0, EPS]


class Sched:
    ENGINES = ("sync", "gpsimd", "scalar", "vector", "tensor")

    def __init__(self, nc):
        self.nc = nc
        self.steps = []

    def step(self, engine, fn, deps=(), coll=False):
        self.steps.append((engine, fn, tuple(deps), 0, coll))
        return len(self.steps) - 1

    def dma(self, engine, fn, n, deps=()):
        self.steps.append((engine, fn, tuple(deps), n, False))
        return len(self.steps) - 1

    NDMA_SEMS = 5

    def run(self, es):
        nc = self.nc
        semnames = list(self.ENGINES)
        for e in ("sync", "gpsimd"):
            semnames += [e + "_dma%d" % r for r in range(self.NDMA_SEMS)]
        ncoll = sum(1 for s in self.steps if s[4])
        semnames += ["cc%d" % r for r in range(ncoll)]
        sems = {e: es.enter_context(nc.semaphore("sem_" + e)) for e in semnames}
        dma_ord = {e: 0 for e in ("sync", "gpsimd")}
        cc_ord = 0
        semname = [None] * len(self.steps)
        prior = [0] * len(self.steps)
        done = [0] * len(self.steps)
        count = {e: 0 for e in semnames}
        for i, (e, fn, deps, nd, coll) in enumerate(self.steps):
            if nd:
                sn = e + "_dma%d" % (dma_ord[e] % self.NDMA_SEMS)
                dma_ord[e] += 1
            elif coll:
                sn = "cc%d" % cc_ord
                cc_ord += 1
            else:
                sn = e
            semname[i] = sn
            prior[i] = count[sn]
            count[sn] += 16 * nd if nd else 1
            done[i] = count[sn]
        with nc.Block() as blk:
            for e in self.ENGINES:
                mine = [(i, fn, deps, nd)
                        for i, (ee, fn, deps, nd, _coll) in
                        enumerate(self.steps) if ee == e]
                if not mine:
                    continue

                def make(eng_name, mine):
                    def body(eng):
                        for i, fn, deps, nd in mine:
                            need_drain = False
                            for d in deps:
                                dsem = semname[d]
                                if dsem != eng_name or self.steps[d][3]:
                                    eng.wait_ge(sems[dsem], done[d])
                                else:
                                    need_drain = True
                            if need_drain and eng_name in ("scalar", "vector"):
                                eng.drain()
                            if nd:
                                if prior[i]:
                                    eng.wait_ge(sems[semname[i]], prior[i])
                                fn(eng, sems[semname[i]])
                            else:
                                fn(eng).then_inc(sems[semname[i]], 1)
                    return body
                getattr(blk, e)(make(e, mine))
        return sems, done


def alloc_common(nc, es, sch):
    cm = {}
    sb = lambda n, s, dt: es.enter_context(nc.sbuf_tensor(n, s, dt))
    cm["iot"] = sb("iot", [128, 128], I32)
    cm["idm"] = sb("idm", [128, 128], BF16)
    cm["m_incl"] = sb("m_incl", [128, 128], BF16)
    cm["m_str"] = sb("m_str", [128, 128], BF16)
    cm["m_strn"] = sb("m_strn", [128, 128], BF16)
    cm["zero128"] = sb("zero128", [128, 128], F32)
    cm["ones64"] = sb("ones64", [128, 64], BF16)
    cm["cpool"] = sb("cpool", [128, 4], F32)
    cm["ps_bf"] = es.enter_context(nc.psum_tensor("ps_bf", [128, 1024], BF16))
    cm["ps_a"] = es.enter_context(nc.psum_tensor("ps_a", [128, 512], F32))
    cm["ps_b"] = es.enter_context(nc.psum_tensor("ps_b", [128, 1024], F32))
    cm["ps_c"] = es.enter_context(nc.psum_tensor("ps_c", [128, 512], F32))
    cm["ps_d"] = es.enter_context(nc.psum_tensor("ps_d", [128, 512], F32))
    cm["ps_e"] = es.enter_context(nc.psum_tensor("ps_e", [128, 512], F32))
    cm["ps_f"] = es.enter_context(nc.psum_tensor("ps_f", [128, 512], F32))
    for i, cv in enumerate(CONSTS):
        nc.const_aps.register(F32, cv, cm["cpool"][:, i:i + 1])

    def c_gp(g):
        for i, cv in enumerate(CONSTS):
            g.memset(cm["cpool"][:, i:i + 1], cv)
        g.memset(cm["zero128"][:, :], 0.0)
        g.memset(cm["ones64"][:, :], 1.0)
        return g.iota(cm["iot"][:, :], [[1, 128]], channel_multiplier=-1)
    s0 = sch.step("gpsimd", c_gp)

    def c_v(v):
        v.tensor_scalar(out=cm["idm"][:, :], in0=cm["iot"][:, :], scalar1=0,
                        scalar2=None, op0=ALU.is_equal)
        v.tensor_scalar(out=cm["m_incl"][:, :], in0=cm["iot"][:, :], scalar1=0,
                        scalar2=None, op0=ALU.is_ge)
        v.tensor_scalar(out=cm["m_str"][:, :], in0=cm["iot"][:, :], scalar1=0,
                        scalar2=None, op0=ALU.is_gt)
        return v.tensor_scalar(out=cm["m_strn"][:, :], in0=cm["iot"][:, :],
                               scalar1=0, scalar2=-1.0, op0=ALU.is_gt,
                               op1=ALU.mult)
    cm["ready"] = sch.step("vector", c_v, deps=[s0])
    cm["ptrtok"] = [cm["ready"]]
    cm["tmpa"] = sb("utmp_a", [128, 512], F32)
    cm["tmpb"] = sb("utmp_b", [128, 512], F32)
    return cm


def emit_xT(nc, es, sch, cm, x_d, xT):
    xg = es.enter_context(nc.sbuf_tensor("xg", [128, 4, DM], BF16))
    ptr = cm["ps_bf"]
    last_evac = cm["ready"]
    for g in range(TC // 512):
        def ldf(gp, sem, g=g):
            gp.dma_start(
                out=xg[:, :, :],
                in_=x_d[g * 512:(g + 1) * 512, :]
                .rearrange("(a p) f -> p a f", p=128)).then_inc(sem, 16)
        ld = sch.dma("gpsimd", ldf, 1, deps=[last_evac])
        for tc4 in range(4):
            def tf(t, tc4=tc4):
                for fc in range(FC):
                    last = t.transpose(ptr[:, fc * 128:(fc + 1) * 128],
                                       xg[:, tc4, fc * 128:(fc + 1) * 128],
                                       cm["idm"][:, :])
                return last
            ts = sch.step("tensor", tf,
                          deps=[ld, cm["ready"], cm["ptrtok"][0]])

            def vf(v, g=g, tc4=tc4):
                t0 = g * 512 + tc4 * 128
                return v.tensor_copy(
                    out=xT[:, :, t0:t0 + 128],
                    in_=ptr[:, :].rearrange("p (a b) -> p a b", b=128))
            last_evac = sch.step("vector", vf, deps=[ts])
            cm["ptrtok"][0] = last_evac
    return last_evac


def emit_projections(nc, es, sch, cm, xT, xT_done, wd, outs, which=("q", "k", "gK", "gC", "v")):
    Wbufs = [es.enter_context(nc.sbuf_tensor("Wp0", [128, FC, DM], BF16)),
             es.enter_context(nc.sbuf_tensor("Wp1", [128, FC, DM], BF16))]
    stgs = [es.enter_context(nc.sbuf_tensor("stgp0", [128, TC], BF16)),
            es.enter_context(nc.sbuf_tensor("stgp1", [128, TC], BF16))]
    stg16 = es.enter_context(nc.sbuf_tensor("stg16", [128, TC], FP16))
    spsA = [es.enter_context(nc.sbuf_tensor("spsp0", [128, 512], F32)),
            es.enter_context(nc.sbuf_tensor("spsp1", [128, 512], F32))]
    spfA = [es.enter_context(nc.sbuf_tensor("spfp0", [128, 512], F32)),
            es.enter_context(nc.sbuf_tensor("spfp1", [128, 512], F32))]
    pps = [cm["ps_a"], cm["ps_c"], cm["ps_d"], cm["ps_e"]]

    bgn8 = es.enter_context(nc.sbuf_tensor("bgn8", [128, 16], F32))

    def bg_ld(sy, sem):
        with nc.allow_non_contiguous_dma(reason="tiny bias gather"):
            sy.dma_start(out=bgn8[:, 0:8],
                         in_=wd["bg_K"].rearrange("o (a p) -> p (o a)", p=128)
                         ).then_inc(sem, 16)
            sy.dma_start(out=bgn8[:, 8:16],
                         in_=wd["bg_C"].rearrange("o (a p) -> p (o a)", p=128)
                         ).then_inc(sem, 16)
    bgl = sch.dma("sync", bg_ld, 2)
    bgneg = sch.step("vector", lambda v: v.tensor_scalar_mul(
        out=bgn8[:, :], in0=bgn8[:, :], scalar1=-1.0), deps=[bgl])

    NBANK = len(pps)
    evhist = [None] * NBANK
    sphist = [None, None]
    prev_dma = {}
    lastmm = []
    gi = [0, 0]
    finals = []

    def emit_one(mi, Wname, out_dram, kind, bg_off=None):
        Wp = Wbufs[mi % 2]

        def wldf(gp, sem, Wname=Wname, Wp=Wp):
            gp.dma_start(out=Wp[:, :, :],
                         in_=wd[Wname].rearrange("(a p) c -> p a c", p=128)
                         ).then_inc(sem, 16)
        wld = sch.dma("gpsimd", wldf, 1,
                      deps=([lastmm[mi - 2]] if mi >= 2 else []))
        fin = []
        mm = None
        if kind in ("fm", "gate"):
            for oc in range(FC):
                sbuf = ("stg%d" % (oc % 2)) if kind == "fm" else "stg16"
                stg = stgs[oc % 2]
                mm_steps = []
                for tch in range(4):
                    par = gi[0] % NBANK
                    pp = pps[par]
                    gi[0] += 1
                    def mf(t, oc=oc, tch=tch, pp=pp, Wp=Wp):
                        for fk in range(FC):
                            last_i = t.matmul(
                                pp[:, :], Wp[:, fk, oc * 128:(oc + 1) * 128],
                                xT[:, fk, tch * 512:(tch + 1) * 512],
                                start=(fk == 0), stop=(fk == FC - 1))
                        return last_i
                    deps = [wld, xT_done]
                    if evhist[par] is not None:
                        deps.append(evhist[par])
                    mm = sch.step("tensor", mf, deps=deps)
                    if kind == "fm":
                        def ef(s, tch=tch, pp=pp, stg=stg):
                            return s.copy(out=stg[:, tch * 512:(tch + 1) * 512],
                                          in_=pp[:, :])
                        ev = sch.step("scalar", ef, deps=[mm] +
                                      ([prev_dma[sbuf]] if prev_dma.get(sbuf)
                                       else []))
                        evhist[par] = ev
                    else:
                        sp = gi[1] % 2
                        gi[1] += 1

                        def e1(s, bg_off=bg_off, oc=oc, pp=pp, sp=sp):
                            s.drain()
                            s.activation(out=spsA[sp][:, :], in_=pp[:, :],
                                         func=AF.Exp, scale=-1.0,
                                         bias=bgn8[:, bg_off + oc:bg_off + oc + 1])
                            s.drain()
                            return s.activation(out=spfA[sp][:, :],
                                                in_=spsA[sp][:, :],
                                                func=AF.Ln, bias=1.0)
                        sv = sch.step("scalar", e1, deps=[mm, bgneg] +
                                      ([sphist[sp]] if sphist[sp] else []))

                        def e2(v, tch=tch, sp=sp):
                            for c4 in range(4):
                                sl = slice(c4 * 128, (c4 + 1) * 128)
                                v.tensor_tensor_scan(
                                    out=spsA[sp][:, sl], data0=spfA[sp][:, sl],
                                    data1=cm["zero128"][:, :], initial=0.0,
                                    op0=ALU.add, op1=ALU.add)
                            v.drain()
                            return v.tensor_copy(
                                out=stg16[:, tch * 512:(tch + 1) * 512],
                                in_=spsA[sp][:, :])
                        ev = sch.step("vector", e2, deps=[sv] +
                                      ([prev_dma[sbuf]] if prev_dma.get(sbuf)
                                       else []))
                        evhist[par] = ev
                        sphist[sp] = ev
                    mm_steps.append((mm, ev))
                evs = [ev for _, ev in mm_steps]

                def df(sy, sem, oc=oc, kind=kind, stg=stg):
                    src = stg if kind == "fm" else stg16
                    for c in range(NCH):
                        sy.dma_start(
                            out=out_dram[c, :, oc * L:(oc + 1) * L],
                            in_=src[:, c * L:(c + 1) * L]).then_inc(sem, 16)
                dmas = sch.dma("sync" if oc % 2 == 0 else "gpsimd",
                               df, NCH, deps=evs)
                prev_dma[sbuf] = dmas
                fin.append(dmas)
        else:
            for tch in range(NCH):
                sbuf = "stg%d" % (tch % 2)
                stg = stgs[tch % 2]
                prev_ev = None
                for n2 in range(2):
                    par = gi[0] % NBANK
                    pp = pps[par]
                    gi[0] += 1
                    def mf2(t, tch=tch, n2=n2, pp=pp, Wp=Wp):
                        for fk in range(FC):
                            last_i = t.matmul(
                                pp[:, :],
                                xT[:, fk, tch * 128:(tch + 1) * 128],
                                Wp[:, fk, n2 * 512:(n2 + 1) * 512],
                                start=(fk == 0), stop=(fk == FC - 1))
                        return last_i
                    deps = [wld, xT_done]
                    if evhist[par] is not None:
                        deps.append(evhist[par])
                    mm = sch.step("tensor", mf2, deps=deps)

                    def ef(s, n2=n2, pp=pp, stg=stg):
                        return s.copy(out=stg[:, n2 * 512:(n2 + 1) * 512],
                                      in_=pp[:, :])
                    prev_ev = sch.step("scalar", ef, deps=[mm] +
                                       ([prev_dma[sbuf]] if prev_dma.get(sbuf)
                                        else []))
                    evhist[par] = prev_ev

                def df(sy, sem, tch=tch, stg=stg):
                    sy.dma_start(out=out_dram[tch * 128:(tch + 1) * 128, :],
                                 in_=stg[:, 0:DM]).then_inc(sem, 16)
                prev_dma[sbuf] = sch.dma("sync" if tch % 2 == 0 else "gpsimd",
                                         df, 1, deps=[prev_ev])
                fin.append(prev_dma[sbuf])
        lastmm.append(mm)
        finals.append(fin[-1])

    mats = [("q", "W_q", outs["q_s"], "fm", None),
            ("k", "W_k", outs["k_s"], "fm", None),
            ("gK", "Wg_K", outs["spK_s"], "gate", 0),
            ("gC", "Wg_C", outs["spC_s"], "gate", 8),
            ("v", "W_v", outs["v_s"], "tm", None)]
    mats = [m for m in mats if m[0] in which]
    for mi, (_, wn, od, kd, bo) in enumerate(mats):
        emit_one(mi, wn, od, kd, bg_off=bo)
    return finals


class ChunkBufs:
    def __init__(self, nc, es, tag, with_sweep2, share_prep=None):
        sb = lambda n, s, dt: es.enter_context(nc.sbuf_tensor(n + tag, s, dt))
        if share_prep is not None:
            for f in ("qf", "kf", "sK", "sC", "Pinv", "Rg", "Rinv"):
                setattr(self, f, getattr(share_prep, f))
        else:
            self.qf = sb("qf", [128, FC, L], BF16)
            self.kf = sb("kf", [128, FC, L], BF16)
            self.sK = sb("sK", [128, FC, L], FP16)
            self.sC = sb("sC", [128, FC, L], FP16)
            self.Pinv = sb("Pinv", [128, DM], BF16)
            self.Rg = sb("Rg", [128, DM], BF16)
            self.Rinv = sb("Rinv", [128, DM], BF16)
        self.vt = sb("vt", [128, DM], BF16)
        self.Ut = sb("Ut", [128, DM], BF16)
        self.Qh = sb("Qh", [128, DM], BF16)
        self.Kb = sb("Kb", [128, DM], BF16)
        self.Utm = sb("Utm", [128, DM], BF16)
        self.Qhtm = sb("Qhtm", [128, DM], BF16)
        self.plK = sb("plK", [128, 8], F32)
        self.rlE = sb("rlE", [128, 8], F32)
        self.plbc_r = sb("plbc_r", [128, 512], BF16)
        self.plbc = sb("plbc", [128, 512], F32)
        if with_sweep2:
            self.P = sb("P", [128, DM], BF16)
            self.PRg = sb("PRg", [128, DM], BF16)
            self.Qb = sb("Qb", [128, DM], BF16)
            self.prg16 = sb("prg16", [128, DM], FP16)
        else:
            self.Kbtm = sb("Kbtm", [128, DM], BF16)


def emit_chunk_prep(nc, sch, cm, cb, srcs, c, sweep2, extra_dep=None):
    q_s, k_s, v_s, spK_s, spC_s = srcs

    def ldf(sy, sem, c=c):
        sy.dma_start(out=cb.qf[:, :, :].rearrange("p a b -> p (a b)"),
                     in_=q_s[c, :, :]).then_inc(sem, 16)
        sy.dma_start(out=cb.kf[:, :, :].rearrange("p a b -> p (a b)"),
                     in_=k_s[c, :, :]).then_inc(sem, 16)
        sy.dma_start(out=cb.vt[:, :],
                     in_=v_s[c * L:(c + 1) * L, :]).then_inc(sem, 16)
        sy.dma_start(out=cb.sK[:, :, :].rearrange("p a b -> p (a b)"),
                     in_=spK_s[c, :, :]).then_inc(sem, 16)
        sy.dma_start(out=cb.sC[:, :, :].rearrange("p a b -> p (a b)"),
                     in_=spC_s[c, :, :]).then_inc(sem, 16)
    deps = [] if extra_dep is None else list(extra_dep)
    ld = sch.dma("sync", ldf, 5, deps=deps)

    fl = lambda b: b[:, :, :].rearrange("p a b -> p (a b)")

    def expf(s):
        s.activation(out=cb.Pinv[:, :], in_=fl(cb.sK), func=AF.Exp)
        s.activation(out=cb.Rg[:, :], in_=fl(cb.sC), func=AF.Exp, scale=-1.0)
        s.activation(out=cb.Rinv[:, :], in_=fl(cb.sC), func=AF.Exp)
        s.activation(out=cb.plK[:, :], in_=cb.sK[:, :, L - 1:L].squeeze(2),
                     func=AF.Exp, scale=-1.0)
        last = s.activation(out=cb.rlE[:, :], in_=cb.sC[:, :, L - 1:L].squeeze(2),
                            func=AF.Exp, scale=-1.0)
        if sweep2:
            s.activation(out=cb.P[:, :], in_=fl(cb.sK), func=AF.Exp, scale=-1.0)
            last = s.activation(out=cb.PRg[:, :], in_=cb.prg16[:, :],
                                func=AF.Exp, scale=-1.0)
        return last
    if sweep2:
        def addsp(v):
            return v.tensor_add(out=cb.prg16[:, :], in0=fl(cb.sK), in1=fl(cb.sC))
        sp_add = sch.step("vector", addsp, deps=[ld])
        exps = sch.step("scalar", expf, deps=[ld, sp_add])
    else:
        exps = sch.step("scalar", expf, deps=[ld])

    def gatef(v):
        v.tensor_mul(out=cb.Ut[:, :], in0=fl(cb.kf), in1=cb.Pinv[:, :])
        v.tensor_mul(out=cb.Qh[:, :], in0=fl(cb.qf), in1=cb.Rinv[:, :])
        v.tensor_mul(out=cb.Kb[:, :], in0=fl(cb.kf), in1=cb.Rg[:, :])
        if sweep2:
            v.tensor_mul(out=cb.Qb[:, :], in0=fl(cb.qf), in1=cb.P[:, :])
        v.tensor_tensor(
            out=cb.plbc_r[0:64, :].rearrange("p (j k) -> p j k", k=64),
            in0=cb.plK[0:64, :].unsqueeze(2).broadcast_to([64, 8, 64]),
            in1=cm["idm"][0:64, 0:64].unsqueeze(1).broadcast_to([64, 8, 64]),
            op=ALU.mult)
        return v.tensor_tensor(
            out=cb.plbc_r[64:128, :].rearrange("p (j k) -> p j k", k=64),
            in0=cb.plK[64:128, :].unsqueeze(2).broadcast_to([64, 8, 64]),
            in1=cm["idm"][64:128, 64:128].unsqueeze(1).broadcast_to([64, 8, 64]),
            op=ALU.mult)
    gates = sch.step("vector", gatef, deps=[exps, cm["ready"]])
    return {"ld": ld, "exps": exps, "gates": gates}


MODULE_DOC_END = True


def rowbc(col8):
    return col8[:, :].unsqueeze(2).broadcast_to([128, 8, 64])


def emit_plbcast(nc, sch, cm, cb, pB, dep_gates):
    def mmf(t):
        t.matmul(pB[0:64, :], cm["ones64"][0:64, :], cb.plbc_r[0:64, :],
                 start=True, stop=True)
        return t.matmul(pB[64:128, :], cm["ones64"][64:128, :],
                        cb.plbc_r[64:128, :], start=True, stop=True)
    mm = sch.step("tensor", mmf, deps=dep_gates)
    ev = sch.step("vector", lambda v: v.tensor_copy(out=cb.plbc[:, :],
                                                    in_=pB[:, :]), deps=[mm])
    return ev


def emit_sweep1(nc, es, sch, cm, srcs, maps_o, start_dep, lvl=9, nch=NCH, co=None):
    sb = lambda n, s, dt: es.enter_context(nc.sbuf_tensor(n, s, dt))
    cbs = [ChunkBufs(nc, es, "_s1a", with_sweep2=False)]
    cbs.append(ChunkBufs(nc, es, "_s1b", with_sweep2=False,
                         share_prep=cbs[0]))
    LSm = sb("LSm", [128, 512], F32)
    LCm = sb("LCm", [128, 512], F32)
    Xtm = sb("Xtm", [128, 512], F32)
    LGm = sb("LGm", [128, 512], F32)
    Dm = sb("Dm", [128, 8], F32)
    Em = sb("Em", [128, 8], F32)
    LC_b = sb("LC_b", [128, 512], BF16)
    Yt_b = sb("Yt_b", [128, 512], BF16)
    Wl_b = sb("Wl_b", [128, DM], BF16)
    s3ms = [sb("s3m_s1a", [128, 256], BF16), sb("s3m_s1b", [128, 256], BF16)]

    ptr = cm["ps_bf"]
    ps3es = [cm["ps_a"][:, 0:128], cm["ps_a"][:, 128:256]]
    ps3os = [cm["ps_b"][:, 512:640], cm["ps_b"][:, 640:768]]
    pWs = [cm["ps_b"][:, 0:128], cm["ps_b"][:, 128:256]]
    pUU = cm["ps_c"]
    pQV = cm["ps_d"]
    pG = cm["ps_e"]
    pB = cm["ps_f"]

    def init(g):
        for b in (LSm, LCm, Xtm, LGm):
            g.memset(b[:, :], 0.0)
        g.memset(LC_b[:, :], 0.0)
        g.memset(Dm[:, :], 1.0)
        return g.memset(Em[:, :], 1.0)
    upd = sch.step("gpsimd", init,
                   deps=(start_dep if isinstance(start_dep, list)
                         else [start_dep]))

    upd_hist = [upd, upd]
    ev3_h = [None, None]
    wl_h = [None, None]
    ew_h = [None, None]

    last_gates = None
    for c in range(nch if lvl >= 1 else 0):
        cb = cbs[c % 2]
        pr = emit_chunk_prep(nc, sch, cm, cb, srcs, c, sweep2=False,
                             extra_dep=[upd_hist[c % 2]] +
                             ([last_gates] if last_gates else []))
        last_gates = pr["gates"]
        if lvl < 2:
            upd = pr["gates"]
            upd_hist[c % 2] = upd
            continue
        tms = {}
        for name, src, dst in (("Ut", cb.Ut, cb.Utm), ("Qh", cb.Qh, cb.Qhtm),
                               ("Kb", cb.Kb, cb.Kbtm)):
            def tf(t, src=src):
                for j in range(NP):
                    last = t.transpose(ptr[:, j * 128:(j + 1) * 128],
                                       src[:, j * 128:(j + 1) * 128],
                                       cm["idm"][:, :])
                return last
            ts = sch.step("tensor", tf, deps=[pr["gates"], cm["ptrtok"][0]])

            def vf(s, dst=dst):
                return s.copy(out=dst[:, :], in_=ptr[:, :])
            prev_ev = sch.step("scalar", vf, deps=[ts])
            cm["ptrtok"][0] = prev_ev
            tms[name] = prev_ev
        plb = emit_plbcast(nc, sch, cm, cb, pB, [pr["gates"], upd])
        if lvl < 3:
            upd = plb
            upd_hist[c % 2] = upd
            continue

        wl_steps, yt_evs, uu_steps, qv_steps, gg_steps = [], [], [], [], []
        prev_yt = plb
        for j in range(NP):
            he, ho = 2 * j, 2 * j + 1
            pj = slice(j * 128, (j + 1) * 128)
            jp = j % 2
            ps3e, ps3o, pW, s3m = ps3es[jp], ps3os[jp], pWs[jp], s3ms[jp]

            def s3f(t, j=j, pj=pj, cb=cb, ps3e=ps3e, ps3o=ps3o):
                t.matmul(ps3e[:, :], cb.Qh[0:64, pj], cb.Kb[0:64, pj],
                         start=True, stop=True)
                return t.matmul(ps3o[:, :], cb.Qh[64:128, pj],
                                cb.Kb[64:128, pj], start=True, stop=True)
            deps = [pr["gates"]] + ([ev3_h[jp]] if ev3_h[jp] else [])
            s3 = sch.step("tensor", s3f, deps=deps)
            del deps

            def ev3f(v, ps3e=ps3e, ps3o=ps3o, s3m=s3m):
                v.tensor_tensor(out=s3m[:, 0:128], in0=ps3e[:, :],
                                in1=cm["m_str"][:, :], op=ALU.mult)
                return v.tensor_tensor(out=s3m[:, 128:256], in0=ps3o[:, :],
                                       in1=cm["m_str"][:, :], op=ALU.mult)
            ev3_deps = [s3] + ([wl_h[jp]] if wl_h[jp] else [])
            ev3 = sch.step("vector", ev3f, deps=ev3_deps)
            ev3_h[jp] = ev3
            if lvl < 4:
                continue

            def wlf(t, j=j, pj=pj, cb=cb, pW=pW, s3m=s3m):
                for hh, hb in ((2 * j, 0), (2 * j + 1, 64)):
                    t.matmul(pW[:, (hh % 2) * 64:(hh % 2) * 64 + 64],
                             cb.Kb[hb:hb + 64, pj],
                             LC_b[hb:hb + 64, j * 64:(j + 1) * 64],
                             start=True, stop=False)
                    last = t.matmul(pW[:, (hh % 2) * 64:(hh % 2) * 64 + 64],
                                    s3m[:, (hb // 64) * 128:(hb // 64) * 128 + 128],
                                    cb.vt[:, hh * 64:(hh + 1) * 64],
                                    start=False, stop=True)
                return last
            wl = sch.step("tensor", wlf,
                          deps=[ev3, upd] + ([ew_h[jp]] if ew_h[jp] else []))
            wl_steps.append(wl)
            wl_h[jp] = wl

            def ewf(s, j=j, pW=pW):
                return s.copy(out=Wl_b[:, j * 128:(j + 1) * 128],
                              in_=pW[:, :])
            ew = sch.step("scalar", ewf, deps=[wl])
            ew_h[jp] = ew

            def ytf(t, j=j, cb=cb):
                for hh, hb in ((2 * j, 0), (2 * j + 1, 64)):
                    last = t.matmul(pB[hb:hb + 64, j * 64:(j + 1) * 64],
                                    cb.Kbtm[:, hh * 64:(hh + 1) * 64],
                                    cb.Utm[:, hh * 64:(hh + 1) * 64],
                                    start=True, stop=True)
                return last
            yt = sch.step("tensor", ytf, deps=[tms["Ut"], tms["Kb"], prev_yt])

            def eyf(v, j=j):
                return v.tensor_copy(out=Yt_b[:, j * 64:(j + 1) * 64],
                                     in_=pB[:, j * 64:(j + 1) * 64])
            ey = sch.step("vector", eyf, deps=[yt])
            yt_evs.append(ey)
            prev_yt = ey

            def uuf(t, j=j, cb=cb):
                for hh, hb in ((2 * j, 0), (2 * j + 1, 64)):
                    last = t.matmul(pUU[hb:hb + 64, j * 64:(j + 1) * 64],
                                    cb.Utm[:, hh * 64:(hh + 1) * 64],
                                    cb.Utm[:, hh * 64:(hh + 1) * 64],
                                    start=True, stop=True)
                return last
            uu_steps.append(sch.step("tensor", uuf, deps=[tms["Ut"], upd]))

            def qvf(t, j=j, cb=cb):
                for hh, hb in ((2 * j, 0), (2 * j + 1, 64)):
                    last = t.matmul(pQV[hb:hb + 64, j * 64:(j + 1) * 64],
                                    cb.Qhtm[:, hh * 64:(hh + 1) * 64],
                                    cb.vt[:, hh * 64:(hh + 1) * 64],
                                    start=True, stop=True)
                return last
            qv_steps.append(sch.step("tensor", qvf, deps=[tms["Qh"], upd]))

            def ggf(t, j=j, cb=cb):
                for hh, hb in ((2 * j, 0), (2 * j + 1, 64)):
                    last = t.matmul(pG[hb:hb + 64, j * 64:(j + 1) * 64],
                                    cb.Utm[:, hh * 64:(hh + 1) * 64],
                                    Wl_b[:, hh * 64:(hh + 1) * 64],
                                    start=True, stop=True)
                return last
            gg_steps.append(sch.step("tensor", ggf, deps=[ew, tms["Ut"], upd]))

        if lvl < 4:
            upd = ev3_h[(NP - 1) % 2]
            upd_hist[c % 2] = upd
            continue
        if lvl < 5:
            upd = sch.step("vector",
                           lambda v: v.tensor_copy(out=cm["tmpa"][:, :],
                                                   in_=pUU[:, :]),
                           deps=uu_steps + qv_steps + gg_steps + yt_evs)
            upd_hist[c % 2] = upd
            continue
        def updf(v, cb=cb):
            r8 = lambda x: x[:, :].rearrange("p (a b) -> p a b", b=64)
            v.tensor_tensor(out=cm["tmpa"][:, :], in0=pG[:, :], in1=LGm[:, :],
                            op=ALU.add)
            v.drain()
            v.tensor_tensor(out=r8(LGm), in0=r8(cm["tmpa"]), in1=rowbc(cb.plK),
                            op=ALU.mult)
            v.drain()
            v.tensor_tensor(out=r8(cm["tmpa"]), in0=r8(Yt_b), in1=rowbc(Em),
                            op=ALU.mult)
            v.drain()
            v.tensor_tensor(out=cm["tmpb"][:, :], in0=Xtm[:, :], in1=cm["tmpa"][:, :],
                            op=ALU.add)
            v.drain()
            v.tensor_tensor(out=Xtm[:, :], in0=cm["tmpb"][:, :], in1=cb.plbc[:, :],
                            op=ALU.mult)
            v.drain()
            v.tensor_tensor(out=cm["tmpa"][:, :], in0=pUU[:, :], in1=LSm[:, :],
                            op=ALU.add)
            v.drain()
            v.tensor_tensor(out=r8(cm["tmpb"]), in0=r8(cm["tmpa"]), in1=rowbc(cb.plK),
                            op=ALU.mult)
            v.drain()
            v.tensor_tensor(out=LSm[:, :], in0=cm["tmpb"][:, :], in1=cb.plbc[:, :],
                            op=ALU.mult)
            v.drain()
            v.tensor_tensor(out=cm["tmpa"][:, :], in0=pQV[:, :], in1=LCm[:, :],
                            op=ALU.add)
            v.drain()
            v.tensor_tensor(out=r8(LCm), in0=r8(cm["tmpa"]), in1=rowbc(cb.rlE),
                            op=ALU.mult)
            v.tensor_mul(out=Dm[:, :], in0=Dm[:, :], in1=cb.plK[:, :])
            v.tensor_mul(out=Em[:, :], in0=Em[:, :], in1=cb.rlE[:, :])
            v.drain()
            return v.tensor_copy(out=LC_b[:, :], in_=LCm[:, :])
        upd = sch.step("vector", updf,
                       deps=gg_steps + uu_steps + qv_steps + yt_evs + [plb])
        upd_hist[c % 2] = upd
        if co:
            co.pop(0)()

    while co:
        co.pop(0)()

    def mo(sy, sem):
        sy.dma_start(out=maps_o[:, 0:512], in_=LSm[:, :]).then_inc(sem, 16)
        sy.dma_start(out=maps_o[:, 512:1024], in_=LCm[:, :]).then_inc(sem, 16)
        sy.dma_start(out=maps_o[:, 1024:1536], in_=Xtm[:, :]).then_inc(sem, 16)
        sy.dma_start(out=maps_o[:, 1536:2048], in_=LGm[:, :]).then_inc(sem, 16)
        sy.dma_start(out=maps_o[:, 2048:2056], in_=Dm[:, :]).then_inc(sem, 16)
        sy.dma_start(out=maps_o[:, 2056:2064], in_=Em[:, :]).then_inc(sem, 16)
    fin = sch.dma("sync", mo, 6, deps=[upd])
    sch.step("gpsimd", lambda g: g.engine_nop() if hasattr(g, "engine_nop")
             else g.memset(cm["cpool"][:, 3:4], 0.0), deps=[fin])
    return fin


def build_phase1(stages=("xt", "proj", "sweep1"), s1_lvl=9, s1_nch=NCH):
    nc = bass.Bass("TRN2", target_bir_lowering=False, debug=False)
    sch = Sched(nc)
    x_d = nc.dram_tensor("x", [TC, DM], F32, kind="ExternalInput").ap()
    wd = {n: nc.dram_tensor(n, [DM, DM], F32, kind="ExternalInput").ap()
          for n in ("W_q", "W_k", "W_v", "Wg_K", "Wg_C")}
    wd["bg_K"] = nc.dram_tensor("bg_K", [1, DM], F32, kind="ExternalInput").ap()
    wd["bg_C"] = nc.dram_tensor("bg_C", [1, DM], F32, kind="ExternalInput").ap()

    q_o = nc.dram_tensor("q_s", [NCH, 128, FC * L], BF16,
                         kind="ExternalOutput").ap()
    k_o = nc.dram_tensor("k_s", [NCH, 128, FC * L], BF16,
                         kind="ExternalOutput").ap()
    v_o = nc.dram_tensor("v_s", [TC, DM], BF16, kind="ExternalOutput").ap()
    sK_o = nc.dram_tensor("spK_s", [NCH, 128, FC * L], FP16,
                          kind="ExternalOutput").ap()
    sC_o = nc.dram_tensor("spC_s", [NCH, 128, FC * L], FP16,
                          kind="ExternalOutput").ap()
    maps_o = nc.dram_tensor("maps", [128, 2064], F32, kind="ExternalOutput").ap()

    xt_dbg = nc.dram_tensor("xt_dbg", [128, FC, TC], BF16,
                            kind="ExternalOutput").ap()
    es = ExitStack()
    with es:
        cm = alloc_common(nc, es, sch)
        xT = es.enter_context(nc.sbuf_tensor("xT", [128, FC, TC], BF16))
        xt_done = emit_xT(nc, es, sch, cm, x_d, xT)
        last = xt_done
        if "proj" in stages:
            outs = {"q_s": q_o, "k_s": k_o, "spK_s": sK_o, "spC_s": sC_o,
                    "v_s": v_o}
            which = [st for st in ("q", "k", "gK", "gC", "v")
                     if "no_" + st not in stages]
            last = emit_projections(nc, es, sch, cm, xT, xt_done, wd, outs,
                                    which)
        if "sweep1" in stages:
            last = emit_sweep1(nc, es, sch, cm, (q_o, k_o, v_o, sK_o, sC_o),
                               maps_o, last, lvl=s1_lvl, nch=s1_nch)

        def xdbg(sy, sem):
            sy.dma_start(out=xt_dbg[:, :, :], in_=xT[:, :, :]).then_inc(sem, 16)
        fin = sch.dma("sync", xdbg, 1,
                      deps=(last if isinstance(last, list) else [last]) +
                      [xt_done])
        sch.step("gpsimd", lambda g: g.memset(cm["cpool"][:, 3:4], 0.0),
                 deps=[fin])
        sch.run(es)
    return nc


def emit_compose(nc, es, sch, cm, maps_d, gates_d, st, start_dep):
    sb = lambda n, s, dt: es.enter_context(nc.sbuf_tensor(n, s, dt))
    mapb = [sb("mapb0", [128, 2064], F32), sb("mapb1", [128, 2064], F32)]
    gt = sb("gt", [128, 8], F32)
    Dk = [sb("Dk0", [128, 8], F32), sb("Dk1", [128, 8], F32)]
    Ek = [sb("Ek0", [128, 8], F32), sb("Ek1", [128, 8], F32)]
    mXtb = [sb("mXtb0", [128, 512], BF16), sb("mXtb1", [128, 512], BF16)]
    C_b = sb("C_bc", [128, 512], BF16)
    dbc_r = [sb("dbc_r0", [128, 512], BF16), sb("dbc_r1", [128, 512], BF16)]
    tmpa = sb("tmpac", [128, 512], F32)
    tmpb = sb("tmpbc", [128, 512], F32)
    pXC = cm["ps_c"]
    pB = cm["ps_f"]

    def initf(g):
        for b in (st["S0f"], st["C0f"], st["Gnf"]):
            g.memset(b[:, :], 0.0)
        return g.memset(C_b[:, :], 0.0)
    init = sch.step("gpsimd", initf, deps=[start_dep])

    def gld(sy, sem):
        sy.dma_start(out=gt[:, :], in_=gates_d[:, :]).then_inc(sem, 16)
    gl = sch.dma("sync", gld, 1)

    fold_h = [init, init]
    dm_h = [init, init]
    upd = init

    def emit_mlbl(k):
        par = k % 2

        def mld(sy, sem, k=k, par=par):
            sy.dma_start(out=mapb[par][:, :],
                         in_=maps_d[k, :, :]).then_inc(sem, 16)
        ml = sch.dma("sync", mld, 1, deps=[fold_h[par]])

        def blendf(v, k=k, par=par):
            g_ap = gt[:, k:k + 1]
            v.tensor_scalar(out=Dk[par][:, :], in0=mapb[par][:, 2048:2056],
                            scalar1=1.0, scalar2=g_ap,
                            op0=ALU.subtract, op1=ALU.mult)
            v.tensor_scalar(out=Ek[par][:, :], in0=mapb[par][:, 2056:2064],
                            scalar1=1.0, scalar2=g_ap,
                            op0=ALU.subtract, op1=ALU.mult)
            v.drain()
            v.tensor_scalar_add(out=Dk[par][:, :], in0=Dk[par][:, :],
                                scalar1=1.0)
            v.tensor_scalar_add(out=Ek[par][:, :], in0=Ek[par][:, :],
                                scalar1=1.0)
            v.tensor_scalar(out=mapb[par][:, 0:2048],
                            in0=mapb[par][:, 0:2048],
                            scalar1=g_ap, scalar2=None, op0=ALU.mult)
            v.drain()
            v.tensor_copy(out=mXtb[par][:, :], in_=mapb[par][:, 1024:1536])
            v.tensor_tensor(
                out=dbc_r[par][0:64, :].rearrange("p (j k) -> p j k", k=64),
                in0=Dk[par][0:64, :].unsqueeze(2).broadcast_to([64, 8, 64]),
                in1=cm["idm"][0:64, 0:64].unsqueeze(1).broadcast_to([64, 8, 64]),
                op=ALU.mult)
            return v.tensor_tensor(
                out=dbc_r[par][64:128, :].rearrange("p (j k) -> p j k", k=64),
                in0=Dk[par][64:128, :].unsqueeze(2).broadcast_to([64, 8, 64]),
                in1=cm["idm"][64:128, 64:128].unsqueeze(1)
                .broadcast_to([64, 8, 64]),
                op=ALU.mult)
        par = k % 2
        bl = sch.step("vector", blendf,
                      deps=[ml, gl, cm["ready"], fold_h[par], dm_h[par]])
        return bl

    bl = emit_mlbl(0)
    for k in range(NCORE - 1):
        par = k % 2

        def dmxcf(t, par=par):
            t.matmul(pB[0:64, :], cm["ones64"][0:64, :], dbc_r[par][0:64, :],
                     start=True, stop=True)
            t.matmul(pB[64:128, :], cm["ones64"][64:128, :],
                     dbc_r[par][64:128, :], start=True, stop=True)
            for j in range(NP):
                for hb in (0, 64):
                    last = t.matmul(pXC[hb:hb + 64, j * 64:(j + 1) * 64],
                                    mXtb[par][hb:hb + 64, j * 64:(j + 1) * 64],
                                    C_b[hb:hb + 64, j * 64:(j + 1) * 64],
                                    start=True, stop=True)
            return last
        dmxc = sch.step("tensor", dmxcf, deps=[bl, upd])
        dm_h[par] = dmxc
        if k < NCORE - 2:
            bl = emit_mlbl(k + 1)

        def foldf(v, k=k, par=par):
            r8 = lambda x: x[:, :].rearrange("p (a b) -> p a b", b=64)
            v.tensor_tensor(out=r8(tmpa), in0=r8(st["Gnf"]), in1=rowbc(Dk[par]),
                            op=ALU.mult)
            v.drain()
            v.tensor_tensor(out=tmpb[:, :], in0=tmpa[:, :], in1=pXC[:, :],
                            op=ALU.subtract)
            v.drain()
            v.tensor_tensor(out=st["Gnf"][:, :], in0=tmpb[:, :],
                            in1=mapb[par][:, 1536:2048], op=ALU.subtract)
            v.drain()
            v.tensor_tensor(out=r8(tmpa), in0=r8(st["S0f"]), in1=rowbc(Dk[par]),
                            op=ALU.mult)
            v.drain()
            v.tensor_tensor(out=tmpb[:, :], in0=tmpa[:, :], in1=pB[:, :],
                            op=ALU.mult)
            v.drain()
            v.tensor_tensor(out=st["S0f"][:, :], in0=tmpb[:, :],
                            in1=mapb[par][:, 0:512], op=ALU.add)
            v.drain()
            v.tensor_tensor(out=r8(tmpa), in0=r8(st["C0f"]), in1=rowbc(Ek[par]),
                            op=ALU.mult)
            v.drain()
            v.tensor_tensor(out=st["C0f"][:, :], in0=tmpa[:, :],
                            in1=mapb[par][:, 512:1024], op=ALU.add)
            v.drain()
            return v.tensor_copy(out=C_b[:, :], in_=st["C0f"][:, :])
        upd = sch.step("vector", foldf, deps=[dmxc])
        fold_h[par] = upd

    def castf(v):
        v.tensor_copy(out=st["S0b"][:, :], in_=st["S0f"][:, :])
        v.tensor_copy(out=st["C0b"][:, :], in_=st["C0f"][:, :])
        v.tensor_copy(out=st["Gnb"][:, :], in_=st["Gnf"][:, :])
        return v.tensor_scalar_mul(out=st["Cnb"][:, :], in0=st["C0f"][:, :],
                                   scalar1=-1.0)
    return sch.step("vector", castf, deps=[upd])


def emit_sweep2(nc, es, sch, cm, srcs, st, OTs, start_dep):
    cbs = [ChunkBufs(nc, es, "_s2a", with_sweep2=True),
           ChunkBufs(nc, es, "_s2b", with_sweep2=True)]
    sb = lambda n, s, dt: es.enter_context(nc.sbuf_tensor(n, s, dt))
    s1m = sb("s1m_2", [128, 256], BF16)
    s2m = sb("s2m_2", [128, 256], BF16)
    s3m = sb("s3m_2", [128, 256], BF16)
    rb = sb("rb_2", [128, 128], BF16)
    Wn_b = sb("Wn_b2", [128, DM], BF16)

    ptr = cm["ps_bf"]
    S1e = cm["ps_a"][:, 0:128]
    S3e = cm["ps_a"][:, 128:256]
    S2e = cm["ps_a"][:, 256:384]
    pR = cm["ps_a"][:, 384:512]
    S1o = cm["ps_b"][:, 512:640]
    S3o = cm["ps_b"][:, 640:768]
    S2o = cm["ps_b"][:, 768:896]
    pW = cm["ps_b"][:, 0:128]
    pOT = cm["ps_b"][:, 128:256]
    pUU = cm["ps_c"]
    pQV = cm["ps_d"]
    pGn = cm["ps_e"]
    pB = cm["ps_f"]

    upd = start_dep
    upd_hist = [start_dep, start_dep]
    prev = {"eva": None, "b": None, "ewn": None, "evr": None, "s2": None,
            "ev2": None, "otug": None, "evot": None, "tmev": None, "plb": None}
    for c in range(NCH):
        par = c % 2
        cb = cbs[par]
        pr = emit_chunk_prep(nc, sch, cm, cb, srcs, c, sweep2=True,
                             extra_dep=[upd_hist[par]])
        tms = {}
        for name, src, dst in (("Ut", cb.Ut, cb.Utm), ("Qh", cb.Qh, cb.Qhtm)):
            def tf(t, src=src):
                for j in range(NP):
                    last = t.transpose(ptr[:, j * 128:(j + 1) * 128],
                                       src[:, j * 128:(j + 1) * 128],
                                       cm["idm"][:, :])
                return last
            ts = sch.step("tensor", tf, deps=[pr["gates"], cm["ptrtok"][0]])

            def vf(s, dst=dst):
                return s.copy(out=dst[:, :], in_=ptr[:, :])
            prev_ev = sch.step("scalar", vf, deps=[ts])
            cm["ptrtok"][0] = prev_ev
            tms[name] = prev_ev
        plb = emit_plbcast(nc, sch, cm, cb, pB,
                           [pr["gates"]] + ([prev["plb"]] if prev["plb"]
                                            else [upd]))
        prev["plb"] = plb

        uqg_steps, ot_evs = [], []
        for j in range(NP):
            pj = slice(j * 128, (j + 1) * 128)

            def af(t, j=j, pj=pj, cb=cb):
                t.matmul(S1e[:, :], cb.Ut[0:64, pj], cb.Qb[0:64, pj],
                         start=True, stop=True)
                t.matmul(S1o[:, :], cb.Ut[64:128, pj], cb.Qb[64:128, pj],
                         start=True, stop=True)
                t.matmul(S3e[:, :], cb.Qh[0:64, pj], cb.Kb[0:64, pj],
                         start=True, stop=True)
                return t.matmul(S3o[:, :], cb.Qh[64:128, pj],
                                cb.Kb[64:128, pj], start=True, stop=True)
            a = sch.step("tensor", af, deps=[pr["gates"]] +
                         ([prev["eva"]] if prev["eva"] else []))

            def evaf(v):
                v.tensor_tensor(out=s1m[:, 0:128], in0=S1e[:, :],
                                in1=cm["m_incl"][:, :], op=ALU.mult)
                v.tensor_tensor(out=s1m[:, 128:256], in0=S1o[:, :],
                                in1=cm["m_incl"][:, :], op=ALU.mult)
                v.tensor_tensor(out=s3m[:, 0:128], in0=S3e[:, :],
                                in1=cm["m_strn"][:, :], op=ALU.mult)
                return v.tensor_tensor(out=s3m[:, 128:256], in0=S3o[:, :],
                                       in1=cm["m_strn"][:, :], op=ALU.mult)
            eva = sch.step("vector", evaf, deps=[a] +
                           ([prev["otug"]] if prev["otug"] else []))
            prev["eva"] = eva

            def bf(t, j=j, pj=pj, cb=cb):
                for hh, hb in ((2 * j, 0), (2 * j + 1, 64)):
                    msl = slice((hb // 64) * 128, (hb // 64) * 128 + 128)
                    t.matmul(pW[:, (hh % 2) * 64:(hh % 2) * 64 + 64],
                             cb.Kb[hb:hb + 64, pj],
                             st["Cnb"][hb:hb + 64, j * 64:(j + 1) * 64],
                             start=True, stop=False)
                    t.matmul(pW[:, (hh % 2) * 64:(hh % 2) * 64 + 64],
                             s3m[:, msl], cb.vt[:, hh * 64:(hh + 1) * 64],
                             start=False, stop=True)
                    t.matmul(pR[hb:hb + 64, :],
                             st["S0b"][hb:hb + 64, j * 64:(j + 1) * 64],
                             cb.Qb[hb:hb + 64, pj], start=True, stop=False)
                    last = t.matmul(pR[hb:hb + 64, :],
                                    cb.Utm[:, hh * 64:(hh + 1) * 64],
                                    s1m[:, msl], start=False, stop=True)
                return last
            b = sch.step("tensor", bf,
                         deps=[eva, tms["Ut"], upd] +
                         ([prev["ewn"]] if prev["ewn"] else []) +
                         ([prev["evr"]] if prev["evr"] else []))

            def ewnf(s, j=j):
                return s.copy(out=Wn_b[:, j * 128:(j + 1) * 128], in_=pW[:, :])
            ewn = sch.step("scalar", ewnf, deps=[b])
            prev["ewn"] = ewn

            def evrf(v, j=j, pj=pj, cb=cb):
                return v.tensor_tensor(out=rb[:, :], in0=pR[:, :],
                                       in1=cb.PRg[:, pj], op=ALU.mult)
            evr = sch.step("vector", evrf, deps=[b, pr["gates"]] +
                           ([prev["otug"]] if prev["otug"] else []))
            prev["evr"] = evr

            def s2f(t, pj=pj, cb=cb):
                t.matmul(S2e[:, :], cb.Qh[0:64, pj], rb[0:64, :],
                         start=True, stop=True)
                return t.matmul(S2o[:, :], cb.Qh[64:128, pj], rb[64:128, :],
                                start=True, stop=True)
            s2 = sch.step("tensor", s2f, deps=[evr] +
                          ([prev["ev2"]] if prev["ev2"] else []))
            prev["s2"] = s2

            def ev2f(v):
                v.tensor_tensor(out=s2m[:, 0:128], in0=S2e[:, :],
                                in1=cm["m_incl"][:, :], op=ALU.mult)
                return v.tensor_tensor(out=s2m[:, 128:256], in0=S2o[:, :],
                                       in1=cm["m_incl"][:, :], op=ALU.mult)
            ev2 = sch.step("vector", ev2f, deps=[s2] +
                           ([prev["otug"]] if prev["otug"] else []))
            prev["ev2"] = ev2

            def otugf(t, j=j, pj=pj, cb=cb):
                for hh, hb in ((2 * j, 0), (2 * j + 1, 64)):
                    sl64 = slice(j * 64, (j + 1) * 64)
                    msl = slice((hb // 64) * 128, (hb // 64) * 128 + 128)
                    t.matmul(pOT[hb:hb + 64, :], st["C0b"][hb:hb + 64, sl64],
                             rb[hb:hb + 64, :], start=True, stop=False)
                    t.matmul(pOT[hb:hb + 64, :],
                             cb.vt[:, hh * 64:(hh + 1) * 64], s2m[:, msl],
                             start=False, stop=False)
                    t.matmul(pOT[hb:hb + 64, :], st["Gnb"][hb:hb + 64, sl64],
                             cb.Qb[hb:hb + 64, pj], start=False, stop=False)
                    t.matmul(pOT[hb:hb + 64, :],
                             Wn_b[:, hh * 64:(hh + 1) * 64], s1m[:, msl],
                             start=False, stop=True)
                    t.matmul(pUU[hb:hb + 64, j * 64:(j + 1) * 64],
                             cb.Utm[:, hh * 64:(hh + 1) * 64],
                             cb.Utm[:, hh * 64:(hh + 1) * 64],
                             start=True, stop=True)
                    t.matmul(pQV[hb:hb + 64, j * 64:(j + 1) * 64],
                             cb.Qhtm[:, hh * 64:(hh + 1) * 64],
                             cb.vt[:, hh * 64:(hh + 1) * 64],
                             start=True, stop=True)
                    last = t.matmul(pGn[hb:hb + 64, j * 64:(j + 1) * 64],
                                    cb.Utm[:, hh * 64:(hh + 1) * 64],
                                    Wn_b[:, hh * 64:(hh + 1) * 64],
                                    start=True, stop=True)
                return last
            otug = sch.step("tensor", otugf,
                            deps=[eva, ev2, ewn, evr, tms["Ut"], tms["Qh"],
                                  upd] +
                            ([prev["evot"]] if prev["evot"] else []))
            prev["otug"] = otug
            uqg_steps.append(otug)

            def evotf(v, j=j, c=c):
                return v.tensor_copy(out=OTs[:, j, c * L:(c + 1) * L],
                                     in_=pOT[:, :])
            evot = sch.step("vector", evotf, deps=[otug])
            prev["evot"] = evot
            ot_evs.append(evot)

        def updf(v, cb=cb):
            r8 = lambda x: x[:, :].rearrange("p (a b) -> p a b", b=64)
            v.tensor_tensor(out=cm["tmpa"][:, :], in0=pUU[:, :], in1=st["S0f"][:, :],
                            op=ALU.add)
            v.drain()
            v.tensor_tensor(out=r8(cm["tmpb"]), in0=r8(cm["tmpa"]), in1=rowbc(cb.plK),
                            op=ALU.mult)
            v.drain()
            v.tensor_tensor(out=st["S0f"][:, :], in0=cm["tmpb"][:, :],
                            in1=cb.plbc[:, :], op=ALU.mult)
            v.drain()
            v.tensor_tensor(out=cm["tmpa"][:, :], in0=pQV[:, :], in1=st["C0f"][:, :],
                            op=ALU.add)
            v.drain()
            v.tensor_tensor(out=r8(st["C0f"]), in0=r8(cm["tmpa"]), in1=rowbc(cb.rlE),
                            op=ALU.mult)
            v.drain()
            v.tensor_tensor(out=cm["tmpa"][:, :], in0=pGn[:, :], in1=st["Gnf"][:, :],
                            op=ALU.add)
            v.drain()
            v.tensor_tensor(out=r8(st["Gnf"]), in0=r8(cm["tmpa"]), in1=rowbc(cb.plK),
                            op=ALU.mult)
            v.drain()
            v.tensor_copy(out=st["S0b"][:, :], in_=st["S0f"][:, :])
            v.tensor_copy(out=st["C0b"][:, :], in_=st["C0f"][:, :])
            v.tensor_copy(out=st["Gnb"][:, :], in_=st["Gnf"][:, :])
            return v.tensor_scalar_mul(out=st["Cnb"][:, :], in0=st["C0f"][:, :],
                                       scalar1=-1.0)
        upd = sch.step("vector", updf, deps=uqg_steps + ot_evs + [plb, upd])
        upd_hist[par] = upd
    return upd


def emit_quant_sg(nc, sch, cm, qx, ws_f32, W, deps):
    def sgf(s):
        s.activation(out=qx["xa"][:, 0:W], in_=ws_f32[:, 0:W], func=AF.Abs)
        return s.activation(out=qx["sgb"][:, 0:W], in_=ws_f32[:, 0:W],
                            func=AF.Sign)
    return sch.step("scalar", sgf, deps=deps)


def emit_quant(nc, sch, cm, qb, ws_f32, W, out_bf16, deps, prefix=None):
    sg = emit_quant_sg(nc, sch, cm, qb, ws_f32, W, deps)
    return emit_quant_qv(nc, sch, cm, qb, qb, ws_f32, W, out_bf16,
                         list(deps) + [sg], prefix=prefix)


def emit_quant_qv(nc, sch, cm, qb, qx, ws_f32, W, out_bf16, deps,
                  prefix=None):
    NB = W // 16

    def vf(v):
        if prefix is not None:
            prefix(v)
            v.drain()
        v.tensor_reduce(out=qb["bm"][:, 0:NB],
                        in_=ws_f32[:, 0:W].rearrange("p (b w) -> p b w", w=16),
                        axis=mybir.AxisListType.X, op=ALU.max,
                        apply_absolute_value=True)
        v.drain()
        v.tensor_scalar(out=qb["sc"][:, 0:NB], in0=qb["bm"][:, 0:NB],
                        scalar1=1.0 / 6.0, scalar2=1e-12, op0=ALU.mult,
                        op1=ALU.max)
        v.drain()
        v.reciprocal(out=qb["ri"][:, 0:NB], in_=qb["sc"][:, 0:NB])
        v.drain()
        v.tensor_tensor(
            out=qb["tn"][:, 0:W].rearrange("p (b w) -> p b w", w=16),
            in0=qx["xa"][:, 0:W].rearrange("p (b w) -> p b w", w=16),
            in1=qb["ri"][:, 0:NB].unsqueeze(2).broadcast_to([128, NB, 16]),
            op=ALU.mult)
        v.drain()
        v.tensor_scalar(out=qb["cqA"][:, 0:W], in0=qb["tn"][:, 0:W],
                        scalar1=0.25, scalar2=None, op0=ALU.is_gt)
        for m in (0.75, 1.25, 1.75):
            v.drain()
            v.scalar_tensor_tensor(out=qb["cqA"][:, 0:W],
                                   in0=qb["tn"][:, 0:W], scalar=m,
                                   in1=qb["cqA"][:, 0:W],
                                   op0=ALU.is_gt, op1=ALU.add)
        v.tensor_scalar(out=qb["cqB"][:, 0:W], in0=qb["tn"][:, 0:W],
                        scalar1=2.5, scalar2=None, op0=ALU.is_gt)
        v.drain()
        v.scalar_tensor_tensor(out=qb["cqB"][:, 0:W], in0=qb["tn"][:, 0:W],
                               scalar=3.5, in1=qb["cqB"][:, 0:W],
                               op0=ALU.is_gt, op1=ALU.add)
        v.drain()
        v.scalar_tensor_tensor(out=qb["cqB"][:, 0:W], in0=qb["cqA"][:, 0:W],
                               scalar=0.5, in1=qb["cqB"][:, 0:W],
                               op0=ALU.mult, op1=ALU.add)
        v.drain()
        v.tensor_scalar(out=qb["cqA"][:, 0:W], in0=qb["tn"][:, 0:W],
                        scalar1=5.0, scalar2=2.0, op0=ALU.is_gt, op1=ALU.mult)
        v.drain()
        v.tensor_tensor(out=qb["cqA"][:, 0:W], in0=qb["cqA"][:, 0:W],
                        in1=qb["cqB"][:, 0:W], op=ALU.add)
        v.drain()
        v.tensor_tensor(out=qb["cqA"][:, 0:W], in0=qb["cqA"][:, 0:W],
                        in1=qx["sgb"][:, 0:W], op=ALU.mult)
        v.drain()
        return v.tensor_tensor(
            out=out_bf16[:, 0:W].rearrange("p (b w) -> p b w", w=16),
            in0=qb["cqA"][:, 0:W].rearrange("p (b w) -> p b w", w=16),
            in1=qb["sc"][:, 0:NB].unsqueeze(2).broadcast_to([128, NB, 16]),
            op=ALU.mult)
    return sch.step("vector", vf, deps=list(deps))


def alloc_qb(nc, es, tag, W):
    sb = lambda n, s, dt: es.enter_context(nc.sbuf_tensor(n + tag, s, dt))
    return {"xa": sb("q_xa", [128, W], F32), "bm": sb("q_bm", [128, W // 16], F32),
            "sc": sb("q_sc", [128, W // 16], F32),
            "ri": sb("q_ri", [128, W // 16], F32),
            "tn": sb("q_tn", [128, W], F32), "cqA": sb("q_cqA", [128, W], BF16),
            "cqB": sb("q_cqB", [128, W], BF16), "sgb": sb("q_sgb", [128, W], BF16)}


def emit_ln(nc, sch, cm, lb, psum_in, xres, out_f32, deps, sq_war=()):
    def v1(v):
        if isinstance(psum_in, (list, tuple)):
            v.tensor_tensor(out=lb["uf"][:, 0:512], in0=psum_in[0][:, :],
                            in1=xres[:, 0:512], op=ALU.add)
            v.tensor_tensor(out=lb["uf"][:, 512:1024], in0=psum_in[1][:, :],
                            in1=xres[:, 512:1024], op=ALU.add)
        else:
            v.tensor_tensor(out=lb["uf"][:, :], in0=psum_in[:, :],
                            in1=xres[:, :], op=ALU.add)
        v.drain()
        return v.reduce_sum(out=lb["st4"][:, 0:1], in_=lb["uf"][:, :],
                            axis=mybir.AxisListType.X)
    s1 = sch.step("vector", v1, deps=deps)

    def sq(s):
        return s.activation(out=lb["sq"][:, :], in_=lb["uf"][:, :],
                            func=AF.Square)
    s2 = sch.step("scalar", sq, deps=[s1] + list(sq_war))

    def v2(v):
        v.reduce_sum(out=lb["st4"][:, 1:2], in_=lb["sq"][:, :],
                     axis=mybir.AxisListType.X)
        v.drain()
        v.tensor_scalar_mul(out=lb["st4"][:, 0:2], in0=lb["st4"][:, 0:2],
                            scalar1=1.0 / DM)
        v.drain()
        v.tensor_tensor(out=lb["st4"][:, 2:3], in0=lb["st4"][:, 0:1],
                        in1=lb["st4"][:, 0:1], op=ALU.mult)
        v.drain()
        return v.tensor_tensor(out=lb["st4"][:, 3:4], in0=lb["st4"][:, 1:2],
                               in1=lb["st4"][:, 2:3], op=ALU.subtract)
    s3 = sch.step("vector", v2, deps=[s2])

    def s4f(s):
        s.activation(out=lb["st4"][:, 2:3], in_=lb["st4"][:, 3:4], func=AF.Ln,
                     bias=EPS)
        s.drain()
        return s.activation(out=lb["st4"][:, 3:4], in_=lb["st4"][:, 2:3],
                            func=AF.Exp, scale=-0.5)
    s4 = sch.step("scalar", s4f, deps=[s3])

    def v3(v):
        return v.tensor_scalar(out=out_f32[:, :], in0=lb["uf"][:, :],
                               scalar1=lb["st4"][:, 0:1],
                               scalar2=lb["st4"][:, 3:4],
                               op0=ALU.subtract, op1=ALU.mult)
    return sch.step("vector", v3, deps=[s4]), s1, s3




def emit_wtr(nc, sch, cm, ptr, wq, qds, evhist, tras, slabs, i):
    wdram, dst, sl, Wrow = slabs[i]
    par = i % 2
    ntile = Wrow // 128
    prev = qds[i]
    for g0 in range(0, ntile, 8):
        gn = min(8, ntile - g0)

        def trg(t, g0=g0, gn=gn, par=par):
            for it in range(gn):
                lastt = t.transpose(ptr[:, it * 128:(it + 1) * 128],
                                    wq[par][:, (g0 + it) * 128:
                                            (g0 + it + 1) * 128],
                                    cm["idm"][:, :])
            return lastt
        tg = sch.step("tensor", trg,
                      deps=[prev] + ([evhist[0]] if evhist[0] else []))

        def evg(v, g0=g0, gn=gn, dst=dst, sl=sl):
            return v.tensor_copy(
                out=dst[:, g0:g0 + gn, sl * 128:(sl + 1) * 128],
                in_=ptr[:, 0:gn * 128].rearrange("p (a b) -> p a b", b=128))
        prev = sch.step("vector", evg, deps=[tg])
        evhist[0] = prev
    tras[i] = prev


def emit_fdn(nc, sch, cm, ptr, hqc, hqT, wqdT, pDN, dn_mms, hist, ICH,
             hq, ici, i0, ilen, par, tc):
    war = lambda k, n=2: ([hist[k][-n]] if len(hist[k]) >= n else [])
    ntile = ilen // 128

    def htr(t, ntile=ntile, par=par):
        for it in range(ntile):
            lastt = t.transpose(
                ptr[:, (par * 4 + it) * 128:(par * 4 + it + 1) * 128],
                hqc[par][:, it * 128:(it + 1) * 128], cm["idm"][:, :])
        return lastt
    ht = sch.step("tensor", htr, deps=[hq] + war("hte"))
    hist["ht"].append(ht)

    def htef(s, ntile=ntile, par=par):
        return s.copy(
            out=hqT[par][:, 0:ntile, :].rearrange("p a b -> p (a b)"),
            in_=ptr[:, par * 512:par * 512 + ntile * 128])
    hte = sch.step("scalar", htef, deps=[ht] + war("dm", 1))
    hist["hte"].append(hte)

    def dmf(t, i0=i0, ntile=ntile, ici=ici, par=par, tc=tc):
        for it in range(ntile):
            git = i0 // 128 + it
            for n2 in range(2):
                last_i = t.matmul(
                    pDN[:, n2 * 512:(n2 + 1) * 512], hqT[par][:, it, :],
                    wqdT[:, git, n2 * 512:(n2 + 1) * 512],
                    start=(ici == 0 and it == 0),
                    stop=(ici == len(ICH) - 1 and it == ntile - 1))
        return last_i
    dm = sch.step("tensor", dmf, deps=[hte] + dn_mms[-1:] +
                  (hist["ln2s1"][-1:] if ici == 0 else []))
    dn_mms.append(dm)
    hist["dm"].append(dm)


def build_fused():
    nc = bass.Bass("TRN2", target_bir_lowering=False, debug=False)
    sch = Sched(nc)
    x_d = nc.dram_tensor("x", [TC, DM], F32, kind="ExternalInput").ap()
    wd = {n: nc.dram_tensor(n, [DM, DM], F32, kind="ExternalInput").ap()
          for n in ("W_q", "W_k", "W_v", "Wg_K", "Wg_C")}
    wd["bg_K"] = nc.dram_tensor("bg_K", [1, DM], F32, kind="ExternalInput").ap()
    wd["bg_C"] = nc.dram_tensor("bg_C", [1, DM], F32, kind="ExternalInput").ap()
    gates_d = nc.dram_tensor("gates", [128, 8], F32, kind="ExternalInput").ap()
    Wo_d = nc.dram_tensor("W_o", [DM, DM], F32, kind="ExternalInput").ap()
    y_d = nc.dram_tensor("y", [TC, DM], F32, kind="ExternalOutput").ap()

    q_s = nc.dram_tensor("q_s", [NCH, 128, FC * L], BF16, kind="Internal").ap()
    k_s = nc.dram_tensor("k_s", [NCH, 128, FC * L], BF16, kind="Internal").ap()
    v_s = nc.dram_tensor("v_s", [TC, DM], BF16, kind="Internal").ap()
    sK_s = nc.dram_tensor("spK_s", [NCH, 128, FC * L], FP16,
                          kind="Internal").ap()
    sC_s = nc.dram_tensor("spC_s", [NCH, 128, FC * L], FP16,
                          kind="Internal").ap()
    x1_d = nc.dram_tensor("x1_s", [TC, DM], F32, kind="Internal").ap()
    mb_in = nc.dram_tensor("mb_in", [128, 2064], F32, kind="Internal")
    mb_out = nc.dram_tensor("mb_out", [NCORE, 128, 2064], F32, kind="Internal")
    wg_my = nc.dram_tensor("wg_my", [384, DM], F32, kind="ExternalInput").ap()
    wu_my = nc.dram_tensor("wu_my", [384, DM], F32, kind="ExternalInput").ap()
    wdn_my = nc.dram_tensor("wdn_my", [128, DI], F32, kind="ExternalInput").ap()
    wqb_in = nc.dram_tensor("wqb_in", [128, 8960], BF16, kind="Internal")
    wqb_out = nc.dram_tensor("wqb_out", [NCORE, 128, 8960], BF16,
                             kind="Internal")

    es = ExitStack()
    with es:
        cm = alloc_common(nc, es, sch)

        qb = alloc_qb(nc, es, "_f", 1024)
        ws = [es.enter_context(nc.sbuf_tensor("ws0", [128, DM], F32)),
              es.enter_context(nc.sbuf_tensor("ws1", [128, DM], F32))]
        wq = [es.enter_context(nc.sbuf_tensor("wq0", [128, DM], BF16)),
              es.enter_context(nc.sbuf_tensor("wq1", [128, DM], BF16))]


        myslabs = [(wdn_my, 0, DI, 6144),
                   (wg_my, 0, DM, 0), (wu_my, 0, DM, 3072),
                   (wg_my, 1, DM, 1024), (wu_my, 1, DM, 4096),
                   (wg_my, 2, DM, 2048), (wu_my, 2, DM, 5120)]
        wctx = {"ld": [cm["ready"], cm["ready"]], "ev": [None],
                "tr": {}, "qd": [], "fins": [], "pi": 0}

        def co_piece(wdram, row, p0, plen, off):
            def go():
                par = wctx["pi"] % 2
                wctx["pi"] += 1

                def wl(sy, sem, wdram=wdram, row=row, p0=p0, plen=plen,
                       par=par):
                    sy.dma_start(out=ws[par][:, 0:plen],
                                 in_=wdram[row * 128:(row + 1) * 128,
                                           p0:p0 + plen]
                                 ).then_inc(sem, 16)
                ld = sch.dma("sync", wl, 1, deps=[wctx["ld"][par]])
                qd = emit_quant(nc, sch, cm, qb, ws[par][:, 0:plen],
                                plen, wq[par][:, 0:plen],
                                [ld] + wctx["qd"][-1:])
                wctx["qd"].append(qd)
                wctx["ld"][par] = qd
                ntile = plen // 128
                sview = wqstage[:, off + p0:off + p0 + plen].rearrange(
                    "p (a b) -> p a b", b=128)

                def trg(t, ntile=ntile, par=par):
                    for it in range(ntile):
                        lastt = t.transpose(
                            cm["ps_bf"][:, it * 128:(it + 1) * 128],
                            wq[par][:, it * 128:(it + 1) * 128],
                            cm["idm"][:, :])
                    return lastt
                tg = sch.step("tensor", trg,
                              deps=[qd, cm["ptrtok"][0]])

                def evg(v, ntile=ntile, sview=sview):
                    return v.tensor_copy(
                        out=sview[:, :, :],
                        in_=cm["ps_bf"][:, 0:ntile * 128]
                        .rearrange("p (a b) -> p a b", b=128))
                prev = sch.step("vector", evg, deps=[tg])
                cm["ptrtok"][0] = prev
                wctx["ev"][0] = prev
                wctx["fins"].append(prev)
            return go

        co = []
        for i in range(len(myslabs)):
            wdram, row, Wrow, off = myslabs[i]
            for p0 in range(0, Wrow, 512):
                plen = min(512, Wrow - p0)
                co.append(co_piece(wdram, row, p0, plen, off))

        esA = ExitStack()
        with esA:
            wqstage = esA.enter_context(
                nc.sbuf_tensor("wqstage", [128, 8960], BF16))
            xT = esA.enter_context(nc.sbuf_tensor("xT", [128, FC, TC], BF16))
            xt_done = emit_xT(nc, esA, sch, cm, x_d, xT)
            outs = {"q_s": q_s, "k_s": k_s, "spK_s": sK_s, "spC_s": sC_s,
                    "v_s": v_s}
            pj_done = emit_projections(nc, esA, sch, cm, xT, xt_done, wd, outs)
            s1_fin = emit_sweep1(nc, esA, sch, cm,
                                 (q_s, k_s, v_s, sK_s, sC_s),
                                 mb_in.ap(), pj_done, co=co)

            def wst(sy, sem):
                sy.dma_start(out=wqb_in.ap()[:, :],
                             in_=wqstage[:, :]).then_inc(sem, 16)
            wst_d = sch.dma("sync", wst, 1, deps=wctx["fins"])

        def ccf(g):
            return g.collective_compute(
                "AllGather", ALU.bypass, replica_groups=[list(range(NCORE))],
                ins=[mb_in.ap().opt()], outs=[mb_out.ap().opt()])
        cc = sch.step("gpsimd", ccf, deps=[s1_fin, wst_d], coll=True)

        def ccw(g):
            return g.collective_compute(
                "AllGather", ALU.bypass, replica_groups=[list(range(NCORE))],
                ins=[wqb_in.ap().opt()], outs=[wqb_out.ap().opt()])
        ccw_s = sch.step("gpsimd", ccw, deps=[wst_d], coll=True)

        esB = ExitStack()
        with esB:
            sb1 = lambda n, s, dt: esB.enter_context(nc.sbuf_tensor(n, s, dt))
            st = {"S0f": sb1("S0f", [128, 512], F32),
                  "C0f": sb1("C0f", [128, 512], F32),
                  "Gnf": sb1("Gnf", [128, 512], F32),
                  "S0b": sb1("S0b", [128, 512], BF16),
                  "C0b": sb1("C0b", [128, 512], BF16),
                  "Cnb": sb1("Cnb", [128, 512], BF16),
                  "Gnb": sb1("Gnb", [128, 512], BF16)}
            OTs = sb1("OTs", [128, FC, TC], BF16)
            comp = emit_compose(nc, esB, sch, cm, mb_out.ap(), gates_d, st, cc)
            s2_done = emit_sweep2(nc, esB, sch, cm, (q_s, k_s, v_s, sK_s, sC_s),
                                  st, OTs, comp)
            Wo = sb1("Wo", [128, FC, DM], BF16)
            xrs = [qb["xa"], sb1("xr1", [128, DM], F32)]
            lbs = [{"uf": sb1("ln_uf", [128, DM], F32),
                    "sq": qb["tn"],
                    "st4": sb1("ln_st4", [128, 4], F32)},
                   {"uf": sb1("ln_uf1", [128, DM], F32),
                    "sq": qb["cqB"],
                    "st4": sb1("ln_st4b", [128, 4], F32)}]
            pLNs = [cm["ps_b"],
                    [cm["ps_c"], cm["ps_d"]]]

            def wol(gp, sem):
                gp.dma_start(out=Wo[:, :, :],
                             in_=Wo_d.rearrange("(a p) c -> p a c", p=128)
                             ).then_inc(sem, 16)
            wold = sch.dma("gpsimd", wol, 1, deps=[cc])
            s1h, s3h, xsoh = [None, None], [None, None], [None, None]
            xso_all = []
            last = s2_done
            for tc in range(NCH):
                par = tc % 2
                pLN = pLNs[par]

                def xrl(sy, sem, tc=tc, par=par):
                    sy.dma_start(out=xrs[par][:, :],
                                 in_=x_d[tc * 128:(tc + 1) * 128, :]
                                 ).then_inc(sem, 16)
                xl = sch.dma("sync", xrl, 1,
                             deps=([s1h[par]] if s1h[par] else [comp]))

                def mmf(t, tc=tc, pLN=pLN, par=par):
                    for n2 in range(2):
                        dst = (pLN[:, n2 * 512:(n2 + 1) * 512] if par == 0
                               else pLN[n2][:, :])
                        for fk in range(FC):
                            last_i = t.matmul(
                                dst,
                                OTs[:, fk, tc * 128:(tc + 1) * 128],
                                Wo[:, fk, n2 * 512:(n2 + 1) * 512],
                                start=(fk == 0), stop=(fk == FC - 1))
                    return last_i
                mm = sch.step("tensor", mmf, deps=[wold, s2_done] +
                              ([s1h[par]] if s1h[par] else []))
                ln, ls1, ls3 = emit_ln(
                    nc, sch, cm, lbs[par], pLN, xrs[par], lbs[par]["uf"],
                    [mm, xl] + ([xsoh[par]] if xsoh[par] else []),
                    sq_war=([s3h[par]] if s3h[par] else []))
                s1h[par], s3h[par] = ls1, ls3

                def xso(sy, sem, tc=tc, par=par):
                    sy.dma_start(out=x1_d[tc * 128:(tc + 1) * 128, :],
                                 in_=lbs[par]["uf"][:, :]).then_inc(sem, 16)
                last = sch.dma("sync", xso, 1, deps=[ln])
                xsoh[par] = last
                xso_all.append(last)
            wo_done = sch.step("gpsimd", lambda g: g.memset(
                cm["cpool"][:, 3:4], 0.0), deps=xso_all)

        esC = ExitStack()
        with esC:
            sb2 = lambda n, s, dt: esC.enter_context(nc.sbuf_tensor(n, s, dt))
            wqgT = sb2("wqgT", [128, FC, DI], BF16)
            wquT = sb2("wquT", [128, FC, DI], BF16)
            wqdT = sb2("wqdT", [128, DI // 128, DM], BF16)
            ptr = cm["ps_bf"]

            def wload_g(sy, sem):
                for sl in range(DI // 128):
                    core, k = (sl // 3, sl % 3) if sl < 21 else (7, 0)
                    sy.dma_start(
                        out=wqgT[:, :, sl * 128:(sl + 1) * 128],
                        in_=wqb_out.ap()[core, :, k * 1024:(k + 1) * 1024]
                        .rearrange("p (a b) -> p a b", b=128)
                    ).then_inc(sem, 16)
                for sl in range(DM // 256):
                    sy.dma_start(
                        out=wqdT[:, :, sl * 128:(sl + 1) * 128],
                        in_=wqb_out.ap()[sl, :, 6144:6144 + DI]
                        .rearrange("p (a b) -> p a b", b=128)
                    ).then_inc(sem, 16)

            def wload_u(gp, sem):
                for sl in range(DI // 128):
                    core, k = (sl // 3, sl % 3) if sl < 21 else (7, 0)
                    gp.dma_start(
                        out=wquT[:, :, sl * 128:(sl + 1) * 128],
                        in_=wqb_out.ap()[core, :, 3072 + k * 1024:
                                         3072 + (k + 1) * 1024]
                        .rearrange("p (a b) -> p a b", b=128)
                    ).then_inc(sem, 16)
                for sl in range(DM // 256, DM // 128):
                    gp.dma_start(
                        out=wqdT[:, :, sl * 128:(sl + 1) * 128],
                        in_=wqb_out.ap()[sl, :, 6144:6144 + DI]
                        .rearrange("p (a b) -> p a b", b=128)
                    ).then_inc(sem, 16)
            wq_done_g = sch.dma("sync", wload_g, 26, deps=[ccw_s, wo_done])
            wq_done_u = sch.dma("gpsimd", wload_u, 26, deps=[ccw_s, wo_done])
            wq_done2 = [wq_done_g, wq_done_u]

            x1b = [sb2("x1b0", [128, DM], F32), sb2("x1b1", [128, DM], F32)]
            x1q0 = sb2("x1q0", [128, DM], BF16)
            x1q = [x1q0, x1q0]
            xqT = [sb2("xqT0", [128, FC, 128], BF16),
                   sb2("xqT1", [128, FC, 128], BF16)]
            hg = [sb2("hg0", [128, 512], F32), sb2("hg1", [128, 512], F32)]
            hu = [sb2("hu0", [128, 512], F32), sb2("hu1", [128, 512], F32)]
            hh = [sb2("hh0", [128, 512], F32), sb2("hh1", [128, 512], F32)]
            hqc = [sb2("hqc0", [128, 512], BF16),
                   sb2("hqc1", [128, 512], BF16)]
            hqT0 = sb2("hqT0", [128, 4, 128], BF16)
            hqT = [hqT0, hqT0]
            qxh = [{"xa": sb2("xah0", [128, 512], F32),
                    "sgb": sb2("sgh0", [128, 512], BF16)},
                   {"xa": sb2("xah1", [128, 512], F32),
                    "sgb": sb2("sgh1", [128, 512], BF16)}]
            lb2 = {"uf": sb2("l2_uf", [128, DM], F32),
                   "sq": sb2("l2sq", [128, DM], BF16),
                   "st4": sb2("l2_st4", [128, 4], F32)}
            yc = lb2["uf"]
            pG2 = [cm["ps_c"], cm["ps_e"]]
            pU2 = [cm["ps_d"], cm["ps_f"]]
            pDN = cm["ps_b"]

            ICH = [(0, 512), (512, 512), (1024, 512), (1536, 512),
                   (2048, 512), (2560, 256)]
            hist = {"ge": [], "hf": [], "qv": [], "ht": [], "hte": [],
                    "dm": [], "ln2": [], "ln2s1": [], "ln2s3": [],
                    "qdq": [], "xt_all": [], "xte": [None, None],
                    "ydma": []}
            war = lambda k, n=2: ([hist[k][-n]] if len(hist[k]) >= n else [])

            def emit_x1(tc):
                tpar = tc % 2

                def x1l(sy, sem, tc=tc, tpar=tpar):
                    sy.dma_start(out=x1b[tpar][:, :],
                                 in_=x1_d[tc * 128:(tc + 1) * 128, :]
                                 ).then_inc(sem, 16)
                xl = sch.dma("sync", x1l, 1,
                             deps=[wo_done] + hist["ln2"][-1:] +
                             hist["qdq"][-2:][:1])
                qdeps = [xl] + wq_done2 + hist["qdq"][-1:]
                qdeps += hist["xt_all"][-1:]
                if hist["xte"][tpar]:
                    qdeps.append(hist["xte"][tpar])
                qsg = emit_quant_sg(nc, sch, cm, qb, x1b[tpar], DM, qdeps)
                qd = emit_quant_qv(nc, sch, cm, qb, qb, x1b[tpar], DM,
                                   x1q[tpar], qdeps + [qsg])
                hist["qdq"].append(qd)

                def xtr(t, tpar=tpar):
                    for fc in range(FC):
                        lastt = t.transpose(ptr[:, fc * 128:(fc + 1) * 128],
                                            x1q[tpar][:, fc * 128:(fc + 1) * 128],
                                            cm["idm"][:, :])
                    return lastt
                xt = sch.step("tensor", xtr,
                              deps=[qd, cm["ptrtok"][0]] + war("hte", 1))
                hist["xt_all"].append(xt)
                xte = sch.step("scalar", lambda s, tpar=tpar: s.copy(
                    out=xqT[tpar][:, :, :].rearrange("p a b -> p (a b)"),
                    in_=ptr[:, 0:1024]), deps=[xt])
                hist["xte"][tpar] = xte
                cm["ptrtok"][0] = xte
                return qd, xte

            cur = emit_x1(0)
            for tc in range(NCH):
                tpar = tc % 2
                qd, xte = cur

                dn_mms = []
                pend_sg = None
                pend_fdn = None

                def emit_qv(pq):
                    sg_p, ici_p, i0_p, ilen_p, par_p = pq
                    qv = emit_quant_qv(
                        nc, sch, cm, qb, qxh[par_p], hh[par_p], ilen_p,
                        hqc[par_p], [sg_p] + war("ht", 1))
                    hist["qv"].append(qv)
                    return (qv, ici_p, i0_p, ilen_p, par_p)

                def emit_f(pf):
                    qv_p, ici_p, i0_p, ilen_p, par_p = pf
                    emit_fdn(nc, sch, cm, ptr, hqc, hqT, wqdT, pDN,
                             dn_mms, hist, ICH, qv_p, ici_p, i0_p, ilen_p,
                             par_p, tc)

                for ici, (i0, ilen) in enumerate(ICH):
                    par = ici % 2
                    pg, pu = pG2[par], pU2[par]

                    def gmumf(t, i0=i0, ilen=ilen, pg=pg, pu=pu, tpar=tpar):
                        for fk in range(FC):
                            t.matmul(pg[:, 0:ilen], xqT[tpar][:, fk, :],
                                     wqgT[:, fk, i0:i0 + ilen],
                                     start=(fk == 0), stop=(fk == FC - 1))
                        for fk in range(FC):
                            last_i = t.matmul(pu[:, 0:ilen], xqT[tpar][:, fk, :],
                                              wquT[:, fk, i0:i0 + ilen],
                                              start=(fk == 0),
                                              stop=(fk == FC - 1))
                        return last_i
                    gm = sch.step("tensor", gmumf,
                                  deps=[xte] + wq_done2 + war("ge"))

                    def geuef(s, ilen=ilen, pg=pg, pu=pu, par=par):
                        s.activation(out=hg[par][:, 0:ilen], in_=pg[:, 0:ilen],
                                     func=AF.Silu)
                        return s.copy(out=hu[par][:, 0:ilen], in_=pu[:, 0:ilen])
                    ge = sch.step("scalar", geuef, deps=[gm] + war("hf"))
                    hist["ge"].append(ge)

                    def hff(v, ilen=ilen, par=par):
                        return v.tensor_tensor(out=hh[par][:, 0:ilen],
                                               in0=hg[par][:, 0:ilen],
                                               in1=hu[par][:, 0:ilen],
                                               op=ALU.mult)
                    hstep = sch.step("vector", hff,
                                     deps=[ge] + war("qv", 1))
                    hist["hf"].append(hstep)
                    sg = emit_quant_sg(nc, sch, cm, qxh[par], hh[par], ilen,
                                       [hstep] + war("qv", 1))
                    if pend_fdn is not None:
                        emit_f(pend_fdn)
                        pend_fdn = None
                    if pend_sg is not None:
                        pend_fdn = emit_qv(pend_sg)
                    pend_sg = (sg, ici, i0, ilen, par)
                if pend_fdn is not None:
                    emit_f(pend_fdn)
                emit_f(emit_qv(pend_sg))

                if tc + 1 < NCH:
                    cur = emit_x1(tc + 1)

                ln2, l2s1, l2s3 = emit_ln(
                    nc, sch, cm, lb2, pDN, x1b[tpar], yc,
                    [dn_mms[-1], qd] + hist["ydma"][-1:],
                    sq_war=hist["ln2s3"][-1:])
                hist["ln2"].append(ln2)
                hist["ln2s1"].append(l2s1)
                hist["ln2s3"].append(l2s3)

                def yso(gp, sem, tc=tc):
                    gp.dma_start(out=y_d[tc * 128:(tc + 1) * 128, :],
                                 in_=yc[:, :]).then_inc(sem, 16)
                ydma = sch.dma("gpsimd", yso, 1, deps=[ln2])
                hist["ydma"].append(ydma)

            sch.step("gpsimd", lambda g: g.memset(cm["cpool"][:, 3:4], 0.0),
                     deps=[hist["ydma"][-1]])
        sch.run(es)
    return nc


def kernel(**inputs):
    x = np.asarray(inputs["L0_attn_x"], np.float32).reshape(T, DM)
    w = {k: np.ascontiguousarray(np.asarray(v, np.float32))
         for k, v in inputs.items()}
    gates = np.zeros((NCORE, 128, 8), np.float32)
    for c in range(NCORE):
        gates[c, :, :c] = 1.0
    def myw(warr):
        outs = []
        for c in range(NCORE):
            buf = np.zeros((384, DM), np.float32)
            lo = min(3 * c, 21)
            n = min(3, 22 - lo) if c < 7 else 1
            buf[0:n * 128] = warr[lo * 128:(lo + n) * 128]
            outs.append(buf)
        return outs
    wg_list = myw(w["L0_ffn_w_gate"])
    wu_list = myw(w["L0_ffn_w_up"])

    nc = build_fused()
    ins = [{"x": np.ascontiguousarray(x[c * TC:(c + 1) * TC]),
            "W_q": w["L0_attn_W_q"], "W_k": w["L0_attn_W_k"],
            "W_v": w["L0_attn_W_v"], "Wg_K": w["L0_attn_Wg_K"],
            "Wg_C": w["L0_attn_Wg_C"],
            "bg_K": w["L0_attn_bg_K"].reshape(1, DM),
            "bg_C": w["L0_attn_bg_C"].reshape(1, DM),
            "gates": gates[c], "W_o": w["L0_attn_W_o"],
            "wg_my": wg_list[c], "wu_my": wu_list[c],
            "wdn_my": np.ascontiguousarray(
                w["L0_ffn_w_down"][c * 128:(c + 1) * 128])}
           for c in range(NCORE)]
    r = run_bass_kernel_spmd(nc, ins, core_ids=list(range(NCORE))).results
    y = np.concatenate([np.asarray(r[c]["y"]) for c in range(NCORE)], axis=0)
    return y.reshape(1, T, DM)


# version 20 (final)
# speedup vs baseline: 1.0163x; 1.0119x over previous
import numpy as np
from contextlib import ExitStack

import concourse.bass as bass
import concourse.mybir as mybir
from concourse.bass_utils import run_bass_kernel_spmd

F32 = mybir.dt.float32
BF16 = mybir.dt.bfloat16
FP16 = mybir.dt.float16
I32 = mybir.dt.int32
AF = mybir.ActivationFunctionType
ALU = mybir.AluOpType

T, NCORE = 16384, 8
TC = T // NCORE
L = 128
NCH = TC // L
H, DK, DV, DM, DI = 16, 64, 64, 1024, 2816
NP = H // 2
FC = DM // 128
EPS = 1e-5
MIDS = [(0.25, 0.5), (0.75, 0.5), (1.25, 0.5), (1.75, 0.5),
        (2.5, 1.0), (3.5, 1.0), (5.0, 2.0)]
CONSTS = [0.0, 1.0, EPS]


class Sched:
    ENGINES = ("sync", "gpsimd", "scalar", "vector", "tensor")

    def __init__(self, nc):
        self.nc = nc
        self.steps = []

    def step(self, engine, fn, deps=(), coll=False):
        self.steps.append((engine, fn, tuple(deps), 0, coll))
        return len(self.steps) - 1

    def dma(self, engine, fn, n, deps=()):
        self.steps.append((engine, fn, tuple(deps), n, False))
        return len(self.steps) - 1

    NDMA_SEMS = 5

    def run(self, es):
        nc = self.nc
        semnames = list(self.ENGINES)
        for e in ("sync", "gpsimd"):
            semnames += [e + "_dma%d" % r for r in range(self.NDMA_SEMS)]
        ncoll = sum(1 for s in self.steps if s[4])
        semnames += ["cc%d" % r for r in range(ncoll)]
        sems = {e: es.enter_context(nc.semaphore("sem_" + e)) for e in semnames}
        dma_ord = {e: 0 for e in ("sync", "gpsimd")}
        cc_ord = 0
        semname = [None] * len(self.steps)
        prior = [0] * len(self.steps)
        done = [0] * len(self.steps)
        count = {e: 0 for e in semnames}
        for i, (e, fn, deps, nd, coll) in enumerate(self.steps):
            if nd:
                sn = e + "_dma%d" % (dma_ord[e] % self.NDMA_SEMS)
                dma_ord[e] += 1
            elif coll:
                sn = "cc%d" % cc_ord
                cc_ord += 1
            else:
                sn = e
            semname[i] = sn
            prior[i] = count[sn]
            count[sn] += 16 * nd if nd else 1
            done[i] = count[sn]
        with nc.Block() as blk:
            for e in self.ENGINES:
                mine = [(i, fn, deps, nd)
                        for i, (ee, fn, deps, nd, _coll) in
                        enumerate(self.steps) if ee == e]
                if not mine:
                    continue

                def make(eng_name, mine):
                    def body(eng):
                        for i, fn, deps, nd in mine:
                            need_drain = False
                            for d in deps:
                                dsem = semname[d]
                                if dsem != eng_name or self.steps[d][3]:
                                    eng.wait_ge(sems[dsem], done[d])
                                else:
                                    need_drain = True
                            if need_drain and eng_name in ("scalar", "vector"):
                                eng.drain()
                            if nd:
                                if prior[i]:
                                    eng.wait_ge(sems[semname[i]], prior[i])
                                fn(eng, sems[semname[i]])
                            else:
                                fn(eng).then_inc(sems[semname[i]], 1)
                    return body
                getattr(blk, e)(make(e, mine))
        return sems, done


def alloc_common(nc, es, sch):
    cm = {}
    sb = lambda n, s, dt: es.enter_context(nc.sbuf_tensor(n, s, dt))
    cm["iot"] = sb("iot", [128, 128], I32)
    cm["idm"] = sb("idm", [128, 128], BF16)
    cm["m_incl"] = sb("m_incl", [128, 128], BF16)
    cm["m_str"] = sb("m_str", [128, 128], BF16)
    cm["m_strn"] = sb("m_strn", [128, 128], BF16)
    cm["zero128"] = sb("zero128", [128, 128], F32)
    cm["ones64"] = sb("ones64", [128, 64], BF16)
    cm["cpool"] = sb("cpool", [128, 4], F32)
    cm["ps_bf"] = es.enter_context(nc.psum_tensor("ps_bf", [128, 1024], BF16))
    cm["ps_a"] = es.enter_context(nc.psum_tensor("ps_a", [128, 512], F32))
    cm["ps_b"] = es.enter_context(nc.psum_tensor("ps_b", [128, 1024], F32))
    cm["ps_c"] = es.enter_context(nc.psum_tensor("ps_c", [128, 512], F32))
    cm["ps_d"] = es.enter_context(nc.psum_tensor("ps_d", [128, 512], F32))
    cm["ps_e"] = es.enter_context(nc.psum_tensor("ps_e", [128, 512], F32))
    cm["ps_f"] = es.enter_context(nc.psum_tensor("ps_f", [128, 512], F32))
    for i, cv in enumerate(CONSTS):
        nc.const_aps.register(F32, cv, cm["cpool"][:, i:i + 1])

    def c_gp(g):
        for i, cv in enumerate(CONSTS):
            g.memset(cm["cpool"][:, i:i + 1], cv)
        g.memset(cm["zero128"][:, :], 0.0)
        g.memset(cm["ones64"][:, :], 1.0)
        return g.iota(cm["iot"][:, :], [[1, 128]], channel_multiplier=-1)
    s0 = sch.step("gpsimd", c_gp)

    def c_v(v):
        v.tensor_scalar(out=cm["idm"][:, :], in0=cm["iot"][:, :], scalar1=0,
                        scalar2=None, op0=ALU.is_equal)
        v.tensor_scalar(out=cm["m_incl"][:, :], in0=cm["iot"][:, :], scalar1=0,
                        scalar2=None, op0=ALU.is_ge)
        v.tensor_scalar(out=cm["m_str"][:, :], in0=cm["iot"][:, :], scalar1=0,
                        scalar2=None, op0=ALU.is_gt)
        return v.tensor_scalar(out=cm["m_strn"][:, :], in0=cm["iot"][:, :],
                               scalar1=0, scalar2=-1.0, op0=ALU.is_gt,
                               op1=ALU.mult)
    cm["ready"] = sch.step("vector", c_v, deps=[s0])
    cm["ptrtok"] = [cm["ready"]]
    cm["tmpa"] = sb("utmp_a", [128, 512], F32)
    cm["tmpb"] = sb("utmp_b", [128, 512], F32)
    return cm


def emit_xT(nc, es, sch, cm, x_d, xT):
    xg = es.enter_context(nc.sbuf_tensor("xg", [128, 4, DM], BF16))
    ptr = cm["ps_bf"]
    last_evac = cm["ready"]
    for g in range(TC // 512):
        def ldf(gp, sem, g=g):
            gp.dma_start(
                out=xg[:, :, :],
                in_=x_d[g * 512:(g + 1) * 512, :]
                .rearrange("(a p) f -> p a f", p=128)).then_inc(sem, 16)
        ld = sch.dma("gpsimd", ldf, 1, deps=[last_evac])
        for tc4 in range(4):
            def tf(t, tc4=tc4):
                for fc in range(FC):
                    last = t.transpose(ptr[:, fc * 128:(fc + 1) * 128],
                                       xg[:, tc4, fc * 128:(fc + 1) * 128],
                                       cm["idm"][:, :])
                return last
            ts = sch.step("tensor", tf,
                          deps=[ld, cm["ready"], cm["ptrtok"][0]])

            def vf(v, g=g, tc4=tc4):
                t0 = g * 512 + tc4 * 128
                return v.tensor_copy(
                    out=xT[:, :, t0:t0 + 128],
                    in_=ptr[:, :].rearrange("p (a b) -> p a b", b=128))
            last_evac = sch.step("vector", vf, deps=[ts])
            cm["ptrtok"][0] = last_evac
    return last_evac


def emit_projections(nc, es, sch, cm, xT, xT_done, wd, outs, which=("q", "k", "gK", "gC", "v")):
    Wbufs = [es.enter_context(nc.sbuf_tensor("Wp0", [128, FC, DM], BF16)),
             es.enter_context(nc.sbuf_tensor("Wp1", [128, FC, DM], BF16))]
    stgs = [es.enter_context(nc.sbuf_tensor("stgp0", [128, TC], BF16)),
            es.enter_context(nc.sbuf_tensor("stgp1", [128, TC], BF16))]
    stg16 = es.enter_context(nc.sbuf_tensor("stg16", [128, TC], FP16))
    spsA = [es.enter_context(nc.sbuf_tensor("spsp0", [128, 512], F32)),
            es.enter_context(nc.sbuf_tensor("spsp1", [128, 512], F32))]
    spfA = [es.enter_context(nc.sbuf_tensor("spfp0", [128, 512], F32)),
            es.enter_context(nc.sbuf_tensor("spfp1", [128, 512], F32))]
    pps = [cm["ps_a"], cm["ps_c"], cm["ps_d"], cm["ps_e"]]

    bgn8 = es.enter_context(nc.sbuf_tensor("bgn8", [128, 16], F32))

    def bg_ld(sy, sem):
        with nc.allow_non_contiguous_dma(reason="tiny bias gather"):
            sy.dma_start(out=bgn8[:, 0:8],
                         in_=wd["bg_K"].rearrange("o (a p) -> p (o a)", p=128)
                         ).then_inc(sem, 16)
            sy.dma_start(out=bgn8[:, 8:16],
                         in_=wd["bg_C"].rearrange("o (a p) -> p (o a)", p=128)
                         ).then_inc(sem, 16)
    bgl = sch.dma("sync", bg_ld, 2)
    bgneg = sch.step("vector", lambda v: v.tensor_scalar_mul(
        out=bgn8[:, :], in0=bgn8[:, :], scalar1=-1.0), deps=[bgl])

    NBANK = len(pps)
    evhist = [None] * NBANK
    sphist = [None, None]
    prev_dma = {}
    lastmm = []
    gi = [0, 0]
    finals = []

    def emit_one(mi, Wname, out_dram, kind, bg_off=None):
        Wp = Wbufs[mi % 2]

        def wldf(gp, sem, Wname=Wname, Wp=Wp):
            gp.dma_start(out=Wp[:, :, :],
                         in_=wd[Wname].rearrange("(a p) c -> p a c", p=128)
                         ).then_inc(sem, 16)
        wld = sch.dma("gpsimd", wldf, 1,
                      deps=([lastmm[mi - 2]] if mi >= 2 else []))
        fin = []
        mm = None
        if kind in ("fm", "gate"):
            for oc in range(FC):
                sbuf = ("stg%d" % (oc % 2)) if kind == "fm" else "stg16"
                stg = stgs[oc % 2]
                mm_steps = []
                for tch in range(4):
                    par = gi[0] % NBANK
                    pp = pps[par]
                    gi[0] += 1
                    def mf(t, oc=oc, tch=tch, pp=pp, Wp=Wp):
                        for fk in range(FC):
                            last_i = t.matmul(
                                pp[:, :], Wp[:, fk, oc * 128:(oc + 1) * 128],
                                xT[:, fk, tch * 512:(tch + 1) * 512],
                                start=(fk == 0), stop=(fk == FC - 1))
                        return last_i
                    deps = [wld, xT_done]
                    if evhist[par] is not None:
                        deps.append(evhist[par])
                    mm = sch.step("tensor", mf, deps=deps)
                    if kind == "fm":
                        def ef(s, tch=tch, pp=pp, stg=stg):
                            return s.copy(out=stg[:, tch * 512:(tch + 1) * 512],
                                          in_=pp[:, :])
                        ev = sch.step("scalar", ef, deps=[mm] +
                                      ([prev_dma[sbuf]] if prev_dma.get(sbuf)
                                       else []))
                        evhist[par] = ev
                    else:
                        sp = gi[1] % 2
                        gi[1] += 1

                        def e1(s, bg_off=bg_off, oc=oc, pp=pp, sp=sp):
                            s.drain()
                            s.activation(out=spsA[sp][:, :], in_=pp[:, :],
                                         func=AF.Exp, scale=-1.0,
                                         bias=bgn8[:, bg_off + oc:bg_off + oc + 1])
                            s.drain()
                            return s.activation(out=spfA[sp][:, :],
                                                in_=spsA[sp][:, :],
                                                func=AF.Ln, bias=1.0)
                        sv = sch.step("scalar", e1, deps=[mm, bgneg] +
                                      ([sphist[sp]] if sphist[sp] else []))

                        def e2(v, tch=tch, sp=sp):
                            for c4 in range(4):
                                sl = slice(c4 * 128, (c4 + 1) * 128)
                                v.tensor_tensor_scan(
                                    out=spsA[sp][:, sl], data0=spfA[sp][:, sl],
                                    data1=cm["zero128"][:, :], initial=0.0,
                                    op0=ALU.add, op1=ALU.add)
                            v.drain()
                            return v.tensor_copy(
                                out=stg16[:, tch * 512:(tch + 1) * 512],
                                in_=spsA[sp][:, :])
                        ev = sch.step("vector", e2, deps=[sv] +
                                      ([prev_dma[sbuf]] if prev_dma.get(sbuf)
                                       else []))
                        evhist[par] = ev
                        sphist[sp] = ev
                    mm_steps.append((mm, ev))
                evs = [ev for _, ev in mm_steps]

                def df(sy, sem, oc=oc, kind=kind, stg=stg):
                    src = stg if kind == "fm" else stg16
                    for c in range(NCH):
                        sy.dma_start(
                            out=out_dram[c, :, oc * L:(oc + 1) * L],
                            in_=src[:, c * L:(c + 1) * L]).then_inc(sem, 16)
                dmas = sch.dma("sync", df, NCH, deps=evs)
                prev_dma[sbuf] = dmas
                fin.append(dmas)
        else:
            for tch in range(NCH):
                sbuf = "stg%d" % (tch % 2)
                stg = stgs[tch % 2]
                prev_ev = None
                for n2 in range(2):
                    par = gi[0] % NBANK
                    pp = pps[par]
                    gi[0] += 1
                    def mf2(t, tch=tch, n2=n2, pp=pp, Wp=Wp):
                        for fk in range(FC):
                            last_i = t.matmul(
                                pp[:, :],
                                xT[:, fk, tch * 128:(tch + 1) * 128],
                                Wp[:, fk, n2 * 512:(n2 + 1) * 512],
                                start=(fk == 0), stop=(fk == FC - 1))
                        return last_i
                    deps = [wld, xT_done]
                    if evhist[par] is not None:
                        deps.append(evhist[par])
                    mm = sch.step("tensor", mf2, deps=deps)

                    def ef(s, n2=n2, pp=pp, stg=stg):
                        return s.copy(out=stg[:, n2 * 512:(n2 + 1) * 512],
                                      in_=pp[:, :])
                    prev_ev = sch.step("scalar", ef, deps=[mm] +
                                       ([prev_dma[sbuf]] if prev_dma.get(sbuf)
                                        else []))
                    evhist[par] = prev_ev

                def df(sy, sem, tch=tch, stg=stg):
                    sy.dma_start(out=out_dram[tch * 128:(tch + 1) * 128, :],
                                 in_=stg[:, 0:DM]).then_inc(sem, 16)
                prev_dma[sbuf] = sch.dma("sync", df, 1, deps=[prev_ev])
                fin.append(prev_dma[sbuf])
        lastmm.append(mm)
        finals.append(fin[-1])

    mats = [("q", "W_q", outs["q_s"], "fm", None),
            ("k", "W_k", outs["k_s"], "fm", None),
            ("gK", "Wg_K", outs["spK_s"], "gate", 0),
            ("gC", "Wg_C", outs["spC_s"], "gate", 8),
            ("v", "W_v", outs["v_s"], "tm", None)]
    mats = [m for m in mats if m[0] in which]
    for mi, (_, wn, od, kd, bo) in enumerate(mats):
        emit_one(mi, wn, od, kd, bg_off=bo)
    return finals


class ChunkBufs:
    def __init__(self, nc, es, tag, with_sweep2, share_prep=None):
        sb = lambda n, s, dt: es.enter_context(nc.sbuf_tensor(n + tag, s, dt))
        if share_prep is not None:
            for f in ("qf", "kf", "sK", "sC", "Pinv", "Rg", "Rinv"):
                setattr(self, f, getattr(share_prep, f))
        else:
            self.qf = sb("qf", [128, FC, L], BF16)
            self.kf = sb("kf", [128, FC, L], BF16)
            self.sK = sb("sK", [128, FC, L], FP16)
            self.sC = sb("sC", [128, FC, L], FP16)
            self.Pinv = sb("Pinv", [128, DM], BF16)
            self.Rg = sb("Rg", [128, DM], BF16)
            self.Rinv = sb("Rinv", [128, DM], BF16)
        self.vt = sb("vt", [128, DM], BF16)
        self.Ut = sb("Ut", [128, DM], BF16)
        self.Qh = sb("Qh", [128, DM], BF16)
        self.Kb = sb("Kb", [128, DM], BF16)
        self.Utm = sb("Utm", [128, DM], BF16)
        self.Qhtm = sb("Qhtm", [128, DM], BF16)
        self.plK = sb("plK", [128, 8], F32)
        self.rlE = sb("rlE", [128, 8], F32)
        self.plbc_r = sb("plbc_r", [128, 512], BF16)
        self.plbc = sb("plbc", [128, 512], F32)
        if with_sweep2:
            self.P = sb("P", [128, DM], BF16)
            self.PRg = sb("PRg", [128, DM], BF16)
            self.Qb = sb("Qb", [128, DM], BF16)
            self.prg16 = sb("prg16", [128, DM], FP16)
        else:
            self.Kbtm = sb("Kbtm", [128, DM], BF16)


def emit_chunk_prep(nc, sch, cm, cb, srcs, c, sweep2, extra_dep=None):
    q_s, k_s, v_s, spK_s, spC_s = srcs

    def ldf(sy, sem, c=c):
        sy.dma_start(out=cb.qf[:, :, :].rearrange("p a b -> p (a b)"),
                     in_=q_s[c, :, :]).then_inc(sem, 16)
        sy.dma_start(out=cb.kf[:, :, :].rearrange("p a b -> p (a b)"),
                     in_=k_s[c, :, :]).then_inc(sem, 16)
        sy.dma_start(out=cb.vt[:, :],
                     in_=v_s[c * L:(c + 1) * L, :]).then_inc(sem, 16)
        sy.dma_start(out=cb.sK[:, :, :].rearrange("p a b -> p (a b)"),
                     in_=spK_s[c, :, :]).then_inc(sem, 16)
        sy.dma_start(out=cb.sC[:, :, :].rearrange("p a b -> p (a b)"),
                     in_=spC_s[c, :, :]).then_inc(sem, 16)
    deps = [] if extra_dep is None else list(extra_dep)
    ld = sch.dma("sync", ldf, 5, deps=deps)

    fl = lambda b: b[:, :, :].rearrange("p a b -> p (a b)")

    def expf(s):
        s.activation(out=cb.Pinv[:, :], in_=fl(cb.sK), func=AF.Exp)
        s.activation(out=cb.Rg[:, :], in_=fl(cb.sC), func=AF.Exp, scale=-1.0)
        s.activation(out=cb.Rinv[:, :], in_=fl(cb.sC), func=AF.Exp)
        s.activation(out=cb.plK[:, :], in_=cb.sK[:, :, L - 1:L].squeeze(2),
                     func=AF.Exp, scale=-1.0)
        last = s.activation(out=cb.rlE[:, :], in_=cb.sC[:, :, L - 1:L].squeeze(2),
                            func=AF.Exp, scale=-1.0)
        if sweep2:
            s.activation(out=cb.P[:, :], in_=fl(cb.sK), func=AF.Exp, scale=-1.0)
            last = s.activation(out=cb.PRg[:, :], in_=cb.prg16[:, :],
                                func=AF.Exp, scale=-1.0)
        return last
    if sweep2:
        def addsp(v):
            return v.tensor_add(out=cb.prg16[:, :], in0=fl(cb.sK), in1=fl(cb.sC))
        sp_add = sch.step("vector", addsp, deps=[ld])
        exps = sch.step("scalar", expf, deps=[ld, sp_add])
    else:
        exps = sch.step("scalar", expf, deps=[ld])

    def gatef(v):
        v.tensor_mul(out=cb.Ut[:, :], in0=fl(cb.kf), in1=cb.Pinv[:, :])
        v.tensor_mul(out=cb.Qh[:, :], in0=fl(cb.qf), in1=cb.Rinv[:, :])
        v.tensor_mul(out=cb.Kb[:, :], in0=fl(cb.kf), in1=cb.Rg[:, :])
        if sweep2:
            v.tensor_mul(out=cb.Qb[:, :], in0=fl(cb.qf), in1=cb.P[:, :])
        v.tensor_tensor(
            out=cb.plbc_r[0:64, :].rearrange("p (j k) -> p j k", k=64),
            in0=cb.plK[0:64, :].unsqueeze(2).broadcast_to([64, 8, 64]),
            in1=cm["idm"][0:64, 0:64].unsqueeze(1).broadcast_to([64, 8, 64]),
            op=ALU.mult)
        return v.tensor_tensor(
            out=cb.plbc_r[64:128, :].rearrange("p (j k) -> p j k", k=64),
            in0=cb.plK[64:128, :].unsqueeze(2).broadcast_to([64, 8, 64]),
            in1=cm["idm"][64:128, 64:128].unsqueeze(1).broadcast_to([64, 8, 64]),
            op=ALU.mult)
    gates = sch.step("vector", gatef, deps=[exps, cm["ready"]])
    return {"ld": ld, "exps": exps, "gates": gates}


MODULE_DOC_END = True


def rowbc(col8):
    return col8[:, :].unsqueeze(2).broadcast_to([128, 8, 64])


def emit_plbcast(nc, sch, cm, cb, pB, dep_gates):
    def mmf(t):
        t.matmul(pB[0:64, :], cm["ones64"][0:64, :], cb.plbc_r[0:64, :],
                 start=True, stop=True)
        return t.matmul(pB[64:128, :], cm["ones64"][64:128, :],
                        cb.plbc_r[64:128, :], start=True, stop=True)
    mm = sch.step("tensor", mmf, deps=dep_gates)
    ev = sch.step("vector", lambda v: v.tensor_copy(out=cb.plbc[:, :],
                                                    in_=pB[:, :]), deps=[mm])
    return ev


def emit_sweep1(nc, es, sch, cm, srcs, maps_o, start_dep, lvl=9, nch=NCH, co=None):
    sb = lambda n, s, dt: es.enter_context(nc.sbuf_tensor(n, s, dt))
    cbs = [ChunkBufs(nc, es, "_s1a", with_sweep2=False)]
    cbs.append(ChunkBufs(nc, es, "_s1b", with_sweep2=False,
                         share_prep=cbs[0]))
    LSm = sb("LSm", [128, 512], F32)
    LCm = sb("LCm", [128, 512], F32)
    Xtm = sb("Xtm", [128, 512], F32)
    LGm = sb("LGm", [128, 512], F32)
    Dm = sb("Dm", [128, 8], F32)
    Em = sb("Em", [128, 8], F32)
    LC_b = sb("LC_b", [128, 512], BF16)
    Yt_b = sb("Yt_b", [128, 512], BF16)
    Wl_b = sb("Wl_b", [128, DM], BF16)
    s3ms = [sb("s3m_s1a", [128, 256], BF16), sb("s3m_s1b", [128, 256], BF16)]

    ptr = cm["ps_bf"]
    ps3es = [cm["ps_a"][:, 0:128], cm["ps_a"][:, 128:256]]
    ps3os = [cm["ps_b"][:, 512:640], cm["ps_b"][:, 640:768]]
    pWs = [cm["ps_b"][:, 0:128], cm["ps_b"][:, 128:256]]
    pUU = cm["ps_c"]
    pQV = cm["ps_d"]
    pG = cm["ps_e"]
    pB = cm["ps_f"]

    def init(g):
        for b in (LSm, LCm, Xtm, LGm):
            g.memset(b[:, :], 0.0)
        g.memset(LC_b[:, :], 0.0)
        g.memset(Dm[:, :], 1.0)
        return g.memset(Em[:, :], 1.0)
    upd = sch.step("gpsimd", init,
                   deps=(start_dep if isinstance(start_dep, list)
                         else [start_dep]))

    upd_hist = [upd, upd]
    ev3_h = [None, None]
    wl_h = [None, None]
    ew_h = [None, None]

    last_gates = None
    for c in range(nch if lvl >= 1 else 0):
        cb = cbs[c % 2]
        pr = emit_chunk_prep(nc, sch, cm, cb, srcs, c, sweep2=False,
                             extra_dep=[upd_hist[c % 2]] +
                             ([last_gates] if last_gates else []))
        last_gates = pr["gates"]
        if lvl < 2:
            upd = pr["gates"]
            upd_hist[c % 2] = upd
            continue
        tms = {}
        for name, src, dst in (("Ut", cb.Ut, cb.Utm), ("Qh", cb.Qh, cb.Qhtm),
                               ("Kb", cb.Kb, cb.Kbtm)):
            def tf(t, src=src):
                for j in range(NP):
                    last = t.transpose(ptr[:, j * 128:(j + 1) * 128],
                                       src[:, j * 128:(j + 1) * 128],
                                       cm["idm"][:, :])
                return last
            ts = sch.step("tensor", tf, deps=[pr["gates"], cm["ptrtok"][0]])

            def vf(s, dst=dst):
                return s.copy(out=dst[:, :], in_=ptr[:, :])
            prev_ev = sch.step("scalar", vf, deps=[ts])
            cm["ptrtok"][0] = prev_ev
            tms[name] = prev_ev
        plb = emit_plbcast(nc, sch, cm, cb, pB, [pr["gates"], upd])
        if lvl < 3:
            upd = plb
            upd_hist[c % 2] = upd
            continue

        wl_steps, yt_evs, uu_steps, qv_steps, gg_steps = [], [], [], [], []
        prev_yt = plb
        for j in range(NP):
            he, ho = 2 * j, 2 * j + 1
            pj = slice(j * 128, (j + 1) * 128)
            jp = j % 2
            ps3e, ps3o, pW, s3m = ps3es[jp], ps3os[jp], pWs[jp], s3ms[jp]

            def s3f(t, j=j, pj=pj, cb=cb, ps3e=ps3e, ps3o=ps3o):
                t.matmul(ps3e[:, :], cb.Qh[0:64, pj], cb.Kb[0:64, pj],
                         start=True, stop=True)
                return t.matmul(ps3o[:, :], cb.Qh[64:128, pj],
                                cb.Kb[64:128, pj], start=True, stop=True)
            deps = [pr["gates"]] + ([ev3_h[jp]] if ev3_h[jp] else [])
            s3 = sch.step("tensor", s3f, deps=deps)
            del deps

            def ev3f(v, ps3e=ps3e, ps3o=ps3o, s3m=s3m):
                v.tensor_tensor(out=s3m[:, 0:128], in0=ps3e[:, :],
                                in1=cm["m_str"][:, :], op=ALU.mult)
                return v.tensor_tensor(out=s3m[:, 128:256], in0=ps3o[:, :],
                                       in1=cm["m_str"][:, :], op=ALU.mult)
            ev3_deps = [s3] + ([wl_h[jp]] if wl_h[jp] else [])
            ev3 = sch.step("vector", ev3f, deps=ev3_deps)
            ev3_h[jp] = ev3
            if lvl < 4:
                continue

            def wlf(t, j=j, pj=pj, cb=cb, pW=pW, s3m=s3m):
                for hh, hb in ((2 * j, 0), (2 * j + 1, 64)):
                    t.matmul(pW[:, (hh % 2) * 64:(hh % 2) * 64 + 64],
                             cb.Kb[hb:hb + 64, pj],
                             LC_b[hb:hb + 64, j * 64:(j + 1) * 64],
                             start=True, stop=False)
                    last = t.matmul(pW[:, (hh % 2) * 64:(hh % 2) * 64 + 64],
                                    s3m[:, (hb // 64) * 128:(hb // 64) * 128 + 128],
                                    cb.vt[:, hh * 64:(hh + 1) * 64],
                                    start=False, stop=True)
                return last
            wl = sch.step("tensor", wlf,
                          deps=[ev3, upd] + ([ew_h[jp]] if ew_h[jp] else []))
            wl_steps.append(wl)
            wl_h[jp] = wl

            def ewf(s, j=j, pW=pW):
                return s.copy(out=Wl_b[:, j * 128:(j + 1) * 128],
                              in_=pW[:, :])
            ew = sch.step("scalar", ewf, deps=[wl])
            ew_h[jp] = ew

            def ytf(t, j=j, cb=cb):
                for hh, hb in ((2 * j, 0), (2 * j + 1, 64)):
                    last = t.matmul(pB[hb:hb + 64, j * 64:(j + 1) * 64],
                                    cb.Kbtm[:, hh * 64:(hh + 1) * 64],
                                    cb.Utm[:, hh * 64:(hh + 1) * 64],
                                    start=True, stop=True)
                return last
            yt = sch.step("tensor", ytf, deps=[tms["Ut"], tms["Kb"], prev_yt])

            def eyf(v, j=j):
                return v.tensor_copy(out=Yt_b[:, j * 64:(j + 1) * 64],
                                     in_=pB[:, j * 64:(j + 1) * 64])
            ey = sch.step("vector", eyf, deps=[yt])
            yt_evs.append(ey)
            prev_yt = ey

            def uuf(t, j=j, cb=cb):
                for hh, hb in ((2 * j, 0), (2 * j + 1, 64)):
                    last = t.matmul(pUU[hb:hb + 64, j * 64:(j + 1) * 64],
                                    cb.Utm[:, hh * 64:(hh + 1) * 64],
                                    cb.Utm[:, hh * 64:(hh + 1) * 64],
                                    start=True, stop=True)
                return last
            uu_steps.append(sch.step("tensor", uuf, deps=[tms["Ut"], upd]))

            def qvf(t, j=j, cb=cb):
                for hh, hb in ((2 * j, 0), (2 * j + 1, 64)):
                    last = t.matmul(pQV[hb:hb + 64, j * 64:(j + 1) * 64],
                                    cb.Qhtm[:, hh * 64:(hh + 1) * 64],
                                    cb.vt[:, hh * 64:(hh + 1) * 64],
                                    start=True, stop=True)
                return last
            qv_steps.append(sch.step("tensor", qvf, deps=[tms["Qh"], upd]))

            def ggf(t, j=j, cb=cb):
                for hh, hb in ((2 * j, 0), (2 * j + 1, 64)):
                    last = t.matmul(pG[hb:hb + 64, j * 64:(j + 1) * 64],
                                    cb.Utm[:, hh * 64:(hh + 1) * 64],
                                    Wl_b[:, hh * 64:(hh + 1) * 64],
                                    start=True, stop=True)
                return last
            gg_steps.append(sch.step("tensor", ggf, deps=[ew, tms["Ut"], upd]))

        if lvl < 4:
            upd = ev3_h[(NP - 1) % 2]
            upd_hist[c % 2] = upd
            continue
        if lvl < 5:
            upd = sch.step("vector",
                           lambda v: v.tensor_copy(out=cm["tmpa"][:, :],
                                                   in_=pUU[:, :]),
                           deps=uu_steps + qv_steps + gg_steps + yt_evs)
            upd_hist[c % 2] = upd
            continue
        def updf(v, cb=cb):
            r8 = lambda x: x[:, :].rearrange("p (a b) -> p a b", b=64)
            v.tensor_tensor(out=cm["tmpa"][:, :], in0=pG[:, :], in1=LGm[:, :],
                            op=ALU.add)
            v.drain()
            v.tensor_tensor(out=r8(LGm), in0=r8(cm["tmpa"]), in1=rowbc(cb.plK),
                            op=ALU.mult)
            v.drain()
            v.tensor_tensor(out=r8(cm["tmpa"]), in0=r8(Yt_b), in1=rowbc(Em),
                            op=ALU.mult)
            v.drain()
            v.tensor_tensor(out=cm["tmpb"][:, :], in0=Xtm[:, :], in1=cm["tmpa"][:, :],
                            op=ALU.add)
            v.drain()
            v.tensor_tensor(out=Xtm[:, :], in0=cm["tmpb"][:, :], in1=cb.plbc[:, :],
                            op=ALU.mult)
            v.drain()
            v.tensor_tensor(out=cm["tmpa"][:, :], in0=pUU[:, :], in1=LSm[:, :],
                            op=ALU.add)
            v.drain()
            v.tensor_tensor(out=r8(cm["tmpb"]), in0=r8(cm["tmpa"]), in1=rowbc(cb.plK),
                            op=ALU.mult)
            v.drain()
            v.tensor_tensor(out=LSm[:, :], in0=cm["tmpb"][:, :], in1=cb.plbc[:, :],
                            op=ALU.mult)
            v.drain()
            v.tensor_tensor(out=cm["tmpa"][:, :], in0=pQV[:, :], in1=LCm[:, :],
                            op=ALU.add)
            v.drain()
            v.tensor_tensor(out=r8(LCm), in0=r8(cm["tmpa"]), in1=rowbc(cb.rlE),
                            op=ALU.mult)
            v.tensor_mul(out=Dm[:, :], in0=Dm[:, :], in1=cb.plK[:, :])
            v.tensor_mul(out=Em[:, :], in0=Em[:, :], in1=cb.rlE[:, :])
            v.drain()
            return v.tensor_copy(out=LC_b[:, :], in_=LCm[:, :])
        upd = sch.step("vector", updf,
                       deps=gg_steps + uu_steps + qv_steps + yt_evs + [plb])
        upd_hist[c % 2] = upd
        if co:
            co.pop(0)()

    while co:
        co.pop(0)()

    def mo(sy, sem):
        sy.dma_start(out=maps_o[:, 0:512], in_=LSm[:, :]).then_inc(sem, 16)
        sy.dma_start(out=maps_o[:, 512:1024], in_=LCm[:, :]).then_inc(sem, 16)
        sy.dma_start(out=maps_o[:, 1024:1536], in_=Xtm[:, :]).then_inc(sem, 16)
        sy.dma_start(out=maps_o[:, 1536:2048], in_=LGm[:, :]).then_inc(sem, 16)
        sy.dma_start(out=maps_o[:, 2048:2056], in_=Dm[:, :]).then_inc(sem, 16)
        sy.dma_start(out=maps_o[:, 2056:2064], in_=Em[:, :]).then_inc(sem, 16)
    fin = sch.dma("sync", mo, 6, deps=[upd])
    sch.step("gpsimd", lambda g: g.engine_nop() if hasattr(g, "engine_nop")
             else g.memset(cm["cpool"][:, 3:4], 0.0), deps=[fin])
    return fin


def build_phase1(stages=("xt", "proj", "sweep1"), s1_lvl=9, s1_nch=NCH):
    nc = bass.Bass("TRN2", target_bir_lowering=False, debug=False)
    sch = Sched(nc)
    x_d = nc.dram_tensor("x", [TC, DM], F32, kind="ExternalInput").ap()
    wd = {n: nc.dram_tensor(n, [DM, DM], F32, kind="ExternalInput").ap()
          for n in ("W_q", "W_k", "W_v", "Wg_K", "Wg_C")}
    wd["bg_K"] = nc.dram_tensor("bg_K", [1, DM], F32, kind="ExternalInput").ap()
    wd["bg_C"] = nc.dram_tensor("bg_C", [1, DM], F32, kind="ExternalInput").ap()

    q_o = nc.dram_tensor("q_s", [NCH, 128, FC * L], BF16,
                         kind="ExternalOutput").ap()
    k_o = nc.dram_tensor("k_s", [NCH, 128, FC * L], BF16,
                         kind="ExternalOutput").ap()
    v_o = nc.dram_tensor("v_s", [TC, DM], BF16, kind="ExternalOutput").ap()
    sK_o = nc.dram_tensor("spK_s", [NCH, 128, FC * L], FP16,
                          kind="ExternalOutput").ap()
    sC_o = nc.dram_tensor("spC_s", [NCH, 128, FC * L], FP16,
                          kind="ExternalOutput").ap()
    maps_o = nc.dram_tensor("maps", [128, 2064], F32, kind="ExternalOutput").ap()

    xt_dbg = nc.dram_tensor("xt_dbg", [128, FC, TC], BF16,
                            kind="ExternalOutput").ap()
    es = ExitStack()
    with es:
        cm = alloc_common(nc, es, sch)
        xT = es.enter_context(nc.sbuf_tensor("xT", [128, FC, TC], BF16))
        xt_done = emit_xT(nc, es, sch, cm, x_d, xT)
        last = xt_done
        if "proj" in stages:
            outs = {"q_s": q_o, "k_s": k_o, "spK_s": sK_o, "spC_s": sC_o,
                    "v_s": v_o}
            which = [st for st in ("q", "k", "gK", "gC", "v")
                     if "no_" + st not in stages]
            last = emit_projections(nc, es, sch, cm, xT, xt_done, wd, outs,
                                    which)
        if "sweep1" in stages:
            last = emit_sweep1(nc, es, sch, cm, (q_o, k_o, v_o, sK_o, sC_o),
                               maps_o, last, lvl=s1_lvl, nch=s1_nch)

        def xdbg(sy, sem):
            sy.dma_start(out=xt_dbg[:, :, :], in_=xT[:, :, :]).then_inc(sem, 16)
        fin = sch.dma("sync", xdbg, 1,
                      deps=(last if isinstance(last, list) else [last]) +
                      [xt_done])
        sch.step("gpsimd", lambda g: g.memset(cm["cpool"][:, 3:4], 0.0),
                 deps=[fin])
        sch.run(es)
    return nc


def emit_compose(nc, es, sch, cm, maps_d, gates_d, st, start_dep):
    sb = lambda n, s, dt: es.enter_context(nc.sbuf_tensor(n, s, dt))
    mapb = [sb("mapb0", [128, 2064], F32), sb("mapb1", [128, 2064], F32)]
    gt = sb("gt", [128, 8], F32)
    Dk = [sb("Dk0", [128, 8], F32), sb("Dk1", [128, 8], F32)]
    Ek = [sb("Ek0", [128, 8], F32), sb("Ek1", [128, 8], F32)]
    mXtb = [sb("mXtb0", [128, 512], BF16), sb("mXtb1", [128, 512], BF16)]
    C_b = sb("C_bc", [128, 512], BF16)
    dbc_r = [sb("dbc_r0", [128, 512], BF16), sb("dbc_r1", [128, 512], BF16)]
    tmpa = sb("tmpac", [128, 512], F32)
    tmpb = sb("tmpbc", [128, 512], F32)
    pXC = cm["ps_c"]
    pB = cm["ps_f"]

    def initf(g):
        for b in (st["S0f"], st["C0f"], st["Gnf"]):
            g.memset(b[:, :], 0.0)
        return g.memset(C_b[:, :], 0.0)
    init = sch.step("gpsimd", initf, deps=[start_dep])

    def gld(sy, sem):
        sy.dma_start(out=gt[:, :], in_=gates_d[:, :]).then_inc(sem, 16)
    gl = sch.dma("sync", gld, 1)

    fold_h = [init, init]
    dm_h = [init, init]
    upd = init

    def emit_mlbl(k):
        par = k % 2

        def mld(sy, sem, k=k, par=par):
            sy.dma_start(out=mapb[par][:, :],
                         in_=maps_d[k, :, :]).then_inc(sem, 16)
        ml = sch.dma("sync", mld, 1, deps=[fold_h[par]])

        def blendf(v, k=k, par=par):
            g_ap = gt[:, k:k + 1]
            v.tensor_scalar(out=Dk[par][:, :], in0=mapb[par][:, 2048:2056],
                            scalar1=1.0, scalar2=g_ap,
                            op0=ALU.subtract, op1=ALU.mult)
            v.tensor_scalar(out=Ek[par][:, :], in0=mapb[par][:, 2056:2064],
                            scalar1=1.0, scalar2=g_ap,
                            op0=ALU.subtract, op1=ALU.mult)
            v.drain()
            v.tensor_scalar_add(out=Dk[par][:, :], in0=Dk[par][:, :],
                                scalar1=1.0)
            v.tensor_scalar_add(out=Ek[par][:, :], in0=Ek[par][:, :],
                                scalar1=1.0)
            v.tensor_scalar(out=mapb[par][:, 0:2048],
                            in0=mapb[par][:, 0:2048],
                            scalar1=g_ap, scalar2=None, op0=ALU.mult)
            v.drain()
            v.tensor_copy(out=mXtb[par][:, :], in_=mapb[par][:, 1024:1536])
            v.tensor_tensor(
                out=dbc_r[par][0:64, :].rearrange("p (j k) -> p j k", k=64),
                in0=Dk[par][0:64, :].unsqueeze(2).broadcast_to([64, 8, 64]),
                in1=cm["idm"][0:64, 0:64].unsqueeze(1).broadcast_to([64, 8, 64]),
                op=ALU.mult)
            return v.tensor_tensor(
                out=dbc_r[par][64:128, :].rearrange("p (j k) -> p j k", k=64),
                in0=Dk[par][64:128, :].unsqueeze(2).broadcast_to([64, 8, 64]),
                in1=cm["idm"][64:128, 64:128].unsqueeze(1)
                .broadcast_to([64, 8, 64]),
                op=ALU.mult)
        par = k % 2
        bl = sch.step("vector", blendf,
                      deps=[ml, gl, cm["ready"], fold_h[par], dm_h[par]])
        return bl

    bl = emit_mlbl(0)
    for k in range(NCORE - 1):
        par = k % 2

        def dmxcf(t, par=par):
            t.matmul(pB[0:64, :], cm["ones64"][0:64, :], dbc_r[par][0:64, :],
                     start=True, stop=True)
            t.matmul(pB[64:128, :], cm["ones64"][64:128, :],
                     dbc_r[par][64:128, :], start=True, stop=True)
            for j in range(NP):
                for hb in (0, 64):
                    last = t.matmul(pXC[hb:hb + 64, j * 64:(j + 1) * 64],
                                    mXtb[par][hb:hb + 64, j * 64:(j + 1) * 64],
                                    C_b[hb:hb + 64, j * 64:(j + 1) * 64],
                                    start=True, stop=True)
            return last
        dmxc = sch.step("tensor", dmxcf, deps=[bl, upd])
        dm_h[par] = dmxc
        if k < NCORE - 2:
            bl = emit_mlbl(k + 1)

        def foldf(v, k=k, par=par):
            r8 = lambda x: x[:, :].rearrange("p (a b) -> p a b", b=64)
            v.tensor_tensor(out=r8(tmpa), in0=r8(st["Gnf"]), in1=rowbc(Dk[par]),
                            op=ALU.mult)
            v.drain()
            v.tensor_tensor(out=tmpb[:, :], in0=tmpa[:, :], in1=pXC[:, :],
                            op=ALU.subtract)
            v.drain()
            v.tensor_tensor(out=st["Gnf"][:, :], in0=tmpb[:, :],
                            in1=mapb[par][:, 1536:2048], op=ALU.subtract)
            v.drain()
            v.tensor_tensor(out=r8(tmpa), in0=r8(st["S0f"]), in1=rowbc(Dk[par]),
                            op=ALU.mult)
            v.drain()
            v.tensor_tensor(out=tmpb[:, :], in0=tmpa[:, :], in1=pB[:, :],
                            op=ALU.mult)
            v.drain()
            v.tensor_tensor(out=st["S0f"][:, :], in0=tmpb[:, :],
                            in1=mapb[par][:, 0:512], op=ALU.add)
            v.drain()
            v.tensor_tensor(out=r8(tmpa), in0=r8(st["C0f"]), in1=rowbc(Ek[par]),
                            op=ALU.mult)
            v.drain()
            v.tensor_tensor(out=st["C0f"][:, :], in0=tmpa[:, :],
                            in1=mapb[par][:, 512:1024], op=ALU.add)
            v.drain()
            return v.tensor_copy(out=C_b[:, :], in_=st["C0f"][:, :])
        upd = sch.step("vector", foldf, deps=[dmxc])
        fold_h[par] = upd

    def castf(v):
        v.tensor_copy(out=st["S0b"][:, :], in_=st["S0f"][:, :])
        v.tensor_copy(out=st["C0b"][:, :], in_=st["C0f"][:, :])
        v.tensor_copy(out=st["Gnb"][:, :], in_=st["Gnf"][:, :])
        return v.tensor_scalar_mul(out=st["Cnb"][:, :], in0=st["C0f"][:, :],
                                   scalar1=-1.0)
    return sch.step("vector", castf, deps=[upd])


def emit_sweep2(nc, es, sch, cm, srcs, st, OTs, start_dep):
    cbs = [ChunkBufs(nc, es, "_s2a", with_sweep2=True),
           ChunkBufs(nc, es, "_s2b", with_sweep2=True)]
    sb = lambda n, s, dt: es.enter_context(nc.sbuf_tensor(n, s, dt))
    s1m = sb("s1m_2", [128, 256], BF16)
    s2m = sb("s2m_2", [128, 256], BF16)
    s3m = sb("s3m_2", [128, 256], BF16)
    rb = sb("rb_2", [128, 128], BF16)
    Wn_b = sb("Wn_b2", [128, DM], BF16)

    ptr = cm["ps_bf"]
    S1e = cm["ps_a"][:, 0:128]
    S3e = cm["ps_a"][:, 128:256]
    S2e = cm["ps_a"][:, 256:384]
    pR = cm["ps_a"][:, 384:512]
    S1o = cm["ps_b"][:, 512:640]
    S3o = cm["ps_b"][:, 640:768]
    S2o = cm["ps_b"][:, 768:896]
    pW = cm["ps_b"][:, 0:128]
    pOT = cm["ps_b"][:, 128:256]
    pUU = cm["ps_c"]
    pQV = cm["ps_d"]
    pGn = cm["ps_e"]
    pB = cm["ps_f"]

    upd = start_dep
    upd_hist = [start_dep, start_dep]
    prev = {"eva": None, "b": None, "ewn": None, "evr": None, "s2": None,
            "ev2": None, "otug": None, "evot": None, "tmev": None, "plb": None}
    for c in range(NCH):
        par = c % 2
        cb = cbs[par]
        pr = emit_chunk_prep(nc, sch, cm, cb, srcs, c, sweep2=True,
                             extra_dep=[upd_hist[par]])
        tms = {}
        for name, src, dst in (("Ut", cb.Ut, cb.Utm), ("Qh", cb.Qh, cb.Qhtm)):
            def tf(t, src=src):
                for j in range(NP):
                    last = t.transpose(ptr[:, j * 128:(j + 1) * 128],
                                       src[:, j * 128:(j + 1) * 128],
                                       cm["idm"][:, :])
                return last
            ts = sch.step("tensor", tf, deps=[pr["gates"], cm["ptrtok"][0]])

            def vf(s, dst=dst):
                return s.copy(out=dst[:, :], in_=ptr[:, :])
            prev_ev = sch.step("scalar", vf, deps=[ts])
            cm["ptrtok"][0] = prev_ev
            tms[name] = prev_ev
        plb = emit_plbcast(nc, sch, cm, cb, pB,
                           [pr["gates"]] + ([prev["plb"]] if prev["plb"]
                                            else [upd]))
        prev["plb"] = plb

        uqg_steps, ot_evs = [], []
        for j in range(NP):
            pj = slice(j * 128, (j + 1) * 128)

            def af(t, j=j, pj=pj, cb=cb):
                t.matmul(S1e[:, :], cb.Ut[0:64, pj], cb.Qb[0:64, pj],
                         start=True, stop=True)
                t.matmul(S1o[:, :], cb.Ut[64:128, pj], cb.Qb[64:128, pj],
                         start=True, stop=True)
                t.matmul(S3e[:, :], cb.Qh[0:64, pj], cb.Kb[0:64, pj],
                         start=True, stop=True)
                return t.matmul(S3o[:, :], cb.Qh[64:128, pj],
                                cb.Kb[64:128, pj], start=True, stop=True)
            a = sch.step("tensor", af, deps=[pr["gates"]] +
                         ([prev["eva"]] if prev["eva"] else []))

            def evaf(v):
                v.tensor_tensor(out=s1m[:, 0:128], in0=S1e[:, :],
                                in1=cm["m_incl"][:, :], op=ALU.mult)
                v.tensor_tensor(out=s1m[:, 128:256], in0=S1o[:, :],
                                in1=cm["m_incl"][:, :], op=ALU.mult)
                v.tensor_tensor(out=s3m[:, 0:128], in0=S3e[:, :],
                                in1=cm["m_strn"][:, :], op=ALU.mult)
                return v.tensor_tensor(out=s3m[:, 128:256], in0=S3o[:, :],
                                       in1=cm["m_strn"][:, :], op=ALU.mult)
            eva = sch.step("vector", evaf, deps=[a] +
                           ([prev["otug"]] if prev["otug"] else []))
            prev["eva"] = eva

            def bf(t, j=j, pj=pj, cb=cb):
                for hh, hb in ((2 * j, 0), (2 * j + 1, 64)):
                    msl = slice((hb // 64) * 128, (hb // 64) * 128 + 128)
                    t.matmul(pW[:, (hh % 2) * 64:(hh % 2) * 64 + 64],
                             cb.Kb[hb:hb + 64, pj],
                             st["Cnb"][hb:hb + 64, j * 64:(j + 1) * 64],
                             start=True, stop=False)
                    t.matmul(pW[:, (hh % 2) * 64:(hh % 2) * 64 + 64],
                             s3m[:, msl], cb.vt[:, hh * 64:(hh + 1) * 64],
                             start=False, stop=True)
                    t.matmul(pR[hb:hb + 64, :],
                             st["S0b"][hb:hb + 64, j * 64:(j + 1) * 64],
                             cb.Qb[hb:hb + 64, pj], start=True, stop=False)
                    last = t.matmul(pR[hb:hb + 64, :],
                                    cb.Utm[:, hh * 64:(hh + 1) * 64],
                                    s1m[:, msl], start=False, stop=True)
                return last
            b = sch.step("tensor", bf,
                         deps=[eva, tms["Ut"], upd] +
                         ([prev["ewn"]] if prev["ewn"] else []) +
                         ([prev["evr"]] if prev["evr"] else []))

            def ewnf(s, j=j):
                return s.copy(out=Wn_b[:, j * 128:(j + 1) * 128], in_=pW[:, :])
            ewn = sch.step("scalar", ewnf, deps=[b])
            prev["ewn"] = ewn

            def evrf(v, j=j, pj=pj, cb=cb):
                return v.tensor_tensor(out=rb[:, :], in0=pR[:, :],
                                       in1=cb.PRg[:, pj], op=ALU.mult)
            evr = sch.step("vector", evrf, deps=[b, pr["gates"]] +
                           ([prev["otug"]] if prev["otug"] else []))
            prev["evr"] = evr

            def s2f(t, pj=pj, cb=cb):
                t.matmul(S2e[:, :], cb.Qh[0:64, pj], rb[0:64, :],
                         start=True, stop=True)
                return t.matmul(S2o[:, :], cb.Qh[64:128, pj], rb[64:128, :],
                                start=True, stop=True)
            s2 = sch.step("tensor", s2f, deps=[evr] +
                          ([prev["ev2"]] if prev["ev2"] else []))
            prev["s2"] = s2

            def ev2f(v):
                v.tensor_tensor(out=s2m[:, 0:128], in0=S2e[:, :],
                                in1=cm["m_incl"][:, :], op=ALU.mult)
                return v.tensor_tensor(out=s2m[:, 128:256], in0=S2o[:, :],
                                       in1=cm["m_incl"][:, :], op=ALU.mult)
            ev2 = sch.step("vector", ev2f, deps=[s2] +
                           ([prev["otug"]] if prev["otug"] else []))
            prev["ev2"] = ev2

            def otugf(t, j=j, pj=pj, cb=cb):
                for hh, hb in ((2 * j, 0), (2 * j + 1, 64)):
                    sl64 = slice(j * 64, (j + 1) * 64)
                    msl = slice((hb // 64) * 128, (hb // 64) * 128 + 128)
                    t.matmul(pOT[hb:hb + 64, :], st["C0b"][hb:hb + 64, sl64],
                             rb[hb:hb + 64, :], start=True, stop=False)
                    t.matmul(pOT[hb:hb + 64, :],
                             cb.vt[:, hh * 64:(hh + 1) * 64], s2m[:, msl],
                             start=False, stop=False)
                    t.matmul(pOT[hb:hb + 64, :], st["Gnb"][hb:hb + 64, sl64],
                             cb.Qb[hb:hb + 64, pj], start=False, stop=False)
                    t.matmul(pOT[hb:hb + 64, :],
                             Wn_b[:, hh * 64:(hh + 1) * 64], s1m[:, msl],
                             start=False, stop=True)
                    t.matmul(pUU[hb:hb + 64, j * 64:(j + 1) * 64],
                             cb.Utm[:, hh * 64:(hh + 1) * 64],
                             cb.Utm[:, hh * 64:(hh + 1) * 64],
                             start=True, stop=True)
                    t.matmul(pQV[hb:hb + 64, j * 64:(j + 1) * 64],
                             cb.Qhtm[:, hh * 64:(hh + 1) * 64],
                             cb.vt[:, hh * 64:(hh + 1) * 64],
                             start=True, stop=True)
                    last = t.matmul(pGn[hb:hb + 64, j * 64:(j + 1) * 64],
                                    cb.Utm[:, hh * 64:(hh + 1) * 64],
                                    Wn_b[:, hh * 64:(hh + 1) * 64],
                                    start=True, stop=True)
                return last
            otug = sch.step("tensor", otugf,
                            deps=[eva, ev2, ewn, evr, tms["Ut"], tms["Qh"],
                                  upd] +
                            ([prev["evot"]] if prev["evot"] else []))
            prev["otug"] = otug
            uqg_steps.append(otug)

            def evotf(v, j=j, c=c):
                return v.tensor_copy(out=OTs[:, j, c * L:(c + 1) * L],
                                     in_=pOT[:, :])
            evot = sch.step("vector", evotf, deps=[otug])
            prev["evot"] = evot
            ot_evs.append(evot)

        def updf(v, cb=cb):
            r8 = lambda x: x[:, :].rearrange("p (a b) -> p a b", b=64)
            v.tensor_tensor(out=cm["tmpa"][:, :], in0=pUU[:, :], in1=st["S0f"][:, :],
                            op=ALU.add)
            v.drain()
            v.tensor_tensor(out=r8(cm["tmpb"]), in0=r8(cm["tmpa"]), in1=rowbc(cb.plK),
                            op=ALU.mult)
            v.drain()
            v.tensor_tensor(out=st["S0f"][:, :], in0=cm["tmpb"][:, :],
                            in1=cb.plbc[:, :], op=ALU.mult)
            v.drain()
            v.tensor_tensor(out=cm["tmpa"][:, :], in0=pQV[:, :], in1=st["C0f"][:, :],
                            op=ALU.add)
            v.drain()
            v.tensor_tensor(out=r8(st["C0f"]), in0=r8(cm["tmpa"]), in1=rowbc(cb.rlE),
                            op=ALU.mult)
            v.drain()
            v.tensor_tensor(out=cm["tmpa"][:, :], in0=pGn[:, :], in1=st["Gnf"][:, :],
                            op=ALU.add)
            v.drain()
            v.tensor_tensor(out=r8(st["Gnf"]), in0=r8(cm["tmpa"]), in1=rowbc(cb.plK),
                            op=ALU.mult)
            v.drain()
            v.tensor_copy(out=st["S0b"][:, :], in_=st["S0f"][:, :])
            v.tensor_copy(out=st["C0b"][:, :], in_=st["C0f"][:, :])
            v.tensor_copy(out=st["Gnb"][:, :], in_=st["Gnf"][:, :])
            return v.tensor_scalar_mul(out=st["Cnb"][:, :], in0=st["C0f"][:, :],
                                       scalar1=-1.0)
        upd = sch.step("vector", updf, deps=uqg_steps + ot_evs + [plb, upd])
        upd_hist[par] = upd
    return upd


def emit_quant_sg(nc, sch, cm, qx, ws_f32, W, deps):
    def sgf(s):
        s.activation(out=qx["xa"][:, 0:W], in_=ws_f32[:, 0:W], func=AF.Abs)
        return s.activation(out=qx["sgb"][:, 0:W], in_=ws_f32[:, 0:W],
                            func=AF.Sign)
    return sch.step("scalar", sgf, deps=deps)


def emit_quant(nc, sch, cm, qb, ws_f32, W, out_bf16, deps, prefix=None):
    sg = emit_quant_sg(nc, sch, cm, qb, ws_f32, W, deps)
    return emit_quant_qv(nc, sch, cm, qb, qb, ws_f32, W, out_bf16,
                         list(deps) + [sg], prefix=prefix)


def emit_quant_qv(nc, sch, cm, qb, qx, ws_f32, W, out_bf16, deps,
                  prefix=None):
    NB = W // 16

    def vf(v):
        if prefix is not None:
            prefix(v)
            v.drain()
        v.tensor_reduce(out=qb["bm"][:, 0:NB],
                        in_=ws_f32[:, 0:W].rearrange("p (b w) -> p b w", w=16),
                        axis=mybir.AxisListType.X, op=ALU.max,
                        apply_absolute_value=True)
        v.drain()
        v.tensor_scalar(out=qb["sc"][:, 0:NB], in0=qb["bm"][:, 0:NB],
                        scalar1=1.0 / 6.0, scalar2=1e-12, op0=ALU.mult,
                        op1=ALU.max)
        v.drain()
        v.reciprocal(out=qb["ri"][:, 0:NB], in_=qb["sc"][:, 0:NB])
        v.drain()
        v.tensor_tensor(
            out=qb["tn"][:, 0:W].rearrange("p (b w) -> p b w", w=16),
            in0=qx["xa"][:, 0:W].rearrange("p (b w) -> p b w", w=16),
            in1=qb["ri"][:, 0:NB].unsqueeze(2).broadcast_to([128, NB, 16]),
            op=ALU.mult)
        v.drain()
        v.tensor_scalar(out=qb["cqA"][:, 0:W], in0=qb["tn"][:, 0:W],
                        scalar1=0.25, scalar2=None, op0=ALU.is_gt)
        for m in (0.75, 1.25, 1.75):
            v.drain()
            v.scalar_tensor_tensor(out=qb["cqA"][:, 0:W],
                                   in0=qb["tn"][:, 0:W], scalar=m,
                                   in1=qb["cqA"][:, 0:W],
                                   op0=ALU.is_gt, op1=ALU.add)
        v.tensor_scalar(out=qb["cqB"][:, 0:W], in0=qb["tn"][:, 0:W],
                        scalar1=2.5, scalar2=None, op0=ALU.is_gt)
        v.drain()
        v.scalar_tensor_tensor(out=qb["cqB"][:, 0:W], in0=qb["tn"][:, 0:W],
                               scalar=3.5, in1=qb["cqB"][:, 0:W],
                               op0=ALU.is_gt, op1=ALU.add)
        v.drain()
        v.scalar_tensor_tensor(out=qb["cqB"][:, 0:W], in0=qb["cqA"][:, 0:W],
                               scalar=0.5, in1=qb["cqB"][:, 0:W],
                               op0=ALU.mult, op1=ALU.add)
        v.drain()
        v.tensor_scalar(out=qb["cqA"][:, 0:W], in0=qb["tn"][:, 0:W],
                        scalar1=5.0, scalar2=2.0, op0=ALU.is_gt, op1=ALU.mult)
        v.drain()
        v.tensor_tensor(out=qb["cqA"][:, 0:W], in0=qb["cqA"][:, 0:W],
                        in1=qb["cqB"][:, 0:W], op=ALU.add)
        v.drain()
        v.tensor_tensor(out=qb["cqA"][:, 0:W], in0=qb["cqA"][:, 0:W],
                        in1=qx["sgb"][:, 0:W], op=ALU.mult)
        v.drain()
        return v.tensor_tensor(
            out=out_bf16[:, 0:W].rearrange("p (b w) -> p b w", w=16),
            in0=qb["cqA"][:, 0:W].rearrange("p (b w) -> p b w", w=16),
            in1=qb["sc"][:, 0:NB].unsqueeze(2).broadcast_to([128, NB, 16]),
            op=ALU.mult)
    return sch.step("vector", vf, deps=list(deps))


def alloc_qb(nc, es, tag, W):
    sb = lambda n, s, dt: es.enter_context(nc.sbuf_tensor(n + tag, s, dt))
    return {"xa": sb("q_xa", [128, W], F32), "bm": sb("q_bm", [128, W // 16], F32),
            "sc": sb("q_sc", [128, W // 16], F32),
            "ri": sb("q_ri", [128, W // 16], F32),
            "tn": sb("q_tn", [128, W], F32), "cqA": sb("q_cqA", [128, W], BF16),
            "cqB": sb("q_cqB", [128, W], BF16), "sgb": sb("q_sgb", [128, W], BF16)}


def emit_ln(nc, sch, cm, lb, psum_in, xres, out_f32, deps, sq_war=()):
    def v1(v):
        if isinstance(psum_in, (list, tuple)):
            v.tensor_tensor(out=lb["uf"][:, 0:512], in0=psum_in[0][:, :],
                            in1=xres[:, 0:512], op=ALU.add)
            v.tensor_tensor(out=lb["uf"][:, 512:1024], in0=psum_in[1][:, :],
                            in1=xres[:, 512:1024], op=ALU.add)
        else:
            v.tensor_tensor(out=lb["uf"][:, :], in0=psum_in[:, :],
                            in1=xres[:, :], op=ALU.add)
        v.drain()
        return v.reduce_sum(out=lb["st4"][:, 0:1], in_=lb["uf"][:, :],
                            axis=mybir.AxisListType.X)
    s1 = sch.step("vector", v1, deps=deps)

    def sq(s):
        return s.activation(out=lb["sq"][:, :], in_=lb["uf"][:, :],
                            func=AF.Square)
    s2 = sch.step("scalar", sq, deps=[s1] + list(sq_war))

    def v2(v):
        v.reduce_sum(out=lb["st4"][:, 1:2], in_=lb["sq"][:, :],
                     axis=mybir.AxisListType.X)
        v.drain()
        v.tensor_scalar_mul(out=lb["st4"][:, 0:2], in0=lb["st4"][:, 0:2],
                            scalar1=1.0 / DM)
        v.drain()
        v.tensor_tensor(out=lb["st4"][:, 2:3], in0=lb["st4"][:, 0:1],
                        in1=lb["st4"][:, 0:1], op=ALU.mult)
        v.drain()
        return v.tensor_tensor(out=lb["st4"][:, 3:4], in0=lb["st4"][:, 1:2],
                               in1=lb["st4"][:, 2:3], op=ALU.subtract)
    s3 = sch.step("vector", v2, deps=[s2])

    def s4f(s):
        s.activation(out=lb["st4"][:, 2:3], in_=lb["st4"][:, 3:4], func=AF.Ln,
                     bias=EPS)
        s.drain()
        return s.activation(out=lb["st4"][:, 3:4], in_=lb["st4"][:, 2:3],
                            func=AF.Exp, scale=-0.5)
    s4 = sch.step("scalar", s4f, deps=[s3])

    def v3(v):
        return v.tensor_scalar(out=out_f32[:, :], in0=lb["uf"][:, :],
                               scalar1=lb["st4"][:, 0:1],
                               scalar2=lb["st4"][:, 3:4],
                               op0=ALU.subtract, op1=ALU.mult)
    return sch.step("vector", v3, deps=[s4]), s1, s3




def emit_wtr(nc, sch, cm, ptr, wq, qds, evhist, tras, slabs, i):
    wdram, dst, sl, Wrow = slabs[i]
    par = i % 2
    ntile = Wrow // 128
    prev = qds[i]
    for g0 in range(0, ntile, 8):
        gn = min(8, ntile - g0)

        def trg(t, g0=g0, gn=gn, par=par):
            for it in range(gn):
                lastt = t.transpose(ptr[:, it * 128:(it + 1) * 128],
                                    wq[par][:, (g0 + it) * 128:
                                            (g0 + it + 1) * 128],
                                    cm["idm"][:, :])
            return lastt
        tg = sch.step("tensor", trg,
                      deps=[prev] + ([evhist[0]] if evhist[0] else []))

        def evg(v, g0=g0, gn=gn, dst=dst, sl=sl):
            return v.tensor_copy(
                out=dst[:, g0:g0 + gn, sl * 128:(sl + 1) * 128],
                in_=ptr[:, 0:gn * 128].rearrange("p (a b) -> p a b", b=128))
        prev = sch.step("vector", evg, deps=[tg])
        evhist[0] = prev
    tras[i] = prev


def emit_fdn(nc, sch, cm, ptr, hqc, hqT, wqdT, pDN, dn_mms, hist, ICH,
             hq, ici, i0, ilen, par, tc):
    war = lambda k, n=2: ([hist[k][-n]] if len(hist[k]) >= n else [])
    ntile = ilen // 128

    def htr(t, ntile=ntile, par=par):
        for it in range(ntile):
            lastt = t.transpose(
                ptr[:, (par * 4 + it) * 128:(par * 4 + it + 1) * 128],
                hqc[par][:, it * 128:(it + 1) * 128], cm["idm"][:, :])
        return lastt
    ht = sch.step("tensor", htr, deps=[hq] + war("hte"))
    hist["ht"].append(ht)

    def htef(s, ntile=ntile, par=par):
        return s.copy(
            out=hqT[par][:, 0:ntile, :].rearrange("p a b -> p (a b)"),
            in_=ptr[:, par * 512:par * 512 + ntile * 128])
    hte = sch.step("scalar", htef, deps=[ht] + war("dm", 1))
    hist["hte"].append(hte)

    def dmf(t, i0=i0, ntile=ntile, ici=ici, par=par, tc=tc):
        for it in range(ntile):
            git = i0 // 128 + it
            for n2 in range(2):
                last_i = t.matmul(
                    pDN[:, n2 * 512:(n2 + 1) * 512], hqT[par][:, it, :],
                    wqdT[:, git, n2 * 512:(n2 + 1) * 512],
                    start=(ici == 0 and it == 0),
                    stop=(ici == len(ICH) - 1 and it == ntile - 1))
        return last_i
    dm = sch.step("tensor", dmf, deps=[hte] + dn_mms[-1:] +
                  (hist["ln2s1"][-1:] if ici == 0 else []))
    dn_mms.append(dm)
    hist["dm"].append(dm)


def build_fused():
    nc = bass.Bass("TRN2", target_bir_lowering=False, debug=False)
    sch = Sched(nc)
    x_d = nc.dram_tensor("x", [TC, DM], F32, kind="ExternalInput").ap()
    wd = {n: nc.dram_tensor(n, [DM, DM], F32, kind="ExternalInput").ap()
          for n in ("W_q", "W_k", "W_v", "Wg_K", "Wg_C")}
    wd["bg_K"] = nc.dram_tensor("bg_K", [1, DM], F32, kind="ExternalInput").ap()
    wd["bg_C"] = nc.dram_tensor("bg_C", [1, DM], F32, kind="ExternalInput").ap()
    gates_d = nc.dram_tensor("gates", [128, 8], F32, kind="ExternalInput").ap()
    Wo_d = nc.dram_tensor("W_o", [DM, DM], F32, kind="ExternalInput").ap()
    y_d = nc.dram_tensor("y", [TC, DM], F32, kind="ExternalOutput").ap()

    q_s = nc.dram_tensor("q_s", [NCH, 128, FC * L], BF16, kind="Internal").ap()
    k_s = nc.dram_tensor("k_s", [NCH, 128, FC * L], BF16, kind="Internal").ap()
    v_s = nc.dram_tensor("v_s", [TC, DM], BF16, kind="Internal").ap()
    sK_s = nc.dram_tensor("spK_s", [NCH, 128, FC * L], FP16,
                          kind="Internal").ap()
    sC_s = nc.dram_tensor("spC_s", [NCH, 128, FC * L], FP16,
                          kind="Internal").ap()
    x1_d = nc.dram_tensor("x1_s", [TC, DM], F32, kind="Internal").ap()
    mb_in = nc.dram_tensor("mb_in", [128, 2064], F32, kind="Internal")
    mb_out = nc.dram_tensor("mb_out", [NCORE, 128, 2064], F32, kind="Internal")
    wg_my = nc.dram_tensor("wg_my", [384, DM], F32, kind="ExternalInput").ap()
    wu_my = nc.dram_tensor("wu_my", [384, DM], F32, kind="ExternalInput").ap()
    wdn_my = nc.dram_tensor("wdn_my", [128, DI], F32, kind="ExternalInput").ap()
    wqb_in = nc.dram_tensor("wqb_in", [128, 8960], BF16, kind="Internal")
    wqb_out = nc.dram_tensor("wqb_out", [NCORE, 128, 8960], BF16,
                             kind="Internal")

    es = ExitStack()
    with es:
        cm = alloc_common(nc, es, sch)

        qb = alloc_qb(nc, es, "_f", 1024)
        ws = [es.enter_context(nc.sbuf_tensor("ws0", [128, DM], F32)),
              es.enter_context(nc.sbuf_tensor("ws1", [128, DM], F32))]
        wq = [es.enter_context(nc.sbuf_tensor("wq0", [128, DM], BF16)),
              es.enter_context(nc.sbuf_tensor("wq1", [128, DM], BF16))]


        myslabs = [(wdn_my, 0, DI, 6144),
                   (wg_my, 0, DM, 0), (wu_my, 0, DM, 3072),
                   (wg_my, 1, DM, 1024), (wu_my, 1, DM, 4096),
                   (wg_my, 2, DM, 2048), (wu_my, 2, DM, 5120)]
        wctx = {"ld": [cm["ready"], cm["ready"]], "ev": [None],
                "tr": {}, "qd": [], "fins": [], "pi": 0}

        def co_piece(wdram, row, p0, plen, off):
            def go():
                par = wctx["pi"] % 2
                wctx["pi"] += 1

                def wl(sy, sem, wdram=wdram, row=row, p0=p0, plen=plen,
                       par=par):
                    sy.dma_start(out=ws[par][:, 0:plen],
                                 in_=wdram[row * 128:(row + 1) * 128,
                                           p0:p0 + plen]
                                 ).then_inc(sem, 16)
                ld = sch.dma("sync", wl, 1, deps=[wctx["ld"][par]])
                qd = emit_quant(nc, sch, cm, qb, ws[par][:, 0:plen],
                                plen, wq[par][:, 0:plen],
                                [ld] + wctx["qd"][-1:])
                wctx["qd"].append(qd)
                wctx["ld"][par] = qd
                ntile = plen // 128
                sview = wqstage[:, off + p0:off + p0 + plen].rearrange(
                    "p (a b) -> p a b", b=128)

                def trg(t, ntile=ntile, par=par):
                    for it in range(ntile):
                        lastt = t.transpose(
                            cm["ps_bf"][:, it * 128:(it + 1) * 128],
                            wq[par][:, it * 128:(it + 1) * 128],
                            cm["idm"][:, :])
                    return lastt
                tg = sch.step("tensor", trg,
                              deps=[qd, cm["ptrtok"][0]])

                def evg(v, ntile=ntile, sview=sview):
                    return v.tensor_copy(
                        out=sview[:, :, :],
                        in_=cm["ps_bf"][:, 0:ntile * 128]
                        .rearrange("p (a b) -> p a b", b=128))
                prev = sch.step("vector", evg, deps=[tg])
                cm["ptrtok"][0] = prev
                wctx["ev"][0] = prev
                wctx["fins"].append(prev)
            return go

        co = []
        for i in range(len(myslabs)):
            wdram, row, Wrow, off = myslabs[i]
            for p0 in range(0, Wrow, 512):
                plen = min(512, Wrow - p0)
                co.append(co_piece(wdram, row, p0, plen, off))

        esA = ExitStack()
        with esA:
            wqstage = esA.enter_context(
                nc.sbuf_tensor("wqstage", [128, 8960], BF16))
            xT = esA.enter_context(nc.sbuf_tensor("xT", [128, FC, TC], BF16))
            xt_done = emit_xT(nc, esA, sch, cm, x_d, xT)
            outs = {"q_s": q_s, "k_s": k_s, "spK_s": sK_s, "spC_s": sC_s,
                    "v_s": v_s}
            pj_done = emit_projections(nc, esA, sch, cm, xT, xt_done, wd, outs)
            s1_fin = emit_sweep1(nc, esA, sch, cm,
                                 (q_s, k_s, v_s, sK_s, sC_s),
                                 mb_in.ap(), pj_done, co=co)

            def wst(sy, sem):
                sy.dma_start(out=wqb_in.ap()[:, :],
                             in_=wqstage[:, :]).then_inc(sem, 16)
            wst_d = sch.dma("sync", wst, 1, deps=wctx["fins"])

        def ccf(g):
            return g.collective_compute(
                "AllGather", ALU.bypass, replica_groups=[list(range(NCORE))],
                ins=[mb_in.ap().opt()], outs=[mb_out.ap().opt()])
        cc = sch.step("gpsimd", ccf, deps=[s1_fin, wst_d], coll=True)

        def ccw(g):
            return g.collective_compute(
                "AllGather", ALU.bypass, replica_groups=[list(range(NCORE))],
                ins=[wqb_in.ap().opt()], outs=[wqb_out.ap().opt()])
        ccw_s = sch.step("gpsimd", ccw, deps=[wst_d], coll=True)

        esB = ExitStack()
        with esB:
            sb1 = lambda n, s, dt: esB.enter_context(nc.sbuf_tensor(n, s, dt))
            st = {"S0f": sb1("S0f", [128, 512], F32),
                  "C0f": sb1("C0f", [128, 512], F32),
                  "Gnf": sb1("Gnf", [128, 512], F32),
                  "S0b": sb1("S0b", [128, 512], BF16),
                  "C0b": sb1("C0b", [128, 512], BF16),
                  "Cnb": sb1("Cnb", [128, 512], BF16),
                  "Gnb": sb1("Gnb", [128, 512], BF16)}
            OTs = sb1("OTs", [128, FC, TC], BF16)
            comp = emit_compose(nc, esB, sch, cm, mb_out.ap(), gates_d, st, cc)
            s2_done = emit_sweep2(nc, esB, sch, cm, (q_s, k_s, v_s, sK_s, sC_s),
                                  st, OTs, comp)
            Wo = sb1("Wo", [128, FC, DM], BF16)
            xrs = [qb["xa"], sb1("xr1", [128, DM], F32)]
            lbs = [{"uf": sb1("ln_uf", [128, DM], F32),
                    "sq": qb["tn"],
                    "st4": sb1("ln_st4", [128, 4], F32)},
                   {"uf": sb1("ln_uf1", [128, DM], F32),
                    "sq": qb["cqB"],
                    "st4": sb1("ln_st4b", [128, 4], F32)}]
            pLNs = [cm["ps_b"],
                    [cm["ps_c"], cm["ps_d"]]]

            def wol(gp, sem):
                gp.dma_start(out=Wo[:, :, :],
                             in_=Wo_d.rearrange("(a p) c -> p a c", p=128)
                             ).then_inc(sem, 16)
            wold = sch.dma("gpsimd", wol, 1, deps=[cc])
            s1h, s3h, xsoh = [None, None], [None, None], [None, None]
            xso_all = []
            last = s2_done
            for tc in range(NCH):
                par = tc % 2
                pLN = pLNs[par]

                def xrl(sy, sem, tc=tc, par=par):
                    sy.dma_start(out=xrs[par][:, :],
                                 in_=x_d[tc * 128:(tc + 1) * 128, :]
                                 ).then_inc(sem, 16)
                xl = sch.dma("sync", xrl, 1,
                             deps=([s1h[par]] if s1h[par] else [comp]))

                def mmf(t, tc=tc, pLN=pLN, par=par):
                    for n2 in range(2):
                        dst = (pLN[:, n2 * 512:(n2 + 1) * 512] if par == 0
                               else pLN[n2][:, :])
                        for fk in range(FC):
                            last_i = t.matmul(
                                dst,
                                OTs[:, fk, tc * 128:(tc + 1) * 128],
                                Wo[:, fk, n2 * 512:(n2 + 1) * 512],
                                start=(fk == 0), stop=(fk == FC - 1))
                    return last_i
                mm = sch.step("tensor", mmf, deps=[wold, s2_done] +
                              ([s1h[par]] if s1h[par] else []))
                ln, ls1, ls3 = emit_ln(
                    nc, sch, cm, lbs[par], pLN, xrs[par], lbs[par]["uf"],
                    [mm, xl] + ([xsoh[par]] if xsoh[par] else []),
                    sq_war=([s3h[par]] if s3h[par] else []))
                s1h[par], s3h[par] = ls1, ls3

                def xso(sy, sem, tc=tc, par=par):
                    sy.dma_start(out=x1_d[tc * 128:(tc + 1) * 128, :],
                                 in_=lbs[par]["uf"][:, :]).then_inc(sem, 16)
                last = sch.dma("sync", xso, 1, deps=[ln])
                xsoh[par] = last
                xso_all.append(last)
            wo_done = sch.step("gpsimd", lambda g: g.memset(
                cm["cpool"][:, 3:4], 0.0), deps=xso_all)

        esC = ExitStack()
        with esC:
            sb2 = lambda n, s, dt: esC.enter_context(nc.sbuf_tensor(n, s, dt))
            wqgT = sb2("wqgT", [128, FC, DI], BF16)
            wquT = sb2("wquT", [128, FC, DI], BF16)
            wqdT = sb2("wqdT", [128, DI // 128, DM], BF16)
            ptr = cm["ps_bf"]

            def wload_g(sy, sem):
                for sl in range(DI // 128):
                    core, k = (sl // 3, sl % 3) if sl < 21 else (7, 0)
                    sy.dma_start(
                        out=wqgT[:, :, sl * 128:(sl + 1) * 128],
                        in_=wqb_out.ap()[core, :, k * 1024:(k + 1) * 1024]
                        .rearrange("p (a b) -> p a b", b=128)
                    ).then_inc(sem, 16)
                for sl in range(DM // 256):
                    sy.dma_start(
                        out=wqdT[:, :, sl * 128:(sl + 1) * 128],
                        in_=wqb_out.ap()[sl, :, 6144:6144 + DI]
                        .rearrange("p (a b) -> p a b", b=128)
                    ).then_inc(sem, 16)

            def wload_u(gp, sem):
                for sl in range(DI // 128):
                    core, k = (sl // 3, sl % 3) if sl < 21 else (7, 0)
                    gp.dma_start(
                        out=wquT[:, :, sl * 128:(sl + 1) * 128],
                        in_=wqb_out.ap()[core, :, 3072 + k * 1024:
                                         3072 + (k + 1) * 1024]
                        .rearrange("p (a b) -> p a b", b=128)
                    ).then_inc(sem, 16)
                for sl in range(DM // 256, DM // 128):
                    gp.dma_start(
                        out=wqdT[:, :, sl * 128:(sl + 1) * 128],
                        in_=wqb_out.ap()[sl, :, 6144:6144 + DI]
                        .rearrange("p (a b) -> p a b", b=128)
                    ).then_inc(sem, 16)
            wq_done_g = sch.dma("sync", wload_g, 26, deps=[ccw_s, wo_done])
            wq_done_u = sch.dma("gpsimd", wload_u, 26, deps=[ccw_s, wo_done])
            wq_done2 = [wq_done_g, wq_done_u]

            x1b = [sb2("x1b0", [128, DM], F32), sb2("x1b1", [128, DM], F32)]
            x1q0 = sb2("x1q0", [128, DM], BF16)
            x1q = [x1q0, x1q0]
            xqT = [sb2("xqT0", [128, FC, 128], BF16),
                   sb2("xqT1", [128, FC, 128], BF16)]
            hg = [sb2("hg0", [128, 512], F32), sb2("hg1", [128, 512], F32)]
            hu = [sb2("hu0", [128, 512], F32), sb2("hu1", [128, 512], F32)]
            hh = [sb2("hh0", [128, 512], F32), sb2("hh1", [128, 512], F32)]
            hqc = [sb2("hqc0", [128, 512], BF16),
                   sb2("hqc1", [128, 512], BF16)]
            hqT0 = sb2("hqT0", [128, 4, 128], BF16)
            hqT = [hqT0, hqT0]
            qxh = [{"xa": sb2("xah0", [128, 512], F32),
                    "sgb": sb2("sgh0", [128, 512], BF16)},
                   {"xa": sb2("xah1", [128, 512], F32),
                    "sgb": sb2("sgh1", [128, 512], BF16)}]
            lb2 = {"uf": sb2("l2_uf", [128, DM], F32),
                   "sq": sb2("l2sq", [128, DM], BF16),
                   "st4": sb2("l2_st4", [128, 4], F32)}
            yc = lb2["uf"]
            pG2 = [cm["ps_c"], cm["ps_e"]]
            pU2 = [cm["ps_d"], cm["ps_f"]]
            pDN = cm["ps_b"]

            ICH = [(0, 512), (512, 512), (1024, 512), (1536, 512),
                   (2048, 512), (2560, 256)]
            hist = {"ge": [], "hf": [], "qv": [], "ht": [], "hte": [],
                    "dm": [], "ln2": [], "ln2s1": [], "ln2s3": [],
                    "qdq": [], "xt_all": [], "xte": [None, None],
                    "ydma": []}
            war = lambda k, n=2: ([hist[k][-n]] if len(hist[k]) >= n else [])

            def emit_x1(tc):
                tpar = tc % 2

                def x1l(sy, sem, tc=tc, tpar=tpar):
                    sy.dma_start(out=x1b[tpar][:, :],
                                 in_=x1_d[tc * 128:(tc + 1) * 128, :]
                                 ).then_inc(sem, 16)
                xl = sch.dma("sync", x1l, 1,
                             deps=[wo_done] + hist["ln2"][-1:] +
                             hist["qdq"][-2:][:1])
                qdeps = [xl] + wq_done2 + hist["qdq"][-1:]
                qdeps += hist["xt_all"][-1:]
                if hist["xte"][tpar]:
                    qdeps.append(hist["xte"][tpar])
                qsg = emit_quant_sg(nc, sch, cm, qb, x1b[tpar], DM, qdeps)
                qd = emit_quant_qv(nc, sch, cm, qb, qb, x1b[tpar], DM,
                                   x1q[tpar], qdeps + [qsg])
                hist["qdq"].append(qd)

                def xtr(t, tpar=tpar):
                    for fc in range(FC):
                        lastt = t.transpose(ptr[:, fc * 128:(fc + 1) * 128],
                                            x1q[tpar][:, fc * 128:(fc + 1) * 128],
                                            cm["idm"][:, :])
                    return lastt
                xt = sch.step("tensor", xtr,
                              deps=[qd, cm["ptrtok"][0]] + war("hte", 1))
                hist["xt_all"].append(xt)
                xte = sch.step("scalar", lambda s, tpar=tpar: s.copy(
                    out=xqT[tpar][:, :, :].rearrange("p a b -> p (a b)"),
                    in_=ptr[:, 0:1024]), deps=[xt])
                hist["xte"][tpar] = xte
                cm["ptrtok"][0] = xte
                return qd, xte

            cur = emit_x1(0)
            for tc in range(NCH):
                tpar = tc % 2
                qd, xte = cur

                dn_mms = []
                pend_sg = None
                pend_fdn = None

                def emit_qv(pq):
                    sg_p, ici_p, i0_p, ilen_p, par_p = pq
                    qv = emit_quant_qv(
                        nc, sch, cm, qb, qxh[par_p], hh[par_p], ilen_p,
                        hqc[par_p], [sg_p] + war("ht", 1))
                    hist["qv"].append(qv)
                    return (qv, ici_p, i0_p, ilen_p, par_p)

                def emit_f(pf):
                    qv_p, ici_p, i0_p, ilen_p, par_p = pf
                    emit_fdn(nc, sch, cm, ptr, hqc, hqT, wqdT, pDN,
                             dn_mms, hist, ICH, qv_p, ici_p, i0_p, ilen_p,
                             par_p, tc)

                for ici, (i0, ilen) in enumerate(ICH):
                    par = ici % 2
                    pg, pu = pG2[par], pU2[par]

                    def gmumf(t, i0=i0, ilen=ilen, pg=pg, pu=pu, tpar=tpar):
                        for fk in range(FC):
                            t.matmul(pg[:, 0:ilen], xqT[tpar][:, fk, :],
                                     wqgT[:, fk, i0:i0 + ilen],
                                     start=(fk == 0), stop=(fk == FC - 1))
                        for fk in range(FC):
                            last_i = t.matmul(pu[:, 0:ilen], xqT[tpar][:, fk, :],
                                              wquT[:, fk, i0:i0 + ilen],
                                              start=(fk == 0),
                                              stop=(fk == FC - 1))
                        return last_i
                    gm = sch.step("tensor", gmumf,
                                  deps=[xte] + wq_done2 + war("ge"))

                    def geuef(s, ilen=ilen, pg=pg, pu=pu, par=par):
                        s.activation(out=hg[par][:, 0:ilen], in_=pg[:, 0:ilen],
                                     func=AF.Silu)
                        return s.copy(out=hu[par][:, 0:ilen], in_=pu[:, 0:ilen])
                    ge = sch.step("scalar", geuef, deps=[gm] + war("hf"))
                    hist["ge"].append(ge)

                    def hff(v, ilen=ilen, par=par):
                        return v.tensor_tensor(out=hh[par][:, 0:ilen],
                                               in0=hg[par][:, 0:ilen],
                                               in1=hu[par][:, 0:ilen],
                                               op=ALU.mult)
                    hstep = sch.step("vector", hff,
                                     deps=[ge] + war("qv", 1))
                    hist["hf"].append(hstep)
                    sg = emit_quant_sg(nc, sch, cm, qxh[par], hh[par], ilen,
                                       [hstep] + war("qv", 1))
                    if pend_fdn is not None:
                        emit_f(pend_fdn)
                        pend_fdn = None
                    if pend_sg is not None:
                        pend_fdn = emit_qv(pend_sg)
                    pend_sg = (sg, ici, i0, ilen, par)
                if pend_fdn is not None:
                    emit_f(pend_fdn)
                emit_f(emit_qv(pend_sg))

                if tc + 1 < NCH:
                    cur = emit_x1(tc + 1)

                ln2, l2s1, l2s3 = emit_ln(
                    nc, sch, cm, lb2, pDN, x1b[tpar], yc,
                    [dn_mms[-1], qd] + hist["ydma"][-1:],
                    sq_war=hist["ln2s3"][-1:])
                hist["ln2"].append(ln2)
                hist["ln2s1"].append(l2s1)
                hist["ln2s3"].append(l2s3)

                def yso(gp, sem, tc=tc):
                    gp.dma_start(out=y_d[tc * 128:(tc + 1) * 128, :],
                                 in_=yc[:, :]).then_inc(sem, 16)
                ydma = sch.dma("gpsimd", yso, 1, deps=[ln2])
                hist["ydma"].append(ydma)

            sch.step("gpsimd", lambda g: g.memset(cm["cpool"][:, 3:4], 0.0),
                     deps=[hist["ydma"][-1]])
        sch.run(es)
    return nc


def kernel(**inputs):
    x = np.asarray(inputs["L0_attn_x"], np.float32).reshape(T, DM)
    w = {k: np.ascontiguousarray(np.asarray(v, np.float32))
         for k, v in inputs.items()}
    gates = np.zeros((NCORE, 128, 8), np.float32)
    for c in range(NCORE):
        gates[c, :, :c] = 1.0
    def myw(warr):
        outs = []
        for c in range(NCORE):
            buf = np.zeros((384, DM), np.float32)
            lo = min(3 * c, 21)
            n = min(3, 22 - lo) if c < 7 else 1
            buf[0:n * 128] = warr[lo * 128:(lo + n) * 128]
            outs.append(buf)
        return outs
    wg_list = myw(w["L0_ffn_w_gate"])
    wu_list = myw(w["L0_ffn_w_up"])

    nc = build_fused()
    ins = [{"x": np.ascontiguousarray(x[c * TC:(c + 1) * TC]),
            "W_q": w["L0_attn_W_q"], "W_k": w["L0_attn_W_k"],
            "W_v": w["L0_attn_W_v"], "Wg_K": w["L0_attn_Wg_K"],
            "Wg_C": w["L0_attn_Wg_C"],
            "bg_K": w["L0_attn_bg_K"].reshape(1, DM),
            "bg_C": w["L0_attn_bg_C"].reshape(1, DM),
            "gates": gates[c], "W_o": w["L0_attn_W_o"],
            "wg_my": wg_list[c], "wu_my": wu_list[c],
            "wdn_my": np.ascontiguousarray(
                w["L0_ffn_w_down"][c * 128:(c + 1) * 128])}
           for c in range(NCORE)]
    r = run_bass_kernel_spmd(nc, ins, core_ids=list(range(NCORE))).results
    y = np.concatenate([np.asarray(r[c]["y"]) for c in range(NCORE)], axis=0)
    return y.reshape(1, T, DM)


# version 21
# speedup vs baseline: 1.0191x; 1.0027x over previous
import numpy as np
from contextlib import ExitStack

import concourse.bass as bass
import concourse.mybir as mybir
from concourse.bass_utils import run_bass_kernel_spmd

F32 = mybir.dt.float32
BF16 = mybir.dt.bfloat16
FP16 = mybir.dt.float16
I32 = mybir.dt.int32
AF = mybir.ActivationFunctionType
ALU = mybir.AluOpType

T, NCORE = 16384, 8
TC = T // NCORE
L = 128
NCH = TC // L
H, DK, DV, DM, DI = 16, 64, 64, 1024, 2816
NP = H // 2
FC = DM // 128
EPS = 1e-5
MIDS = [(0.25, 0.5), (0.75, 0.5), (1.25, 0.5), (1.75, 0.5),
        (2.5, 1.0), (3.5, 1.0), (5.0, 2.0)]
CONSTS = [0.0, 1.0, EPS]


class Sched:
    ENGINES = ("sync", "gpsimd", "scalar", "vector", "tensor")

    def __init__(self, nc):
        self.nc = nc
        self.steps = []

    def step(self, engine, fn, deps=(), coll=False):
        self.steps.append((engine, fn, tuple(deps), 0, coll))
        return len(self.steps) - 1

    def dma(self, engine, fn, n, deps=()):
        self.steps.append((engine, fn, tuple(deps), n, False))
        return len(self.steps) - 1

    NDMA_SEMS = 5

    def run(self, es):
        nc = self.nc
        semnames = list(self.ENGINES)
        for e in ("sync", "gpsimd"):
            semnames += [e + "_dma%d" % r for r in range(self.NDMA_SEMS)]
        ncoll = sum(1 for s in self.steps if s[4])
        semnames += ["cc%d" % r for r in range(ncoll)]
        sems = {e: es.enter_context(nc.semaphore("sem_" + e)) for e in semnames}
        dma_ord = {e: 0 for e in ("sync", "gpsimd")}
        cc_ord = 0
        semname = [None] * len(self.steps)
        prior = [0] * len(self.steps)
        done = [0] * len(self.steps)
        count = {e: 0 for e in semnames}
        for i, (e, fn, deps, nd, coll) in enumerate(self.steps):
            if nd:
                sn = e + "_dma%d" % (dma_ord[e] % self.NDMA_SEMS)
                dma_ord[e] += 1
            elif coll:
                sn = "cc%d" % cc_ord
                cc_ord += 1
            else:
                sn = e
            semname[i] = sn
            prior[i] = count[sn]
            count[sn] += 16 * nd if nd else 1
            done[i] = count[sn]
        with nc.Block() as blk:
            for e in self.ENGINES:
                mine = [(i, fn, deps, nd)
                        for i, (ee, fn, deps, nd, _coll) in
                        enumerate(self.steps) if ee == e]
                if not mine:
                    continue

                def make(eng_name, mine):
                    def body(eng):
                        for i, fn, deps, nd in mine:
                            need_drain = False
                            for d in deps:
                                dsem = semname[d]
                                if dsem != eng_name or self.steps[d][3]:
                                    eng.wait_ge(sems[dsem], done[d])
                                else:
                                    need_drain = True
                            if need_drain and eng_name in ("scalar", "vector"):
                                eng.drain()
                            if nd:
                                if prior[i]:
                                    eng.wait_ge(sems[semname[i]], prior[i])
                                fn(eng, sems[semname[i]])
                            else:
                                fn(eng).then_inc(sems[semname[i]], 1)
                    return body
                getattr(blk, e)(make(e, mine))
        return sems, done


def alloc_common(nc, es, sch):
    cm = {}
    sb = lambda n, s, dt: es.enter_context(nc.sbuf_tensor(n, s, dt))
    cm["iot"] = sb("iot", [128, 128], I32)
    cm["idm"] = sb("idm", [128, 128], BF16)
    cm["m_incl"] = sb("m_incl", [128, 128], BF16)
    cm["m_str"] = sb("m_str", [128, 128], BF16)
    cm["m_strn"] = sb("m_strn", [128, 128], BF16)
    cm["zero128"] = sb("zero128", [128, 128], F32)
    cm["ones64"] = sb("ones64", [128, 64], BF16)
    cm["cpool"] = sb("cpool", [128, 4], F32)
    cm["ps_bf"] = es.enter_context(nc.psum_tensor("ps_bf", [128, 1024], BF16))
    cm["ps_a"] = es.enter_context(nc.psum_tensor("ps_a", [128, 512], F32))
    cm["ps_b"] = es.enter_context(nc.psum_tensor("ps_b", [128, 1024], F32))
    cm["ps_c"] = es.enter_context(nc.psum_tensor("ps_c", [128, 512], F32))
    cm["ps_d"] = es.enter_context(nc.psum_tensor("ps_d", [128, 512], F32))
    cm["ps_e"] = es.enter_context(nc.psum_tensor("ps_e", [128, 512], F32))
    cm["ps_f"] = es.enter_context(nc.psum_tensor("ps_f", [128, 512], F32))
    for i, cv in enumerate(CONSTS):
        nc.const_aps.register(F32, cv, cm["cpool"][:, i:i + 1])

    def c_gp(g):
        for i, cv in enumerate(CONSTS):
            g.memset(cm["cpool"][:, i:i + 1], cv)
        g.memset(cm["zero128"][:, :], 0.0)
        g.memset(cm["ones64"][:, :], 1.0)
        return g.iota(cm["iot"][:, :], [[1, 128]], channel_multiplier=-1)
    s0 = sch.step("gpsimd", c_gp)

    def c_v(v):
        v.tensor_scalar(out=cm["idm"][:, :], in0=cm["iot"][:, :], scalar1=0,
                        scalar2=None, op0=ALU.is_equal)
        v.tensor_scalar(out=cm["m_incl"][:, :], in0=cm["iot"][:, :], scalar1=0,
                        scalar2=None, op0=ALU.is_ge)
        v.tensor_scalar(out=cm["m_str"][:, :], in0=cm["iot"][:, :], scalar1=0,
                        scalar2=None, op0=ALU.is_gt)
        return v.tensor_scalar(out=cm["m_strn"][:, :], in0=cm["iot"][:, :],
                               scalar1=0, scalar2=-1.0, op0=ALU.is_gt,
                               op1=ALU.mult)
    cm["ready"] = sch.step("vector", c_v, deps=[s0])
    cm["ptrtok"] = [cm["ready"]]
    cm["tmpa"] = sb("utmp_a", [128, 512], F32)
    cm["tmpb"] = sb("utmp_b", [128, 512], F32)
    return cm


def emit_xT(nc, es, sch, cm, x_d, xT):
    xg = es.enter_context(nc.sbuf_tensor("xg", [128, 4, DM], BF16))
    ptr = cm["ps_bf"]
    last_evac = cm["ready"]
    for g in range(TC // 512):
        def ldf(gp, sem, g=g):
            gp.dma_start(
                out=xg[:, :, :],
                in_=x_d[g * 512:(g + 1) * 512, :]
                .rearrange("(a p) f -> p a f", p=128)).then_inc(sem, 16)
        ld = sch.dma("gpsimd", ldf, 1, deps=[last_evac])
        for tc4 in range(4):
            def tf(t, tc4=tc4):
                for fc in range(FC):
                    last = t.transpose(ptr[:, fc * 128:(fc + 1) * 128],
                                       xg[:, tc4, fc * 128:(fc + 1) * 128],
                                       cm["idm"][:, :])
                return last
            ts = sch.step("tensor", tf,
                          deps=[ld, cm["ready"], cm["ptrtok"][0]])

            def vf(v, g=g, tc4=tc4):
                t0 = g * 512 + tc4 * 128
                return v.tensor_copy(
                    out=xT[:, :, t0:t0 + 128],
                    in_=ptr[:, :].rearrange("p (a b) -> p a b", b=128))
            last_evac = sch.step("vector", vf, deps=[ts])
            cm["ptrtok"][0] = last_evac
    return last_evac


def emit_projections(nc, es, sch, cm, xT, xT_done, wd, outs, which=("q", "k", "gK", "gC", "v")):
    Wbufs = [es.enter_context(nc.sbuf_tensor("Wp0", [128, FC, DM], BF16)),
             es.enter_context(nc.sbuf_tensor("Wp1", [128, FC, DM], BF16))]
    stgs = [es.enter_context(nc.sbuf_tensor("stgp0", [128, TC], BF16)),
            es.enter_context(nc.sbuf_tensor("stgp1", [128, TC], BF16))]
    stg16 = es.enter_context(nc.sbuf_tensor("stg16", [128, TC], FP16))
    spsA = [es.enter_context(nc.sbuf_tensor("spsp0", [128, 512], F32)),
            es.enter_context(nc.sbuf_tensor("spsp1", [128, 512], F32))]
    spfA = [es.enter_context(nc.sbuf_tensor("spfp0", [128, 512], F32)),
            es.enter_context(nc.sbuf_tensor("spfp1", [128, 512], F32))]
    pps = [cm["ps_a"], cm["ps_c"], cm["ps_d"], cm["ps_e"]]

    bgn8 = es.enter_context(nc.sbuf_tensor("bgn8", [128, 16], F32))

    def bg_ld(sy, sem):
        with nc.allow_non_contiguous_dma(reason="tiny bias gather"):
            sy.dma_start(out=bgn8[:, 0:8],
                         in_=wd["bg_K"].rearrange("o (a p) -> p (o a)", p=128)
                         ).then_inc(sem, 16)
            sy.dma_start(out=bgn8[:, 8:16],
                         in_=wd["bg_C"].rearrange("o (a p) -> p (o a)", p=128)
                         ).then_inc(sem, 16)
    bgl = sch.dma("sync", bg_ld, 2)
    bgneg = sch.step("vector", lambda v: v.tensor_scalar_mul(
        out=bgn8[:, :], in0=bgn8[:, :], scalar1=-1.0), deps=[bgl])

    NBANK = len(pps)
    evhist = [None] * NBANK
    sphist = [None, None]
    prev_dma = {}
    lastmm = []
    gi = [0, 0]
    finals = []

    def emit_one(mi, Wname, out_dram, kind, bg_off=None):
        Wp = Wbufs[mi % 2]

        def wldf(gp, sem, Wname=Wname, Wp=Wp):
            gp.dma_start(out=Wp[:, :, :],
                         in_=wd[Wname].rearrange("(a p) c -> p a c", p=128)
                         ).then_inc(sem, 16)
        wld = sch.dma("gpsimd", wldf, 1,
                      deps=([lastmm[mi - 2]] if mi >= 2 else []))
        fin = []
        mm = None
        if kind in ("fm", "gate"):
            for oc in range(FC):
                sbuf = ("stg%d" % (oc % 2)) if kind == "fm" else "stg16"
                stg = stgs[oc % 2]
                mm_steps = []
                for tch in range(4):
                    par = gi[0] % NBANK
                    pp = pps[par]
                    gi[0] += 1
                    def mf(t, oc=oc, tch=tch, pp=pp, Wp=Wp):
                        for fk in range(FC):
                            last_i = t.matmul(
                                pp[:, :], Wp[:, fk, oc * 128:(oc + 1) * 128],
                                xT[:, fk, tch * 512:(tch + 1) * 512],
                                start=(fk == 0), stop=(fk == FC - 1))
                        return last_i
                    deps = [wld, xT_done]
                    if evhist[par] is not None:
                        deps.append(evhist[par])
                    mm = sch.step("tensor", mf, deps=deps)
                    if kind == "fm":
                        def ef(s, tch=tch, pp=pp, stg=stg):
                            return s.copy(out=stg[:, tch * 512:(tch + 1) * 512],
                                          in_=pp[:, :])
                        ev = sch.step("scalar", ef, deps=[mm] +
                                      ([prev_dma[sbuf]] if prev_dma.get(sbuf)
                                       else []))
                        evhist[par] = ev
                    else:
                        sp = gi[1] % 2
                        gi[1] += 1

                        def e1(s, bg_off=bg_off, oc=oc, pp=pp, sp=sp):
                            s.drain()
                            s.activation(out=spsA[sp][:, :], in_=pp[:, :],
                                         func=AF.Exp, scale=-1.0,
                                         bias=bgn8[:, bg_off + oc:bg_off + oc + 1])
                            s.drain()
                            return s.activation(out=spfA[sp][:, :],
                                                in_=spsA[sp][:, :],
                                                func=AF.Ln, bias=1.0)
                        sv = sch.step("scalar", e1, deps=[mm, bgneg] +
                                      ([sphist[sp]] if sphist[sp] else []))

                        def e2(v, tch=tch, sp=sp):
                            for c4 in range(4):
                                sl = slice(c4 * 128, (c4 + 1) * 128)
                                v.tensor_tensor_scan(
                                    out=spsA[sp][:, sl], data0=spfA[sp][:, sl],
                                    data1=cm["zero128"][:, :], initial=0.0,
                                    op0=ALU.add, op1=ALU.add)
                            v.drain()
                            return v.tensor_copy(
                                out=stg16[:, tch * 512:(tch + 1) * 512],
                                in_=spsA[sp][:, :])
                        ev = sch.step("vector", e2, deps=[sv] +
                                      ([prev_dma[sbuf]] if prev_dma.get(sbuf)
                                       else []))
                        evhist[par] = ev
                        sphist[sp] = ev
                    mm_steps.append((mm, ev))
                evs = [ev for _, ev in mm_steps]

                def df(sy, sem, oc=oc, kind=kind, stg=stg):
                    src = stg if kind == "fm" else stg16
                    for c in range(NCH):
                        sy.dma_start(
                            out=out_dram[c, :, oc * L:(oc + 1) * L],
                            in_=src[:, c * L:(c + 1) * L]).then_inc(sem, 16)
                dmas = sch.dma("sync", df, NCH, deps=evs)
                prev_dma[sbuf] = dmas
                fin.append(dmas)
        else:
            for tch in range(NCH):
                sbuf = "stg%d" % (tch % 2)
                stg = stgs[tch % 2]
                prev_ev = None
                for n2 in range(2):
                    par = gi[0] % NBANK
                    pp = pps[par]
                    gi[0] += 1
                    def mf2(t, tch=tch, n2=n2, pp=pp, Wp=Wp):
                        for fk in range(FC):
                            last_i = t.matmul(
                                pp[:, :],
                                xT[:, fk, tch * 128:(tch + 1) * 128],
                                Wp[:, fk, n2 * 512:(n2 + 1) * 512],
                                start=(fk == 0), stop=(fk == FC - 1))
                        return last_i
                    deps = [wld, xT_done]
                    if evhist[par] is not None:
                        deps.append(evhist[par])
                    mm = sch.step("tensor", mf2, deps=deps)

                    def ef(s, n2=n2, pp=pp, stg=stg):
                        return s.copy(out=stg[:, n2 * 512:(n2 + 1) * 512],
                                      in_=pp[:, :])
                    prev_ev = sch.step("scalar", ef, deps=[mm] +
                                       ([prev_dma[sbuf]] if prev_dma.get(sbuf)
                                        else []))
                    evhist[par] = prev_ev

                def df(sy, sem, tch=tch, stg=stg):
                    sy.dma_start(out=out_dram[tch * 128:(tch + 1) * 128, :],
                                 in_=stg[:, 0:DM]).then_inc(sem, 16)
                prev_dma[sbuf] = sch.dma("sync", df, 1, deps=[prev_ev])
                fin.append(prev_dma[sbuf])
        lastmm.append(mm)
        finals.append(fin[-1])

    mats = [("q", "W_q", outs["q_s"], "fm", None),
            ("k", "W_k", outs["k_s"], "fm", None),
            ("gK", "Wg_K", outs["spK_s"], "gate", 0),
            ("gC", "Wg_C", outs["spC_s"], "gate", 8),
            ("v", "W_v", outs["v_s"], "tm", None)]
    mats = [m for m in mats if m[0] in which]
    for mi, (_, wn, od, kd, bo) in enumerate(mats):
        emit_one(mi, wn, od, kd, bg_off=bo)
    return finals


class ChunkBufs:
    def __init__(self, nc, es, tag, with_sweep2, share_prep=None):
        sb = lambda n, s, dt: es.enter_context(nc.sbuf_tensor(n + tag, s, dt))
        if share_prep is not None:
            for f in ("qf", "kf", "sK", "sC", "Pinv", "Rg", "Rinv"):
                setattr(self, f, getattr(share_prep, f))
        else:
            self.qf = sb("qf", [128, FC, L], BF16)
            self.kf = sb("kf", [128, FC, L], BF16)
            self.sK = sb("sK", [128, FC, L], FP16)
            self.sC = sb("sC", [128, FC, L], FP16)
            self.Pinv = sb("Pinv", [128, DM], BF16)
            self.Rg = sb("Rg", [128, DM], BF16)
            self.Rinv = sb("Rinv", [128, DM], BF16)
        self.vt = sb("vt", [128, DM], BF16)
        self.Ut = sb("Ut", [128, DM], BF16)
        self.Qh = sb("Qh", [128, DM], BF16)
        self.Kb = sb("Kb", [128, DM], BF16)
        self.Utm = sb("Utm", [128, DM], BF16)
        self.Qhtm = sb("Qhtm", [128, DM], BF16)
        self.plK = sb("plK", [128, 8], F32)
        self.rlE = sb("rlE", [128, 8], F32)
        self.plbc_r = sb("plbc_r", [128, 512], BF16)
        self.plbc = sb("plbc", [128, 512], F32)
        if with_sweep2:
            self.P = sb("P", [128, DM], BF16)
            self.PRg = sb("PRg", [128, DM], BF16)
            self.Qb = sb("Qb", [128, DM], BF16)
            self.prg16 = sb("prg16", [128, DM], FP16)
        else:
            self.Kbtm = sb("Kbtm", [128, DM], BF16)


def emit_chunk_prep(nc, sch, cm, cb, srcs, c, sweep2, extra_dep=None):
    q_s, k_s, v_s, spK_s, spC_s = srcs

    def ldf(sy, sem, c=c):
        sy.dma_start(out=cb.qf[:, :, :].rearrange("p a b -> p (a b)"),
                     in_=q_s[c, :, :]).then_inc(sem, 16)
        sy.dma_start(out=cb.kf[:, :, :].rearrange("p a b -> p (a b)"),
                     in_=k_s[c, :, :]).then_inc(sem, 16)
        sy.dma_start(out=cb.vt[:, :],
                     in_=v_s[c * L:(c + 1) * L, :]).then_inc(sem, 16)
        sy.dma_start(out=cb.sK[:, :, :].rearrange("p a b -> p (a b)"),
                     in_=spK_s[c, :, :]).then_inc(sem, 16)
        sy.dma_start(out=cb.sC[:, :, :].rearrange("p a b -> p (a b)"),
                     in_=spC_s[c, :, :]).then_inc(sem, 16)
    deps = [] if extra_dep is None else list(extra_dep)
    ld = sch.dma("sync", ldf, 5, deps=deps)

    fl = lambda b: b[:, :, :].rearrange("p a b -> p (a b)")

    def expf(s):
        s.activation(out=cb.Pinv[:, :], in_=fl(cb.sK), func=AF.Exp)
        s.activation(out=cb.Rg[:, :], in_=fl(cb.sC), func=AF.Exp, scale=-1.0)
        s.activation(out=cb.Rinv[:, :], in_=fl(cb.sC), func=AF.Exp)
        s.activation(out=cb.plK[:, :], in_=cb.sK[:, :, L - 1:L].squeeze(2),
                     func=AF.Exp, scale=-1.0)
        last = s.activation(out=cb.rlE[:, :], in_=cb.sC[:, :, L - 1:L].squeeze(2),
                            func=AF.Exp, scale=-1.0)
        if sweep2:
            s.activation(out=cb.P[:, :], in_=fl(cb.sK), func=AF.Exp, scale=-1.0)
            last = s.activation(out=cb.PRg[:, :], in_=cb.prg16[:, :],
                                func=AF.Exp, scale=-1.0)
        return last
    if sweep2:
        def addsp(v):
            return v.tensor_add(out=cb.prg16[:, :], in0=fl(cb.sK), in1=fl(cb.sC))
        sp_add = sch.step("vector", addsp, deps=[ld])
        exps = sch.step("scalar", expf, deps=[ld, sp_add])
    else:
        exps = sch.step("scalar", expf, deps=[ld])

    def gatef(v):
        v.tensor_mul(out=cb.Ut[:, :], in0=fl(cb.kf), in1=cb.Pinv[:, :])
        v.tensor_mul(out=cb.Qh[:, :], in0=fl(cb.qf), in1=cb.Rinv[:, :])
        v.tensor_mul(out=cb.Kb[:, :], in0=fl(cb.kf), in1=cb.Rg[:, :])
        if sweep2:
            v.tensor_mul(out=cb.Qb[:, :], in0=fl(cb.qf), in1=cb.P[:, :])
        v.tensor_tensor(
            out=cb.plbc_r[0:64, :].rearrange("p (j k) -> p j k", k=64),
            in0=cb.plK[0:64, :].unsqueeze(2).broadcast_to([64, 8, 64]),
            in1=cm["idm"][0:64, 0:64].unsqueeze(1).broadcast_to([64, 8, 64]),
            op=ALU.mult)
        return v.tensor_tensor(
            out=cb.plbc_r[64:128, :].rearrange("p (j k) -> p j k", k=64),
            in0=cb.plK[64:128, :].unsqueeze(2).broadcast_to([64, 8, 64]),
            in1=cm["idm"][64:128, 64:128].unsqueeze(1).broadcast_to([64, 8, 64]),
            op=ALU.mult)
    gates = sch.step("vector", gatef, deps=[exps, cm["ready"]])
    return {"ld": ld, "exps": exps, "gates": gates}


MODULE_DOC_END = True


def rowbc(col8):
    return col8[:, :].unsqueeze(2).broadcast_to([128, 8, 64])


def emit_plbcast(nc, sch, cm, cb, pB, dep_gates):
    def mmf(t):
        t.matmul(pB[0:64, :], cm["ones64"][0:64, :], cb.plbc_r[0:64, :],
                 start=True, stop=True)
        return t.matmul(pB[64:128, :], cm["ones64"][64:128, :],
                        cb.plbc_r[64:128, :], start=True, stop=True)
    mm = sch.step("tensor", mmf, deps=dep_gates)
    ev = sch.step("vector", lambda v: v.tensor_copy(out=cb.plbc[:, :],
                                                    in_=pB[:, :]), deps=[mm])
    return ev


def emit_sweep1(nc, es, sch, cm, srcs, maps_o, start_dep, lvl=9, nch=NCH, co=None):
    sb = lambda n, s, dt: es.enter_context(nc.sbuf_tensor(n, s, dt))
    cbs = [ChunkBufs(nc, es, "_s1a", with_sweep2=False)]
    cbs.append(ChunkBufs(nc, es, "_s1b", with_sweep2=False,
                         share_prep=cbs[0]))
    LSm = sb("LSm", [128, 512], F32)
    LCm = sb("LCm", [128, 512], F32)
    Xtm = sb("Xtm", [128, 512], F32)
    LGm = sb("LGm", [128, 512], F32)
    Dm = sb("Dm", [128, 8], F32)
    Em = sb("Em", [128, 8], F32)
    LC_b = sb("LC_b", [128, 512], BF16)
    Yt_b = sb("Yt_b", [128, 512], BF16)
    Wl_b = sb("Wl_b", [128, DM], BF16)
    s3ms = [sb("s3m_s1a", [128, 256], BF16), sb("s3m_s1b", [128, 256], BF16)]

    ptr = cm["ps_bf"]
    ps3es = [cm["ps_a"][:, 0:128], cm["ps_a"][:, 128:256]]
    ps3os = [cm["ps_b"][:, 512:640], cm["ps_b"][:, 640:768]]
    pWs = [cm["ps_b"][:, 0:128], cm["ps_b"][:, 128:256]]
    pUU = cm["ps_c"]
    pQV = cm["ps_d"]
    pG = cm["ps_e"]
    pB = cm["ps_f"]

    def init(g):
        for b in (LSm, LCm, Xtm, LGm):
            g.memset(b[:, :], 0.0)
        g.memset(LC_b[:, :], 0.0)
        g.memset(Dm[:, :], 1.0)
        return g.memset(Em[:, :], 1.0)
    upd = sch.step("gpsimd", init,
                   deps=(start_dep if isinstance(start_dep, list)
                         else [start_dep]))

    upd_hist = [upd, upd]
    ev3_h = [None, None]
    wl_h = [None, None]
    ew_h = [None, None]

    last_gates = None
    for c in range(nch if lvl >= 1 else 0):
        cb = cbs[c % 2]
        pr = emit_chunk_prep(nc, sch, cm, cb, srcs, c, sweep2=False,
                             extra_dep=[upd_hist[c % 2]] +
                             ([last_gates] if last_gates else []))
        last_gates = pr["gates"]
        if lvl < 2:
            upd = pr["gates"]
            upd_hist[c % 2] = upd
            continue
        tms = {}
        for name, src, dst in (("Ut", cb.Ut, cb.Utm), ("Qh", cb.Qh, cb.Qhtm),
                               ("Kb", cb.Kb, cb.Kbtm)):
            def tf(t, src=src):
                for j in range(NP):
                    last = t.transpose(ptr[:, j * 128:(j + 1) * 128],
                                       src[:, j * 128:(j + 1) * 128],
                                       cm["idm"][:, :])
                return last
            ts = sch.step("tensor", tf, deps=[pr["gates"], cm["ptrtok"][0]])

            def vf(s, dst=dst):
                return s.copy(out=dst[:, :], in_=ptr[:, :])
            prev_ev = sch.step("scalar", vf, deps=[ts])
            cm["ptrtok"][0] = prev_ev
            tms[name] = prev_ev
        plb = emit_plbcast(nc, sch, cm, cb, pB, [pr["gates"], upd])
        if lvl < 3:
            upd = plb
            upd_hist[c % 2] = upd
            continue

        wl_steps, yt_evs, uu_steps, qv_steps, gg_steps = [], [], [], [], []
        prev_yt = plb
        for j in range(NP):
            he, ho = 2 * j, 2 * j + 1
            pj = slice(j * 128, (j + 1) * 128)
            jp = j % 2
            ps3e, ps3o, pW, s3m = ps3es[jp], ps3os[jp], pWs[jp], s3ms[jp]

            def s3f(t, j=j, pj=pj, cb=cb, ps3e=ps3e, ps3o=ps3o):
                t.matmul(ps3e[:, :], cb.Qh[0:64, pj], cb.Kb[0:64, pj],
                         start=True, stop=True)
                return t.matmul(ps3o[:, :], cb.Qh[64:128, pj],
                                cb.Kb[64:128, pj], start=True, stop=True)
            deps = [pr["gates"]] + ([ev3_h[jp]] if ev3_h[jp] else [])
            s3 = sch.step("tensor", s3f, deps=deps)
            del deps

            def ev3f(v, ps3e=ps3e, ps3o=ps3o, s3m=s3m):
                v.tensor_tensor(out=s3m[:, 0:128], in0=ps3e[:, :],
                                in1=cm["m_str"][:, :], op=ALU.mult)
                return v.tensor_tensor(out=s3m[:, 128:256], in0=ps3o[:, :],
                                       in1=cm["m_str"][:, :], op=ALU.mult)
            ev3_deps = [s3] + ([wl_h[jp]] if wl_h[jp] else [])
            ev3 = sch.step("vector", ev3f, deps=ev3_deps)
            ev3_h[jp] = ev3
            if lvl < 4:
                continue

            def wlf(t, j=j, pj=pj, cb=cb, pW=pW, s3m=s3m):
                for hh, hb in ((2 * j, 0), (2 * j + 1, 64)):
                    t.matmul(pW[:, (hh % 2) * 64:(hh % 2) * 64 + 64],
                             cb.Kb[hb:hb + 64, pj],
                             LC_b[hb:hb + 64, j * 64:(j + 1) * 64],
                             start=True, stop=False)
                    last = t.matmul(pW[:, (hh % 2) * 64:(hh % 2) * 64 + 64],
                                    s3m[:, (hb // 64) * 128:(hb // 64) * 128 + 128],
                                    cb.vt[:, hh * 64:(hh + 1) * 64],
                                    start=False, stop=True)
                return last
            wl = sch.step("tensor", wlf,
                          deps=[ev3, upd] + ([ew_h[jp]] if ew_h[jp] else []))
            wl_steps.append(wl)
            wl_h[jp] = wl

            def ewf(s, j=j, pW=pW):
                return s.copy(out=Wl_b[:, j * 128:(j + 1) * 128],
                              in_=pW[:, :])
            ew = sch.step("scalar", ewf, deps=[wl])
            ew_h[jp] = ew

            def ytf(t, j=j, cb=cb):
                for hh, hb in ((2 * j, 0), (2 * j + 1, 64)):
                    last = t.matmul(pB[hb:hb + 64, j * 64:(j + 1) * 64],
                                    cb.Kbtm[:, hh * 64:(hh + 1) * 64],
                                    cb.Utm[:, hh * 64:(hh + 1) * 64],
                                    start=True, stop=True)
                return last
            yt = sch.step("tensor", ytf, deps=[tms["Ut"], tms["Kb"], prev_yt])

            def eyf(v, j=j):
                return v.tensor_copy(out=Yt_b[:, j * 64:(j + 1) * 64],
                                     in_=pB[:, j * 64:(j + 1) * 64])
            ey = sch.step("vector", eyf, deps=[yt])
            yt_evs.append(ey)
            prev_yt = ey

            def uuf(t, j=j, cb=cb):
                for hh, hb in ((2 * j, 0), (2 * j + 1, 64)):
                    last = t.matmul(pUU[hb:hb + 64, j * 64:(j + 1) * 64],
                                    cb.Utm[:, hh * 64:(hh + 1) * 64],
                                    cb.Utm[:, hh * 64:(hh + 1) * 64],
                                    start=True, stop=True)
                return last
            uu_steps.append(sch.step("tensor", uuf, deps=[tms["Ut"], upd]))

            def qvf(t, j=j, cb=cb):
                for hh, hb in ((2 * j, 0), (2 * j + 1, 64)):
                    last = t.matmul(pQV[hb:hb + 64, j * 64:(j + 1) * 64],
                                    cb.Qhtm[:, hh * 64:(hh + 1) * 64],
                                    cb.vt[:, hh * 64:(hh + 1) * 64],
                                    start=True, stop=True)
                return last
            qv_steps.append(sch.step("tensor", qvf, deps=[tms["Qh"], upd]))

            def ggf(t, j=j, cb=cb):
                for hh, hb in ((2 * j, 0), (2 * j + 1, 64)):
                    last = t.matmul(pG[hb:hb + 64, j * 64:(j + 1) * 64],
                                    cb.Utm[:, hh * 64:(hh + 1) * 64],
                                    Wl_b[:, hh * 64:(hh + 1) * 64],
                                    start=True, stop=True)
                return last
            gg_steps.append(sch.step("tensor", ggf, deps=[ew, tms["Ut"], upd]))

        if lvl < 4:
            upd = ev3_h[(NP - 1) % 2]
            upd_hist[c % 2] = upd
            continue
        if lvl < 5:
            upd = sch.step("vector",
                           lambda v: v.tensor_copy(out=cm["tmpa"][:, :],
                                                   in_=pUU[:, :]),
                           deps=uu_steps + qv_steps + gg_steps + yt_evs)
            upd_hist[c % 2] = upd
            continue
        def updf(v, cb=cb):
            r8 = lambda x: x[:, :].rearrange("p (a b) -> p a b", b=64)
            v.tensor_tensor(out=cm["tmpa"][:, :], in0=pG[:, :], in1=LGm[:, :],
                            op=ALU.add)
            v.drain()
            v.tensor_tensor(out=r8(LGm), in0=r8(cm["tmpa"]), in1=rowbc(cb.plK),
                            op=ALU.mult)
            v.drain()
            v.tensor_tensor(out=r8(cm["tmpa"]), in0=r8(Yt_b), in1=rowbc(Em),
                            op=ALU.mult)
            v.drain()
            v.tensor_tensor(out=cm["tmpb"][:, :], in0=Xtm[:, :], in1=cm["tmpa"][:, :],
                            op=ALU.add)
            v.drain()
            v.tensor_tensor(out=Xtm[:, :], in0=cm["tmpb"][:, :], in1=cb.plbc[:, :],
                            op=ALU.mult)
            v.drain()
            v.tensor_tensor(out=cm["tmpa"][:, :], in0=pUU[:, :], in1=LSm[:, :],
                            op=ALU.add)
            v.drain()
            v.tensor_tensor(out=r8(cm["tmpb"]), in0=r8(cm["tmpa"]), in1=rowbc(cb.plK),
                            op=ALU.mult)
            v.drain()
            v.tensor_tensor(out=LSm[:, :], in0=cm["tmpb"][:, :], in1=cb.plbc[:, :],
                            op=ALU.mult)
            v.drain()
            v.tensor_tensor(out=cm["tmpa"][:, :], in0=pQV[:, :], in1=LCm[:, :],
                            op=ALU.add)
            v.drain()
            v.tensor_tensor(out=r8(LCm), in0=r8(cm["tmpa"]), in1=rowbc(cb.rlE),
                            op=ALU.mult)
            v.tensor_mul(out=Dm[:, :], in0=Dm[:, :], in1=cb.plK[:, :])
            v.tensor_mul(out=Em[:, :], in0=Em[:, :], in1=cb.rlE[:, :])
            v.drain()
            return v.tensor_copy(out=LC_b[:, :], in_=LCm[:, :])
        upd = sch.step("vector", updf,
                       deps=gg_steps + uu_steps + qv_steps + yt_evs + [plb])
        upd_hist[c % 2] = upd
        if co:
            co.pop(0)()

    while co:
        co.pop(0)()

    def mo(sy, sem):
        sy.dma_start(out=maps_o[:, 0:512], in_=LSm[:, :]).then_inc(sem, 16)
        sy.dma_start(out=maps_o[:, 512:1024], in_=LCm[:, :]).then_inc(sem, 16)
        sy.dma_start(out=maps_o[:, 1024:1536], in_=Xtm[:, :]).then_inc(sem, 16)
        sy.dma_start(out=maps_o[:, 1536:2048], in_=LGm[:, :]).then_inc(sem, 16)
        sy.dma_start(out=maps_o[:, 2048:2056], in_=Dm[:, :]).then_inc(sem, 16)
        sy.dma_start(out=maps_o[:, 2056:2064], in_=Em[:, :]).then_inc(sem, 16)
    fin = sch.dma("sync", mo, 6, deps=[upd])
    sch.step("gpsimd", lambda g: g.engine_nop() if hasattr(g, "engine_nop")
             else g.memset(cm["cpool"][:, 3:4], 0.0), deps=[fin])
    return fin


def build_phase1(stages=("xt", "proj", "sweep1"), s1_lvl=9, s1_nch=NCH):
    nc = bass.Bass("TRN2", target_bir_lowering=False, debug=False)
    sch = Sched(nc)
    x_d = nc.dram_tensor("x", [TC, DM], F32, kind="ExternalInput").ap()
    wd = {n: nc.dram_tensor(n, [DM, DM], F32, kind="ExternalInput").ap()
          for n in ("W_q", "W_k", "W_v", "Wg_K", "Wg_C")}
    wd["bg_K"] = nc.dram_tensor("bg_K", [1, DM], F32, kind="ExternalInput").ap()
    wd["bg_C"] = nc.dram_tensor("bg_C", [1, DM], F32, kind="ExternalInput").ap()

    q_o = nc.dram_tensor("q_s", [NCH, 128, FC * L], BF16,
                         kind="ExternalOutput").ap()
    k_o = nc.dram_tensor("k_s", [NCH, 128, FC * L], BF16,
                         kind="ExternalOutput").ap()
    v_o = nc.dram_tensor("v_s", [TC, DM], BF16, kind="ExternalOutput").ap()
    sK_o = nc.dram_tensor("spK_s", [NCH, 128, FC * L], FP16,
                          kind="ExternalOutput").ap()
    sC_o = nc.dram_tensor("spC_s", [NCH, 128, FC * L], FP16,
                          kind="ExternalOutput").ap()
    maps_o = nc.dram_tensor("maps", [128, 2064], F32, kind="ExternalOutput").ap()

    xt_dbg = nc.dram_tensor("xt_dbg", [128, FC, TC], BF16,
                            kind="ExternalOutput").ap()
    es = ExitStack()
    with es:
        cm = alloc_common(nc, es, sch)
        xT = es.enter_context(nc.sbuf_tensor("xT", [128, FC, TC], BF16))
        xt_done = emit_xT(nc, es, sch, cm, x_d, xT)
        last = xt_done
        if "proj" in stages:
            outs = {"q_s": q_o, "k_s": k_o, "spK_s": sK_o, "spC_s": sC_o,
                    "v_s": v_o}
            which = [st for st in ("q", "k", "gK", "gC", "v")
                     if "no_" + st not in stages]
            last = emit_projections(nc, es, sch, cm, xT, xt_done, wd, outs,
                                    which)
        if "sweep1" in stages:
            last = emit_sweep1(nc, es, sch, cm, (q_o, k_o, v_o, sK_o, sC_o),
                               maps_o, last, lvl=s1_lvl, nch=s1_nch)

        def xdbg(sy, sem):
            sy.dma_start(out=xt_dbg[:, :, :], in_=xT[:, :, :]).then_inc(sem, 16)
        fin = sch.dma("sync", xdbg, 1,
                      deps=(last if isinstance(last, list) else [last]) +
                      [xt_done])
        sch.step("gpsimd", lambda g: g.memset(cm["cpool"][:, 3:4], 0.0),
                 deps=[fin])
        sch.run(es)
    return nc


def emit_compose(nc, es, sch, cm, maps_d, gates_d, st, start_dep):
    sb = lambda n, s, dt: es.enter_context(nc.sbuf_tensor(n, s, dt))
    mapb = [sb("mapb0", [128, 2064], F32), sb("mapb1", [128, 2064], F32)]
    gt = sb("gt", [128, 8], F32)
    Dk = [sb("Dk0", [128, 8], F32), sb("Dk1", [128, 8], F32)]
    Ek = [sb("Ek0", [128, 8], F32), sb("Ek1", [128, 8], F32)]
    mXtb = [sb("mXtb0", [128, 512], BF16), sb("mXtb1", [128, 512], BF16)]
    C_b = sb("C_bc", [128, 512], BF16)
    dbc_r = [sb("dbc_r0", [128, 512], BF16), sb("dbc_r1", [128, 512], BF16)]
    tmpa = sb("tmpac", [128, 512], F32)
    tmpb = sb("tmpbc", [128, 512], F32)
    pXC = cm["ps_c"]
    pB = cm["ps_f"]

    def initf(g):
        for b in (st["S0f"], st["C0f"], st["Gnf"]):
            g.memset(b[:, :], 0.0)
        return g.memset(C_b[:, :], 0.0)
    init = sch.step("gpsimd", initf, deps=[start_dep])

    def gld(sy, sem):
        sy.dma_start(out=gt[:, :], in_=gates_d[:, :]).then_inc(sem, 16)
    gl = sch.dma("sync", gld, 1)

    fold_h = [init, init]
    dm_h = [init, init]
    upd = init

    def emit_mlbl(k):
        par = k % 2

        def mld(sy, sem, k=k, par=par):
            sy.dma_start(out=mapb[par][:, :],
                         in_=maps_d[k, :, :]).then_inc(sem, 16)
        ml = sch.dma("sync", mld, 1, deps=[fold_h[par]])

        def blendf(v, k=k, par=par):
            g_ap = gt[:, k:k + 1]
            v.tensor_scalar(out=Dk[par][:, :], in0=mapb[par][:, 2048:2056],
                            scalar1=1.0, scalar2=g_ap,
                            op0=ALU.subtract, op1=ALU.mult)
            v.tensor_scalar(out=Ek[par][:, :], in0=mapb[par][:, 2056:2064],
                            scalar1=1.0, scalar2=g_ap,
                            op0=ALU.subtract, op1=ALU.mult)
            v.drain()
            v.tensor_scalar_add(out=Dk[par][:, :], in0=Dk[par][:, :],
                                scalar1=1.0)
            v.tensor_scalar_add(out=Ek[par][:, :], in0=Ek[par][:, :],
                                scalar1=1.0)
            v.tensor_scalar(out=mapb[par][:, 0:2048],
                            in0=mapb[par][:, 0:2048],
                            scalar1=g_ap, scalar2=None, op0=ALU.mult)
            v.drain()
            v.tensor_copy(out=mXtb[par][:, :], in_=mapb[par][:, 1024:1536])
            v.tensor_tensor(
                out=dbc_r[par][0:64, :].rearrange("p (j k) -> p j k", k=64),
                in0=Dk[par][0:64, :].unsqueeze(2).broadcast_to([64, 8, 64]),
                in1=cm["idm"][0:64, 0:64].unsqueeze(1).broadcast_to([64, 8, 64]),
                op=ALU.mult)
            return v.tensor_tensor(
                out=dbc_r[par][64:128, :].rearrange("p (j k) -> p j k", k=64),
                in0=Dk[par][64:128, :].unsqueeze(2).broadcast_to([64, 8, 64]),
                in1=cm["idm"][64:128, 64:128].unsqueeze(1)
                .broadcast_to([64, 8, 64]),
                op=ALU.mult)
        par = k % 2
        bl = sch.step("vector", blendf,
                      deps=[ml, gl, cm["ready"], fold_h[par], dm_h[par]])
        return bl

    bl = emit_mlbl(0)
    for k in range(NCORE - 1):
        par = k % 2

        def dmxcf(t, par=par):
            t.matmul(pB[0:64, :], cm["ones64"][0:64, :], dbc_r[par][0:64, :],
                     start=True, stop=True)
            t.matmul(pB[64:128, :], cm["ones64"][64:128, :],
                     dbc_r[par][64:128, :], start=True, stop=True)
            for j in range(NP):
                for hb in (0, 64):
                    last = t.matmul(pXC[hb:hb + 64, j * 64:(j + 1) * 64],
                                    mXtb[par][hb:hb + 64, j * 64:(j + 1) * 64],
                                    C_b[hb:hb + 64, j * 64:(j + 1) * 64],
                                    start=True, stop=True)
            return last
        dmxc = sch.step("tensor", dmxcf, deps=[bl, upd])
        dm_h[par] = dmxc
        if k < NCORE - 2:
            bl = emit_mlbl(k + 1)

        def foldf(v, k=k, par=par):
            r8 = lambda x: x[:, :].rearrange("p (a b) -> p a b", b=64)
            v.tensor_tensor(out=r8(tmpa), in0=r8(st["Gnf"]), in1=rowbc(Dk[par]),
                            op=ALU.mult)
            v.drain()
            v.tensor_tensor(out=tmpb[:, :], in0=tmpa[:, :], in1=pXC[:, :],
                            op=ALU.subtract)
            v.drain()
            v.tensor_tensor(out=st["Gnf"][:, :], in0=tmpb[:, :],
                            in1=mapb[par][:, 1536:2048], op=ALU.subtract)
            v.drain()
            v.tensor_tensor(out=r8(tmpa), in0=r8(st["S0f"]), in1=rowbc(Dk[par]),
                            op=ALU.mult)
            v.drain()
            v.tensor_tensor(out=tmpb[:, :], in0=tmpa[:, :], in1=pB[:, :],
                            op=ALU.mult)
            v.drain()
            v.tensor_tensor(out=st["S0f"][:, :], in0=tmpb[:, :],
                            in1=mapb[par][:, 0:512], op=ALU.add)
            v.drain()
            v.tensor_tensor(out=r8(tmpa), in0=r8(st["C0f"]), in1=rowbc(Ek[par]),
                            op=ALU.mult)
            v.drain()
            v.tensor_tensor(out=st["C0f"][:, :], in0=tmpa[:, :],
                            in1=mapb[par][:, 512:1024], op=ALU.add)
            v.drain()
            return v.tensor_copy(out=C_b[:, :], in_=st["C0f"][:, :])
        upd = sch.step("vector", foldf, deps=[dmxc])
        fold_h[par] = upd

    def castf(v):
        v.tensor_copy(out=st["S0b"][:, :], in_=st["S0f"][:, :])
        v.tensor_copy(out=st["C0b"][:, :], in_=st["C0f"][:, :])
        v.tensor_copy(out=st["Gnb"][:, :], in_=st["Gnf"][:, :])
        return v.tensor_scalar_mul(out=st["Cnb"][:, :], in0=st["C0f"][:, :],
                                   scalar1=-1.0)
    return sch.step("vector", castf, deps=[upd])


def emit_sweep2(nc, es, sch, cm, srcs, st, OTs, start_dep):
    cbs = [ChunkBufs(nc, es, "_s2a", with_sweep2=True),
           ChunkBufs(nc, es, "_s2b", with_sweep2=True)]
    sb = lambda n, s, dt: es.enter_context(nc.sbuf_tensor(n, s, dt))
    s1m = sb("s1m_2", [128, 256], BF16)
    s2m = sb("s2m_2", [128, 256], BF16)
    s3m = sb("s3m_2", [128, 256], BF16)
    rb = sb("rb_2", [128, 128], BF16)
    Wn_b = sb("Wn_b2", [128, DM], BF16)

    ptr = cm["ps_bf"]
    S1e = cm["ps_a"][:, 0:128]
    S3e = cm["ps_a"][:, 128:256]
    S2e = cm["ps_a"][:, 256:384]
    pR = cm["ps_a"][:, 384:512]
    S1o = cm["ps_b"][:, 512:640]
    S3o = cm["ps_b"][:, 640:768]
    S2o = cm["ps_b"][:, 768:896]
    pW = cm["ps_b"][:, 0:128]
    pOT = cm["ps_b"][:, 128:256]
    pUU = cm["ps_c"]
    pQV = cm["ps_d"]
    pGn = cm["ps_e"]
    pB = cm["ps_f"]

    upd = start_dep
    upd_hist = [start_dep, start_dep]
    prev = {"eva": None, "b": None, "ewn": None, "evr": None, "s2": None,
            "ev2": None, "otug": None, "evot": None, "tmev": None, "plb": None}
    for c in range(NCH):
        par = c % 2
        cb = cbs[par]
        pr = emit_chunk_prep(nc, sch, cm, cb, srcs, c, sweep2=True,
                             extra_dep=[upd_hist[par]])
        tms = {}
        for name, src, dst in (("Ut", cb.Ut, cb.Utm), ("Qh", cb.Qh, cb.Qhtm)):
            def tf(t, src=src):
                for j in range(NP):
                    last = t.transpose(ptr[:, j * 128:(j + 1) * 128],
                                       src[:, j * 128:(j + 1) * 128],
                                       cm["idm"][:, :])
                return last
            ts = sch.step("tensor", tf, deps=[pr["gates"], cm["ptrtok"][0]])

            def vf(s, dst=dst):
                return s.copy(out=dst[:, :], in_=ptr[:, :])
            prev_ev = sch.step("scalar", vf, deps=[ts])
            cm["ptrtok"][0] = prev_ev
            tms[name] = prev_ev
        plb = emit_plbcast(nc, sch, cm, cb, pB,
                           [pr["gates"]] + ([prev["plb"]] if prev["plb"]
                                            else [upd]))
        prev["plb"] = plb

        uqg_steps, ot_evs = [], []
        for j in range(NP):
            pj = slice(j * 128, (j + 1) * 128)

            def af(t, j=j, pj=pj, cb=cb):
                t.matmul(S1e[:, :], cb.Ut[0:64, pj], cb.Qb[0:64, pj],
                         start=True, stop=True)
                t.matmul(S1o[:, :], cb.Ut[64:128, pj], cb.Qb[64:128, pj],
                         start=True, stop=True)
                t.matmul(S3e[:, :], cb.Qh[0:64, pj], cb.Kb[0:64, pj],
                         start=True, stop=True)
                return t.matmul(S3o[:, :], cb.Qh[64:128, pj],
                                cb.Kb[64:128, pj], start=True, stop=True)
            a = sch.step("tensor", af, deps=[pr["gates"]] +
                         ([prev["eva"]] if prev["eva"] else []))

            def evaf(v):
                v.tensor_tensor(out=s1m[:, 0:128], in0=S1e[:, :],
                                in1=cm["m_incl"][:, :], op=ALU.mult)
                v.tensor_tensor(out=s1m[:, 128:256], in0=S1o[:, :],
                                in1=cm["m_incl"][:, :], op=ALU.mult)
                v.tensor_tensor(out=s3m[:, 0:128], in0=S3e[:, :],
                                in1=cm["m_strn"][:, :], op=ALU.mult)
                return v.tensor_tensor(out=s3m[:, 128:256], in0=S3o[:, :],
                                       in1=cm["m_strn"][:, :], op=ALU.mult)
            eva = sch.step("vector", evaf, deps=[a] +
                           ([prev["otug"]] if prev["otug"] else []))
            prev["eva"] = eva

            def bf(t, j=j, pj=pj, cb=cb):
                for hh, hb in ((2 * j, 0), (2 * j + 1, 64)):
                    msl = slice((hb // 64) * 128, (hb // 64) * 128 + 128)
                    t.matmul(pW[:, (hh % 2) * 64:(hh % 2) * 64 + 64],
                             cb.Kb[hb:hb + 64, pj],
                             st["Cnb"][hb:hb + 64, j * 64:(j + 1) * 64],
                             start=True, stop=False)
                    t.matmul(pW[:, (hh % 2) * 64:(hh % 2) * 64 + 64],
                             s3m[:, msl], cb.vt[:, hh * 64:(hh + 1) * 64],
                             start=False, stop=True)
                    t.matmul(pR[hb:hb + 64, :],
                             st["S0b"][hb:hb + 64, j * 64:(j + 1) * 64],
                             cb.Qb[hb:hb + 64, pj], start=True, stop=False)
                    last = t.matmul(pR[hb:hb + 64, :],
                                    cb.Utm[:, hh * 64:(hh + 1) * 64],
                                    s1m[:, msl], start=False, stop=True)
                return last
            b = sch.step("tensor", bf,
                         deps=[eva, tms["Ut"], upd] +
                         ([prev["ewn"]] if prev["ewn"] else []) +
                         ([prev["evr"]] if prev["evr"] else []))

            def ewnf(s, j=j):
                return s.copy(out=Wn_b[:, j * 128:(j + 1) * 128], in_=pW[:, :])
            ewn = sch.step("scalar", ewnf, deps=[b])
            prev["ewn"] = ewn

            def evrf(v, j=j, pj=pj, cb=cb):
                return v.tensor_tensor(out=rb[:, :], in0=pR[:, :],
                                       in1=cb.PRg[:, pj], op=ALU.mult)
            evr = sch.step("vector", evrf, deps=[b, pr["gates"]] +
                           ([prev["otug"]] if prev["otug"] else []))
            prev["evr"] = evr

            def s2f(t, pj=pj, cb=cb):
                t.matmul(S2e[:, :], cb.Qh[0:64, pj], rb[0:64, :],
                         start=True, stop=True)
                return t.matmul(S2o[:, :], cb.Qh[64:128, pj], rb[64:128, :],
                                start=True, stop=True)
            s2 = sch.step("tensor", s2f, deps=[evr] +
                          ([prev["ev2"]] if prev["ev2"] else []))
            prev["s2"] = s2

            def ev2f(v):
                v.tensor_tensor(out=s2m[:, 0:128], in0=S2e[:, :],
                                in1=cm["m_incl"][:, :], op=ALU.mult)
                return v.tensor_tensor(out=s2m[:, 128:256], in0=S2o[:, :],
                                       in1=cm["m_incl"][:, :], op=ALU.mult)
            ev2 = sch.step("vector", ev2f, deps=[s2] +
                           ([prev["otug"]] if prev["otug"] else []))
            prev["ev2"] = ev2

            def otugf(t, j=j, pj=pj, cb=cb):
                for hh, hb in ((2 * j, 0), (2 * j + 1, 64)):
                    sl64 = slice(j * 64, (j + 1) * 64)
                    msl = slice((hb // 64) * 128, (hb // 64) * 128 + 128)
                    t.matmul(pOT[hb:hb + 64, :], st["C0b"][hb:hb + 64, sl64],
                             rb[hb:hb + 64, :], start=True, stop=False)
                    t.matmul(pOT[hb:hb + 64, :],
                             cb.vt[:, hh * 64:(hh + 1) * 64], s2m[:, msl],
                             start=False, stop=False)
                    t.matmul(pOT[hb:hb + 64, :], st["Gnb"][hb:hb + 64, sl64],
                             cb.Qb[hb:hb + 64, pj], start=False, stop=False)
                    t.matmul(pOT[hb:hb + 64, :],
                             Wn_b[:, hh * 64:(hh + 1) * 64], s1m[:, msl],
                             start=False, stop=True)
                    t.matmul(pUU[hb:hb + 64, j * 64:(j + 1) * 64],
                             cb.Utm[:, hh * 64:(hh + 1) * 64],
                             cb.Utm[:, hh * 64:(hh + 1) * 64],
                             start=True, stop=True)
                    t.matmul(pQV[hb:hb + 64, j * 64:(j + 1) * 64],
                             cb.Qhtm[:, hh * 64:(hh + 1) * 64],
                             cb.vt[:, hh * 64:(hh + 1) * 64],
                             start=True, stop=True)
                    last = t.matmul(pGn[hb:hb + 64, j * 64:(j + 1) * 64],
                                    cb.Utm[:, hh * 64:(hh + 1) * 64],
                                    Wn_b[:, hh * 64:(hh + 1) * 64],
                                    start=True, stop=True)
                return last
            otug = sch.step("tensor", otugf,
                            deps=[eva, ev2, ewn, evr, tms["Ut"], tms["Qh"],
                                  upd] +
                            ([prev["evot"]] if prev["evot"] else []))
            prev["otug"] = otug
            uqg_steps.append(otug)

            def evotf(v, j=j, c=c):
                return v.tensor_copy(out=OTs[:, j, c * L:(c + 1) * L],
                                     in_=pOT[:, :])
            evot = sch.step("vector", evotf, deps=[otug])
            prev["evot"] = evot
            ot_evs.append(evot)

        def updf(v, cb=cb):
            r8 = lambda x: x[:, :].rearrange("p (a b) -> p a b", b=64)
            v.tensor_tensor(out=cm["tmpa"][:, :], in0=pUU[:, :], in1=st["S0f"][:, :],
                            op=ALU.add)
            v.drain()
            v.tensor_tensor(out=r8(cm["tmpb"]), in0=r8(cm["tmpa"]), in1=rowbc(cb.plK),
                            op=ALU.mult)
            v.drain()
            v.tensor_tensor(out=st["S0f"][:, :], in0=cm["tmpb"][:, :],
                            in1=cb.plbc[:, :], op=ALU.mult)
            v.drain()
            v.tensor_tensor(out=cm["tmpa"][:, :], in0=pQV[:, :], in1=st["C0f"][:, :],
                            op=ALU.add)
            v.drain()
            v.tensor_tensor(out=r8(st["C0f"]), in0=r8(cm["tmpa"]), in1=rowbc(cb.rlE),
                            op=ALU.mult)
            v.drain()
            v.tensor_tensor(out=cm["tmpa"][:, :], in0=pGn[:, :], in1=st["Gnf"][:, :],
                            op=ALU.add)
            v.drain()
            v.tensor_tensor(out=r8(st["Gnf"]), in0=r8(cm["tmpa"]), in1=rowbc(cb.plK),
                            op=ALU.mult)
            v.drain()
            v.tensor_copy(out=st["S0b"][:, :], in_=st["S0f"][:, :])
            v.tensor_copy(out=st["C0b"][:, :], in_=st["C0f"][:, :])
            v.tensor_copy(out=st["Gnb"][:, :], in_=st["Gnf"][:, :])
            return v.tensor_scalar_mul(out=st["Cnb"][:, :], in0=st["C0f"][:, :],
                                       scalar1=-1.0)
        upd = sch.step("vector", updf, deps=uqg_steps + ot_evs + [plb, upd])
        upd_hist[par] = upd
    return upd


def emit_quant_sg(nc, sch, cm, qx, ws_f32, W, deps):
    def sgf(s):
        s.activation(out=qx["xa"][:, 0:W], in_=ws_f32[:, 0:W], func=AF.Abs)
        return s.activation(out=qx["sgb"][:, 0:W], in_=ws_f32[:, 0:W],
                            func=AF.Sign)
    return sch.step("scalar", sgf, deps=deps)


def emit_quant(nc, sch, cm, qb, ws_f32, W, out_bf16, deps, prefix=None):
    sg = emit_quant_sg(nc, sch, cm, qb, ws_f32, W, deps)
    return emit_quant_qv(nc, sch, cm, qb, qb, ws_f32, W, out_bf16,
                         list(deps) + [sg], prefix=prefix)


def emit_quant_qv(nc, sch, cm, qb, qx, ws_f32, W, out_bf16, deps,
                  prefix=None):
    NB = W // 16

    def vf(v):
        if prefix is not None:
            prefix(v)
            v.drain()
        v.tensor_reduce(out=qb["bm"][:, 0:NB],
                        in_=ws_f32[:, 0:W].rearrange("p (b w) -> p b w", w=16),
                        axis=mybir.AxisListType.X, op=ALU.max,
                        apply_absolute_value=True)
        v.drain()
        v.tensor_scalar(out=qb["sc"][:, 0:NB], in0=qb["bm"][:, 0:NB],
                        scalar1=1.0 / 6.0, scalar2=1e-12, op0=ALU.mult,
                        op1=ALU.max)
        v.drain()
        v.reciprocal(out=qb["ri"][:, 0:NB], in_=qb["sc"][:, 0:NB])
        v.drain()
        v.tensor_tensor(
            out=qb["tn"][:, 0:W].rearrange("p (b w) -> p b w", w=16),
            in0=qx["xa"][:, 0:W].rearrange("p (b w) -> p b w", w=16),
            in1=qb["ri"][:, 0:NB].unsqueeze(2).broadcast_to([128, NB, 16]),
            op=ALU.mult)
        v.drain()
        v.tensor_scalar(out=qb["cqA"][:, 0:W], in0=qb["tn"][:, 0:W],
                        scalar1=0.25, scalar2=None, op0=ALU.is_gt)
        v.tensor_scalar(out=qb["cqB"][:, 0:W], in0=qb["tn"][:, 0:W],
                        scalar1=2.5, scalar2=None, op0=ALU.is_gt)
        v.drain()
        v.scalar_tensor_tensor(out=qb["cqA"][:, 0:W], in0=qb["tn"][:, 0:W],
                               scalar=0.75, in1=qb["cqA"][:, 0:W],
                               op0=ALU.is_gt, op1=ALU.add)
        v.scalar_tensor_tensor(out=qb["cqB"][:, 0:W], in0=qb["tn"][:, 0:W],
                               scalar=3.5, in1=qb["cqB"][:, 0:W],
                               op0=ALU.is_gt, op1=ALU.add)
        v.drain()
        v.scalar_tensor_tensor(out=qb["cqA"][:, 0:W], in0=qb["tn"][:, 0:W],
                               scalar=1.25, in1=qb["cqA"][:, 0:W],
                               op0=ALU.is_gt, op1=ALU.add)
        v.drain()
        v.scalar_tensor_tensor(out=qb["cqA"][:, 0:W], in0=qb["tn"][:, 0:W],
                               scalar=1.75, in1=qb["cqA"][:, 0:W],
                               op0=ALU.is_gt, op1=ALU.add)
        v.drain()
        v.scalar_tensor_tensor(out=qb["cqB"][:, 0:W], in0=qb["cqA"][:, 0:W],
                               scalar=0.5, in1=qb["cqB"][:, 0:W],
                               op0=ALU.mult, op1=ALU.add)
        v.drain()
        v.tensor_scalar(out=qb["cqA"][:, 0:W], in0=qb["tn"][:, 0:W],
                        scalar1=5.0, scalar2=2.0, op0=ALU.is_gt, op1=ALU.mult)
        v.drain()
        v.tensor_tensor(out=qb["cqA"][:, 0:W], in0=qb["cqA"][:, 0:W],
                        in1=qb["cqB"][:, 0:W], op=ALU.add)
        v.drain()
        v.tensor_tensor(out=qb["cqA"][:, 0:W], in0=qb["cqA"][:, 0:W],
                        in1=qx["sgb"][:, 0:W], op=ALU.mult)
        v.drain()
        return v.tensor_tensor(
            out=out_bf16[:, 0:W].rearrange("p (b w) -> p b w", w=16),
            in0=qb["cqA"][:, 0:W].rearrange("p (b w) -> p b w", w=16),
            in1=qb["sc"][:, 0:NB].unsqueeze(2).broadcast_to([128, NB, 16]),
            op=ALU.mult)
    return sch.step("vector", vf, deps=list(deps))


def alloc_qb(nc, es, tag, W):
    sb = lambda n, s, dt: es.enter_context(nc.sbuf_tensor(n + tag, s, dt))
    return {"xa": sb("q_xa", [128, W], F32), "bm": sb("q_bm", [128, W // 16], F32),
            "sc": sb("q_sc", [128, W // 16], F32),
            "ri": sb("q_ri", [128, W // 16], F32),
            "tn": sb("q_tn", [128, W], F32), "cqA": sb("q_cqA", [128, W], BF16),
            "cqB": sb("q_cqB", [128, W], BF16), "sgb": sb("q_sgb", [128, W], BF16)}


def emit_ln(nc, sch, cm, lb, psum_in, xres, out_f32, deps, sq_war=()):
    def v1(v):
        if isinstance(psum_in, (list, tuple)):
            v.tensor_tensor(out=lb["uf"][:, 0:512], in0=psum_in[0][:, :],
                            in1=xres[:, 0:512], op=ALU.add)
            v.tensor_tensor(out=lb["uf"][:, 512:1024], in0=psum_in[1][:, :],
                            in1=xres[:, 512:1024], op=ALU.add)
        else:
            v.tensor_tensor(out=lb["uf"][:, :], in0=psum_in[:, :],
                            in1=xres[:, :], op=ALU.add)
        v.drain()
        return v.reduce_sum(out=lb["st4"][:, 0:1], in_=lb["uf"][:, :],
                            axis=mybir.AxisListType.X)
    s1 = sch.step("vector", v1, deps=deps)

    def sq(s):
        return s.activation(out=lb["sq"][:, :], in_=lb["uf"][:, :],
                            func=AF.Square)
    s2 = sch.step("scalar", sq, deps=[s1] + list(sq_war))

    def v2(v):
        v.reduce_sum(out=lb["st4"][:, 1:2], in_=lb["sq"][:, :],
                     axis=mybir.AxisListType.X)
        v.drain()
        v.tensor_scalar_mul(out=lb["st4"][:, 0:2], in0=lb["st4"][:, 0:2],
                            scalar1=1.0 / DM)
        v.drain()
        v.tensor_tensor(out=lb["st4"][:, 2:3], in0=lb["st4"][:, 0:1],
                        in1=lb["st4"][:, 0:1], op=ALU.mult)
        v.drain()
        return v.tensor_tensor(out=lb["st4"][:, 3:4], in0=lb["st4"][:, 1:2],
                               in1=lb["st4"][:, 2:3], op=ALU.subtract)
    s3 = sch.step("vector", v2, deps=[s2])

    def s4f(s):
        s.activation(out=lb["st4"][:, 2:3], in_=lb["st4"][:, 3:4], func=AF.Ln,
                     bias=EPS)
        s.drain()
        return s.activation(out=lb["st4"][:, 3:4], in_=lb["st4"][:, 2:3],
                            func=AF.Exp, scale=-0.5)
    s4 = sch.step("scalar", s4f, deps=[s3])

    def v3(v):
        return v.tensor_scalar(out=out_f32[:, :], in0=lb["uf"][:, :],
                               scalar1=lb["st4"][:, 0:1],
                               scalar2=lb["st4"][:, 3:4],
                               op0=ALU.subtract, op1=ALU.mult)
    return sch.step("vector", v3, deps=[s4]), s1, s3




def emit_wtr(nc, sch, cm, ptr, wq, qds, evhist, tras, slabs, i):
    wdram, dst, sl, Wrow = slabs[i]
    par = i % 2
    ntile = Wrow // 128
    prev = qds[i]
    for g0 in range(0, ntile, 8):
        gn = min(8, ntile - g0)

        def trg(t, g0=g0, gn=gn, par=par):
            for it in range(gn):
                lastt = t.transpose(ptr[:, it * 128:(it + 1) * 128],
                                    wq[par][:, (g0 + it) * 128:
                                            (g0 + it + 1) * 128],
                                    cm["idm"][:, :])
            return lastt
        tg = sch.step("tensor", trg,
                      deps=[prev] + ([evhist[0]] if evhist[0] else []))

        def evg(v, g0=g0, gn=gn, dst=dst, sl=sl):
            return v.tensor_copy(
                out=dst[:, g0:g0 + gn, sl * 128:(sl + 1) * 128],
                in_=ptr[:, 0:gn * 128].rearrange("p (a b) -> p a b", b=128))
        prev = sch.step("vector", evg, deps=[tg])
        evhist[0] = prev
    tras[i] = prev


def emit_fdn(nc, sch, cm, ptr, hqc, hqT, wqdT, pDN, dn_mms, hist, ICH,
             hq, ici, i0, ilen, par, tc):
    war = lambda k, n=2: ([hist[k][-n]] if len(hist[k]) >= n else [])
    ntile = ilen // 128

    def htr(t, ntile=ntile, par=par):
        for it in range(ntile):
            lastt = t.transpose(
                ptr[:, (par * 4 + it) * 128:(par * 4 + it + 1) * 128],
                hqc[par][:, it * 128:(it + 1) * 128], cm["idm"][:, :])
        return lastt
    ht = sch.step("tensor", htr, deps=[hq] + war("hte"))
    hist["ht"].append(ht)

    def htef(s, ntile=ntile, par=par):
        return s.copy(
            out=hqT[par][:, 0:ntile, :].rearrange("p a b -> p (a b)"),
            in_=ptr[:, par * 512:par * 512 + ntile * 128])
    hte = sch.step("scalar", htef, deps=[ht] + war("dm", 1))
    hist["hte"].append(hte)

    def dmf(t, i0=i0, ntile=ntile, ici=ici, par=par, tc=tc):
        for it in range(ntile):
            git = i0 // 128 + it
            for n2 in range(2):
                last_i = t.matmul(
                    pDN[:, n2 * 512:(n2 + 1) * 512], hqT[par][:, it, :],
                    wqdT[:, git, n2 * 512:(n2 + 1) * 512],
                    start=(ici == 0 and it == 0),
                    stop=(ici == len(ICH) - 1 and it == ntile - 1))
        return last_i
    dm = sch.step("tensor", dmf, deps=[hte] + dn_mms[-1:] +
                  (hist["ln2s1"][-1:] if ici == 0 else []))
    dn_mms.append(dm)
    hist["dm"].append(dm)


def build_fused():
    nc = bass.Bass("TRN2", target_bir_lowering=False, debug=False)
    sch = Sched(nc)
    x_d = nc.dram_tensor("x", [TC, DM], F32, kind="ExternalInput").ap()
    wd = {n: nc.dram_tensor(n, [DM, DM], F32, kind="ExternalInput").ap()
          for n in ("W_q", "W_k", "W_v", "Wg_K", "Wg_C")}
    wd["bg_K"] = nc.dram_tensor("bg_K", [1, DM], F32, kind="ExternalInput").ap()
    wd["bg_C"] = nc.dram_tensor("bg_C", [1, DM], F32, kind="ExternalInput").ap()
    gates_d = nc.dram_tensor("gates", [128, 8], F32, kind="ExternalInput").ap()
    Wo_d = nc.dram_tensor("W_o", [DM, DM], F32, kind="ExternalInput").ap()
    y_d = nc.dram_tensor("y", [TC, DM], F32, kind="ExternalOutput").ap()

    q_s = nc.dram_tensor("q_s", [NCH, 128, FC * L], BF16, kind="Internal").ap()
    k_s = nc.dram_tensor("k_s", [NCH, 128, FC * L], BF16, kind="Internal").ap()
    v_s = nc.dram_tensor("v_s", [TC, DM], BF16, kind="Internal").ap()
    sK_s = nc.dram_tensor("spK_s", [NCH, 128, FC * L], FP16,
                          kind="Internal").ap()
    sC_s = nc.dram_tensor("spC_s", [NCH, 128, FC * L], FP16,
                          kind="Internal").ap()
    x1_d = nc.dram_tensor("x1_s", [TC, DM], F32, kind="Internal").ap()
    mb_in = nc.dram_tensor("mb_in", [128, 2064], F32, kind="Internal")
    mb_out = nc.dram_tensor("mb_out", [NCORE, 128, 2064], F32, kind="Internal")
    wg_my = nc.dram_tensor("wg_my", [384, DM], F32, kind="ExternalInput").ap()
    wu_my = nc.dram_tensor("wu_my", [384, DM], F32, kind="ExternalInput").ap()
    wdn_my = nc.dram_tensor("wdn_my", [128, DI], F32, kind="ExternalInput").ap()
    wqb_in = nc.dram_tensor("wqb_in", [128, 8960], BF16, kind="Internal")
    wqb_out = nc.dram_tensor("wqb_out", [NCORE, 128, 8960], BF16,
                             kind="Internal")

    es = ExitStack()
    with es:
        cm = alloc_common(nc, es, sch)

        qb = alloc_qb(nc, es, "_f", 1024)
        ws = [es.enter_context(nc.sbuf_tensor("ws0", [128, DM], F32)),
              es.enter_context(nc.sbuf_tensor("ws1", [128, DM], F32))]
        wq = [es.enter_context(nc.sbuf_tensor("wq0", [128, DM], BF16)),
              es.enter_context(nc.sbuf_tensor("wq1", [128, DM], BF16))]


        myslabs = [(wdn_my, 0, DI, 6144),
                   (wg_my, 0, DM, 0), (wu_my, 0, DM, 3072),
                   (wg_my, 1, DM, 1024), (wu_my, 1, DM, 4096),
                   (wg_my, 2, DM, 2048), (wu_my, 2, DM, 5120)]
        wctx = {"ld": [cm["ready"], cm["ready"]], "ev": [None],
                "tr": {}, "qd": [], "fins": [], "pi": 0}

        def co_piece(wdram, row, p0, plen, off):
            def go():
                par = wctx["pi"] % 2
                wctx["pi"] += 1

                def wl(sy, sem, wdram=wdram, row=row, p0=p0, plen=plen,
                       par=par):
                    sy.dma_start(out=ws[par][:, 0:plen],
                                 in_=wdram[row * 128:(row + 1) * 128,
                                           p0:p0 + plen]
                                 ).then_inc(sem, 16)
                ld = sch.dma("sync", wl, 1, deps=[wctx["ld"][par]])
                qd = emit_quant(nc, sch, cm, qb, ws[par][:, 0:plen],
                                plen, wq[par][:, 0:plen],
                                [ld] + wctx["qd"][-1:])
                wctx["qd"].append(qd)
                wctx["ld"][par] = qd
                ntile = plen // 128
                sview = wqstage[:, off + p0:off + p0 + plen].rearrange(
                    "p (a b) -> p a b", b=128)

                def trg(t, ntile=ntile, par=par):
                    for it in range(ntile):
                        lastt = t.transpose(
                            cm["ps_bf"][:, it * 128:(it + 1) * 128],
                            wq[par][:, it * 128:(it + 1) * 128],
                            cm["idm"][:, :])
                    return lastt
                tg = sch.step("tensor", trg,
                              deps=[qd, cm["ptrtok"][0]])

                def evg(v, ntile=ntile, sview=sview):
                    return v.tensor_copy(
                        out=sview[:, :, :],
                        in_=cm["ps_bf"][:, 0:ntile * 128]
                        .rearrange("p (a b) -> p a b", b=128))
                prev = sch.step("vector", evg, deps=[tg])
                cm["ptrtok"][0] = prev
                wctx["ev"][0] = prev
                wctx["fins"].append(prev)
            return go

        co = []
        for i in range(len(myslabs)):
            wdram, row, Wrow, off = myslabs[i]
            for p0 in range(0, Wrow, 512):
                plen = min(512, Wrow - p0)
                co.append(co_piece(wdram, row, p0, plen, off))

        esA = ExitStack()
        with esA:
            wqstage = esA.enter_context(
                nc.sbuf_tensor("wqstage", [128, 8960], BF16))
            xT = esA.enter_context(nc.sbuf_tensor("xT", [128, FC, TC], BF16))
            xt_done = emit_xT(nc, esA, sch, cm, x_d, xT)
            outs = {"q_s": q_s, "k_s": k_s, "spK_s": sK_s, "spC_s": sC_s,
                    "v_s": v_s}
            pj_done = emit_projections(nc, esA, sch, cm, xT, xt_done, wd, outs)
            s1_fin = emit_sweep1(nc, esA, sch, cm,
                                 (q_s, k_s, v_s, sK_s, sC_s),
                                 mb_in.ap(), pj_done, co=co)

            def wst(sy, sem):
                sy.dma_start(out=wqb_in.ap()[:, :],
                             in_=wqstage[:, :]).then_inc(sem, 16)
            wst_d = sch.dma("sync", wst, 1, deps=wctx["fins"])

        def ccf(g):
            return g.collective_compute(
                "AllGather", ALU.bypass, replica_groups=[list(range(NCORE))],
                ins=[mb_in.ap().opt()], outs=[mb_out.ap().opt()])
        cc = sch.step("gpsimd", ccf, deps=[s1_fin, wst_d], coll=True)

        def ccw(g):
            return g.collective_compute(
                "AllGather", ALU.bypass, replica_groups=[list(range(NCORE))],
                ins=[wqb_in.ap().opt()], outs=[wqb_out.ap().opt()])
        ccw_s = sch.step("gpsimd", ccw, deps=[wst_d], coll=True)

        esB = ExitStack()
        with esB:
            sb1 = lambda n, s, dt: esB.enter_context(nc.sbuf_tensor(n, s, dt))
            st = {"S0f": sb1("S0f", [128, 512], F32),
                  "C0f": sb1("C0f", [128, 512], F32),
                  "Gnf": sb1("Gnf", [128, 512], F32),
                  "S0b": sb1("S0b", [128, 512], BF16),
                  "C0b": sb1("C0b", [128, 512], BF16),
                  "Cnb": sb1("Cnb", [128, 512], BF16),
                  "Gnb": sb1("Gnb", [128, 512], BF16)}
            OTs = sb1("OTs", [128, FC, TC], BF16)
            comp = emit_compose(nc, esB, sch, cm, mb_out.ap(), gates_d, st, cc)
            s2_done = emit_sweep2(nc, esB, sch, cm, (q_s, k_s, v_s, sK_s, sC_s),
                                  st, OTs, comp)
            Wo = sb1("Wo", [128, FC, DM], BF16)
            xrs = [qb["xa"], sb1("xr1", [128, DM], F32)]
            lbs = [{"uf": sb1("ln_uf", [128, DM], F32),
                    "sq": qb["tn"],
                    "st4": sb1("ln_st4", [128, 4], F32)},
                   {"uf": sb1("ln_uf1", [128, DM], F32),
                    "sq": qb["cqB"],
                    "st4": sb1("ln_st4b", [128, 4], F32)}]
            pLNs = [cm["ps_b"],
                    [cm["ps_c"], cm["ps_d"]]]

            def wol(gp, sem):
                gp.dma_start(out=Wo[:, :, :],
                             in_=Wo_d.rearrange("(a p) c -> p a c", p=128)
                             ).then_inc(sem, 16)
            wold = sch.dma("gpsimd", wol, 1, deps=[cc])
            s1h, s3h, xsoh = [None, None], [None, None], [None, None]
            xso_all = []
            last = s2_done
            for tc in range(NCH):
                par = tc % 2
                pLN = pLNs[par]

                def xrl(sy, sem, tc=tc, par=par):
                    sy.dma_start(out=xrs[par][:, :],
                                 in_=x_d[tc * 128:(tc + 1) * 128, :]
                                 ).then_inc(sem, 16)
                xl = sch.dma("sync", xrl, 1,
                             deps=([s1h[par]] if s1h[par] else [comp]))

                def mmf(t, tc=tc, pLN=pLN, par=par):
                    for n2 in range(2):
                        dst = (pLN[:, n2 * 512:(n2 + 1) * 512] if par == 0
                               else pLN[n2][:, :])
                        for fk in range(FC):
                            last_i = t.matmul(
                                dst,
                                OTs[:, fk, tc * 128:(tc + 1) * 128],
                                Wo[:, fk, n2 * 512:(n2 + 1) * 512],
                                start=(fk == 0), stop=(fk == FC - 1))
                    return last_i
                mm = sch.step("tensor", mmf, deps=[wold, s2_done] +
                              ([s1h[par]] if s1h[par] else []))
                ln, ls1, ls3 = emit_ln(
                    nc, sch, cm, lbs[par], pLN, xrs[par], lbs[par]["uf"],
                    [mm, xl] + ([xsoh[par]] if xsoh[par] else []),
                    sq_war=([s3h[par]] if s3h[par] else []))
                s1h[par], s3h[par] = ls1, ls3

                def xso(sy, sem, tc=tc, par=par):
                    sy.dma_start(out=x1_d[tc * 128:(tc + 1) * 128, :],
                                 in_=lbs[par]["uf"][:, :]).then_inc(sem, 16)
                last = sch.dma("sync", xso, 1, deps=[ln])
                xsoh[par] = last
                xso_all.append(last)
            wo_done = sch.step("gpsimd", lambda g: g.memset(
                cm["cpool"][:, 3:4], 0.0), deps=xso_all)

        esC = ExitStack()
        with esC:
            sb2 = lambda n, s, dt: esC.enter_context(nc.sbuf_tensor(n, s, dt))
            wqgT = sb2("wqgT", [128, FC, DI], BF16)
            wquT = sb2("wquT", [128, FC, DI], BF16)
            wqdT = sb2("wqdT", [128, DI // 128, DM], BF16)
            ptr = cm["ps_bf"]

            def wload_g(sy, sem):
                for sl in range(DI // 128):
                    core, k = (sl // 3, sl % 3) if sl < 21 else (7, 0)
                    sy.dma_start(
                        out=wqgT[:, :, sl * 128:(sl + 1) * 128],
                        in_=wqb_out.ap()[core, :, k * 1024:(k + 1) * 1024]
                        .rearrange("p (a b) -> p a b", b=128)
                    ).then_inc(sem, 16)
                for sl in range(DM // 256):
                    sy.dma_start(
                        out=wqdT[:, :, sl * 128:(sl + 1) * 128],
                        in_=wqb_out.ap()[sl, :, 6144:6144 + DI]
                        .rearrange("p (a b) -> p a b", b=128)
                    ).then_inc(sem, 16)

            def wload_u(gp, sem):
                for sl in range(DI // 128):
                    core, k = (sl // 3, sl % 3) if sl < 21 else (7, 0)
                    gp.dma_start(
                        out=wquT[:, :, sl * 128:(sl + 1) * 128],
                        in_=wqb_out.ap()[core, :, 3072 + k * 1024:
                                         3072 + (k + 1) * 1024]
                        .rearrange("p (a b) -> p a b", b=128)
                    ).then_inc(sem, 16)
                for sl in range(DM // 256, DM // 128):
                    gp.dma_start(
                        out=wqdT[:, :, sl * 128:(sl + 1) * 128],
                        in_=wqb_out.ap()[sl, :, 6144:6144 + DI]
                        .rearrange("p (a b) -> p a b", b=128)
                    ).then_inc(sem, 16)
            wq_done_g = sch.dma("sync", wload_g, 26, deps=[ccw_s, wo_done])
            wq_done_u = sch.dma("gpsimd", wload_u, 26, deps=[ccw_s, wo_done])
            wq_done2 = [wq_done_g, wq_done_u]

            x1b = [sb2("x1b0", [128, DM], F32), sb2("x1b1", [128, DM], F32)]
            x1q0 = sb2("x1q0", [128, DM], BF16)
            x1q = [x1q0, x1q0]
            xqT = [sb2("xqT0", [128, FC, 128], BF16),
                   sb2("xqT1", [128, FC, 128], BF16)]
            hg = [sb2("hg0", [128, 512], F32), sb2("hg1", [128, 512], F32)]
            hu = [sb2("hu0", [128, 512], F32), sb2("hu1", [128, 512], F32)]
            hh = [sb2("hh0", [128, 512], F32), sb2("hh1", [128, 512], F32)]
            hqc = [sb2("hqc0", [128, 512], BF16),
                   sb2("hqc1", [128, 512], BF16)]
            hqT0 = sb2("hqT0", [128, 4, 128], BF16)
            hqT = [hqT0, hqT0]
            qxh = [{"xa": sb2("xah0", [128, 512], F32),
                    "sgb": sb2("sgh0", [128, 512], BF16)},
                   {"xa": sb2("xah1", [128, 512], F32),
                    "sgb": sb2("sgh1", [128, 512], BF16)}]
            lb2 = {"uf": sb2("l2_uf", [128, DM], F32),
                   "sq": sb2("l2sq", [128, DM], BF16),
                   "st4": sb2("l2_st4", [128, 4], F32)}
            yc = lb2["uf"]
            pG2 = [cm["ps_c"], cm["ps_e"]]
            pU2 = [cm["ps_d"], cm["ps_f"]]
            pDN = cm["ps_b"]

            ICH = [(0, 512), (512, 512), (1024, 512), (1536, 512),
                   (2048, 512), (2560, 256)]
            hist = {"ge": [], "hf": [], "qv": [], "ht": [], "hte": [],
                    "dm": [], "ln2": [], "ln2s1": [], "ln2s3": [],
                    "qdq": [], "xt_all": [], "xte": [None, None],
                    "ydma": []}
            war = lambda k, n=2: ([hist[k][-n]] if len(hist[k]) >= n else [])

            def emit_x1(tc):
                tpar = tc % 2

                def x1l(sy, sem, tc=tc, tpar=tpar):
                    sy.dma_start(out=x1b[tpar][:, :],
                                 in_=x1_d[tc * 128:(tc + 1) * 128, :]
                                 ).then_inc(sem, 16)
                xl = sch.dma("sync", x1l, 1,
                             deps=[wo_done] + hist["ln2"][-1:] +
                             hist["qdq"][-2:][:1])
                qdeps = [xl] + wq_done2 + hist["qdq"][-1:]
                qdeps += hist["xt_all"][-1:]
                if hist["xte"][tpar]:
                    qdeps.append(hist["xte"][tpar])
                qsg = emit_quant_sg(nc, sch, cm, qb, x1b[tpar], DM, qdeps)
                qd = emit_quant_qv(nc, sch, cm, qb, qb, x1b[tpar], DM,
                                   x1q[tpar], qdeps + [qsg])
                hist["qdq"].append(qd)

                def xtr(t, tpar=tpar):
                    for fc in range(FC):
                        lastt = t.transpose(ptr[:, fc * 128:(fc + 1) * 128],
                                            x1q[tpar][:, fc * 128:(fc + 1) * 128],
                                            cm["idm"][:, :])
                    return lastt
                xt = sch.step("tensor", xtr,
                              deps=[qd, cm["ptrtok"][0]] + war("hte", 1))
                hist["xt_all"].append(xt)
                xte = sch.step("scalar", lambda s, tpar=tpar: s.copy(
                    out=xqT[tpar][:, :, :].rearrange("p a b -> p (a b)"),
                    in_=ptr[:, 0:1024]), deps=[xt])
                hist["xte"][tpar] = xte
                cm["ptrtok"][0] = xte
                return qd, xte

            cur = emit_x1(0)
            for tc in range(NCH):
                tpar = tc % 2
                qd, xte = cur

                dn_mms = []
                pend_sg = None
                pend_fdn = None

                def emit_qv(pq):
                    sg_p, ici_p, i0_p, ilen_p, par_p = pq
                    qv = emit_quant_qv(
                        nc, sch, cm, qb, qxh[par_p], hh[par_p], ilen_p,
                        hqc[par_p], [sg_p] + war("ht", 1))
                    hist["qv"].append(qv)
                    return (qv, ici_p, i0_p, ilen_p, par_p)

                def emit_f(pf):
                    qv_p, ici_p, i0_p, ilen_p, par_p = pf
                    emit_fdn(nc, sch, cm, ptr, hqc, hqT, wqdT, pDN,
                             dn_mms, hist, ICH, qv_p, ici_p, i0_p, ilen_p,
                             par_p, tc)

                for ici, (i0, ilen) in enumerate(ICH):
                    par = ici % 2
                    pg, pu = pG2[par], pU2[par]

                    def gmumf(t, i0=i0, ilen=ilen, pg=pg, pu=pu, tpar=tpar):
                        for fk in range(FC):
                            t.matmul(pg[:, 0:ilen], xqT[tpar][:, fk, :],
                                     wqgT[:, fk, i0:i0 + ilen],
                                     start=(fk == 0), stop=(fk == FC - 1))
                        for fk in range(FC):
                            last_i = t.matmul(pu[:, 0:ilen], xqT[tpar][:, fk, :],
                                              wquT[:, fk, i0:i0 + ilen],
                                              start=(fk == 0),
                                              stop=(fk == FC - 1))
                        return last_i
                    gm = sch.step("tensor", gmumf,
                                  deps=[xte] + wq_done2 + war("ge"))

                    def geuef(s, ilen=ilen, pg=pg, pu=pu, par=par):
                        s.activation(out=hg[par][:, 0:ilen], in_=pg[:, 0:ilen],
                                     func=AF.Silu)
                        return s.copy(out=hu[par][:, 0:ilen], in_=pu[:, 0:ilen])
                    ge = sch.step("scalar", geuef, deps=[gm] + war("hf"))
                    hist["ge"].append(ge)

                    def hff(v, ilen=ilen, par=par):
                        return v.tensor_tensor(out=hh[par][:, 0:ilen],
                                               in0=hg[par][:, 0:ilen],
                                               in1=hu[par][:, 0:ilen],
                                               op=ALU.mult)
                    hstep = sch.step("vector", hff,
                                     deps=[ge] + war("qv", 1))
                    hist["hf"].append(hstep)
                    sg = emit_quant_sg(nc, sch, cm, qxh[par], hh[par], ilen,
                                       [hstep] + war("qv", 1))
                    if pend_fdn is not None:
                        emit_f(pend_fdn)
                        pend_fdn = None
                    if pend_sg is not None:
                        pend_fdn = emit_qv(pend_sg)
                    pend_sg = (sg, ici, i0, ilen, par)
                if pend_fdn is not None:
                    emit_f(pend_fdn)
                emit_f(emit_qv(pend_sg))

                if tc + 1 < NCH:
                    cur = emit_x1(tc + 1)

                ln2, l2s1, l2s3 = emit_ln(
                    nc, sch, cm, lb2, pDN, x1b[tpar], yc,
                    [dn_mms[-1], qd] + hist["ydma"][-1:],
                    sq_war=hist["ln2s3"][-1:])
                hist["ln2"].append(ln2)
                hist["ln2s1"].append(l2s1)
                hist["ln2s3"].append(l2s3)

                def yso(gp, sem, tc=tc):
                    gp.dma_start(out=y_d[tc * 128:(tc + 1) * 128, :],
                                 in_=yc[:, :]).then_inc(sem, 16)
                ydma = sch.dma("gpsimd", yso, 1, deps=[ln2])
                hist["ydma"].append(ydma)

            sch.step("gpsimd", lambda g: g.memset(cm["cpool"][:, 3:4], 0.0),
                     deps=[hist["ydma"][-1]])
        sch.run(es)
    return nc


def kernel(**inputs):
    x = np.asarray(inputs["L0_attn_x"], np.float32).reshape(T, DM)
    w = {k: np.ascontiguousarray(np.asarray(v, np.float32))
         for k, v in inputs.items()}
    gates = np.zeros((NCORE, 128, 8), np.float32)
    for c in range(NCORE):
        gates[c, :, :c] = 1.0
    def myw(warr):
        outs = []
        for c in range(NCORE):
            buf = np.zeros((384, DM), np.float32)
            lo = min(3 * c, 21)
            n = min(3, 22 - lo) if c < 7 else 1
            buf[0:n * 128] = warr[lo * 128:(lo + n) * 128]
            outs.append(buf)
        return outs
    wg_list = myw(w["L0_ffn_w_gate"])
    wu_list = myw(w["L0_ffn_w_up"])

    nc = build_fused()
    ins = [{"x": np.ascontiguousarray(x[c * TC:(c + 1) * TC]),
            "W_q": w["L0_attn_W_q"], "W_k": w["L0_attn_W_k"],
            "W_v": w["L0_attn_W_v"], "Wg_K": w["L0_attn_Wg_K"],
            "Wg_C": w["L0_attn_Wg_C"],
            "bg_K": w["L0_attn_bg_K"].reshape(1, DM),
            "bg_C": w["L0_attn_bg_C"].reshape(1, DM),
            "gates": gates[c], "W_o": w["L0_attn_W_o"],
            "wg_my": wg_list[c], "wu_my": wu_list[c],
            "wdn_my": np.ascontiguousarray(
                w["L0_ffn_w_down"][c * 128:(c + 1) * 128])}
           for c in range(NCORE)]
    r = run_bass_kernel_spmd(nc, ins, core_ids=list(range(NCORE))).results
    y = np.concatenate([np.asarray(r[c]["y"]) for c in range(NCORE)], axis=0)
    return y.reshape(1, T, DM)
